# version 24
# baseline (speedup 1.0000x reference)
import math
import numpy as np
import ml_dtypes
from contextlib import ExitStack

import concourse.bass as bass
import concourse.mybir as mybir
from concourse.alu_op_type import AluOpType

F32 = mybir.dt.float32
BF16 = mybir.dt.bfloat16
AF = mybir.ActivationFunctionType
BF = ml_dtypes.bfloat16
ENGINES = ("tensor", "vector", "scalar", "gpsimd", "sync")
DMA_SEMS = {"vector": ["vd0"], "scalar": ["sd0", "sd1"],
            "gpsimd": ["gd0", "gd1", "gd2", "gd3"],
            "sync": ["yd0", "yd1", "yd2", "yd3"]}
ALL_SEMS = ["pe", "ve", "ac", "gp", "sy", "xd"] + [s for l in DMA_SEMS.values() for s in l]
ENG_SEM = {"tensor": "pe", "vector": "ve", "scalar": "ac", "gpsimd": "gp", "sync": "sy"}


class Ev:
    __slots__ = ("eng", "sem", "idx", "by", "used", "val")

    def __init__(self, eng, sem, idx, by):
        self.eng, self.sem, self.idx, self.by = eng, sem, idx, by
        self.used = False
        self.val = None


class Buf:
    __slots__ = ("w", "r", "name")

    def __init__(self, name=""):
        self.w = None
        self.r = []
        self.name = name


class Sched:
    def __init__(self):
        self.recs = {e: [] for e in ENGINES}
        self.dma_rot = {e: 0 for e in ENGINES}
        self.last_dma = {}
        self.tail = {e: None for e in ENGINES}
        self.pending = {e: [] for e in ENGINES}

    def barrier(self):
        evs = [e for e in self.tail.values() if e is not None]
        evs += [e for e in self.last_dma.values() if e is not None]
        for eng in ENGINES:
            self.pending[eng] = list(evs)

    def op(self, eng, fn, reads=(), writes=(), waits=(), dma=False, inc_by=None, dma_sem=None):
        if dma:
            if dma_sem is not None:
                sem = dma_sem
            else:
                sl = DMA_SEMS[eng]
                sem = sl[self.dma_rot[eng] % len(sl)]
                self.dma_rot[eng] += 1
        else:
            sem = ENG_SEM[eng]
        by = inc_by if inc_by is not None else (16 if dma else 1)
        ev = Ev(eng, sem, len(self.recs[eng]), by)
        wl = []
        if dma:
            ev.used = True
            prev = self.last_dma.get(sem)
            if prev is not None:
                wl.append(prev)
            self.last_dma[sem] = ev
        for b in reads:
            if b.w is not None:
                wl.append(b.w)
        for b in writes:
            if b.w is not None:
                wl.append(b.w)
            wl.extend(b.r)
        for e in waits:
            wl.append(e)
        seen, wl2 = set(), []
        for e in wl:
            if e is None or id(e) in seen:
                continue
            seen.add(id(e))
            wl2.append(e)
        if self.pending[eng]:
            for e in self.pending[eng]:
                if id(e) not in seen and e is not ev:
                    seen.add(id(e))
                    wl2.append(e)
            self.pending[eng] = []
        self.recs[eng].append((fn, wl2, ev))
        self.tail[eng] = ev
        for b in reads:
            b.r.append(ev)
        for b in writes:
            b.w = ev
            b.r = []
        return ev

    def mark(self, *evs):
        for e in evs:
            if e is not None:
                e.used = True

    def finalize(self, nc, block, sems):
        for eng in ENGINES:
            cnt = {}
            for fn, wl, ev in self.recs[eng]:
                for e in wl:
                    e.used = True
        for eng in ENGINES:
            cnt = {}
            for fn, wl, ev in self.recs[eng]:
                if ev.used:
                    cnt[ev.sem] = cnt.get(ev.sem, 0) + ev.by
                    ev.val = cnt[ev.sem]
        def make(eng):
            def emit(h):
                hw = {}
                for fn, wl, ev in self.recs[eng]:
                    for e in wl:
                        if e.val is None:
                            continue
                        key = e.sem
                        if hw.get(key, 0) >= e.val:
                            continue
                        h.wait_ge(sems[e.sem], e.val)
                        hw[key] = e.val
                    inst = fn(h)
                    if ev.used and ev.val is not None:
                        assert inst is not None, f"{eng} op needs inc but returned None"
                        inst.then_inc(sems[ev.sem], ev.by)
            return emit
        for eng in ENGINES:
            if not self.recs[eng]:
                continue
            getattr(block, eng)(make(eng))


class Cfg:
    def __init__(self, NH=16, S=4096, T=2048, QT=512, n_cores=8):
        self.NH = NH
        self.HC = NH
        self.H = NH * 128
        self.S = S
        self.T = T
        self.QT = QT
        self.KC = S // 128
        self.NQT = T // QT
        self.E, self.I = 8, 16
        self.EI = 128
        self.SCALE = 1.0 / math.sqrt(128.0)
        self.VD = min(512, self.H)
        self.NVD = self.H // self.VD
        self.TB = S // 128
        self.TBo = T // 128
        self.n_cores = n_cores
        self.EPS = 1e-5
        self.TW = min(2 * QT, T)
        self.NTW = T // self.TW
        self.JW = min(512, self.TW)
        self.JT = self.TW // self.JW


def build(cfg: Cfg, debug=False):
    nc = bass.Bass("TRN2", target_bir_lowering=False)
    K = Sched()
    H, S, T, QT, NH, HC, KC = cfg.H, cfg.S, cfg.T, cfg.QT, cfg.NH, cfg.HC, cfg.KC
    NQT, EI, VD, NVD, TB = cfg.NQT, cfg.EI, cfg.VD, cfg.NVD, cfg.TB
    TW, NTW, JW, JT = cfg.TW, cfg.NTW, cfg.JW, cfg.JT
    SC = cfg.SCALE
    IH = 1.0 / H

    xT = nc.dram_tensor("xT", [H, T], F32, kind="ExternalInput")
    wq_h = nc.dram_tensor("wq", [NH, HC * 128, 128], F32, kind="ExternalInput")
    wk_h = nc.dram_tensor("wk", [NH, HC * 128, 128], F32, kind="ExternalInput")
    wv_h = nc.dram_tensor("wv", [NVD, HC * 128, VD], F32, kind="ExternalInput")
    wo_h = nc.dram_tensor("wo", [HC, HC * 128, 128], F32, kind="ExternalInput")
    bqk_h = nc.dram_tensor("bqk", [128, 2 * NH], F32, kind="ExternalInput")
    bo_h = nc.dram_tensor("bo", [128, HC], F32, kind="ExternalInput")
    bv_h = nc.dram_tensor("bv", [1, H], F32, kind="ExternalInput")
    wr_h = nc.dram_tensor("wr", [128, HC * 8], F32, kind="ExternalInput")
    wd_h = nc.dram_tensor("wd", [128, HC * EI], F32, kind="ExternalInput")
    wu_h = nc.dram_tensor("wu", [EI, H], F32, kind="ExternalInput")
    cst_h = nc.dram_tensor("cst", [128, 256], BF16, kind="ExternalInput")
    cstf_h = nc.dram_tensor("cstf", [128, 128], F32, kind="ExternalInput")
    outT = nc.dram_tensor("outT", [H, T], F32, kind="ExternalOutput")

    _scr_kind = "Internal"
    TBo = cfg.TBo
    qscr = nc.dram_tensor("qscr", [NH, 128, T], BF16, kind=_scr_kind)
    kag_i = nc.dram_tensor("kag_i", [NH, 128, T], BF16)
    kag_o = nc.dram_tensor("kag_o", [2, NH, 128, T], BF16, kind=_scr_kind)
    vag_i = nc.dram_tensor("vag_i", [TBo, NH, 128, 128], BF16)
    vag_o = nc.dram_tensor("vag_o", [2, TBo, NH, 128, 128], BF16, kind=_scr_kind)
    wo_scr = nc.dram_tensor("wo_scr", [HC, 128, HC * 128], BF16, kind=_scr_kind)
    NGQ_ = T // (min(2, T // 128) * 128)
    d_ao = nc.dram_tensor("d_ao", [NGQ_, 128, HC * (T // NGQ_)], BF16, kind="ExternalOutput") if debug else None
    d_r1 = nc.dram_tensor("d_r1", [NGQ_, 128, HC * (T // NGQ_)], BF16, kind="ExternalOutput") if debug else None
    d_y = nc.dram_tensor("d_y", [NGQ_, 128, HC * (T // NGQ_)], BF16, kind="ExternalOutput") if debug else None
    d_r2 = nc.dram_tensor("d_r2", [NGQ_, 128, HC * (T // NGQ_)], BF16, kind="ExternalOutput") if debug else None
    d_st = nc.dram_tensor("d_st", [NGQ_, 2, 128, 6 * (T // NGQ_)], F32, kind="ExternalOutput") if debug else None
    d_pj = nc.dram_tensor("d_pj", [NGQ_, 2, 128, T // NGQ_], F32, kind="ExternalOutput") if debug else None
    d_xr = nc.dram_tensor("d_xr", [NGQ_, 2, 128, T // NGQ_], F32, kind="ExternalOutput") if debug else None

    es = ExitStack()
    sbt = lambda name, shape, dt: es.enter_context(nc.sbuf_tensor(name, shape, dt))

    cst = sbt("cst_sb", [128, 256], BF16)
    cstf = sbt("cstf_sb", [128, 128], F32)
    sel_sb = sbt("sel_sb", [8, 128], BF16)
    ones_sb = sbt("ones_sb", [128, 128], BF16)
    onesf_sb = sbt("onesf_sb", [1, 128], F32)
    bqk_sb = sbt("bqk_sb", [128, 2 * NH], F32)
    bo_sb = sbt("bo_sb", [128, HC], F32)
    bv_sb = sbt("bv_sb", [1, H], BF16)
    wr_sb = sbt("wr_sb", [128, HC * 8], BF16)
    wd_sb = sbt("wd_sb", [128, HC * EI], BF16)
    wu_sb = sbt("wu_sb", [128, H], BF16)

    NTB = T // 128
    def _al(x):
        return (x + 63) // 64 * 64
    A_XT = 0
    A_wv = _al(A_XT + HC * T)
    A_wqk = _al(A_wv + HC * VD)
    A_kst = _al(A_wqk + 2 * HC * 128)
    A_vst = _al(A_kst + 2 * 512)
    p1_end = _al(A_vst + 2 * 512)
    GQ = min(2, T // 128)
    QW = GQ * 128
    NGQ = T // QW
    SH = S // 2
    NQS_ = T // 128
    A_q2 = 0
    A_k2 = _al(A_q2 + 2 * T)
    A_v2 = _al(A_k2 + 2 * S)
    A_PT = _al(A_v2 + 2 * S)
    A_P2s = _al(A_PT + KC * QW)
    A_junk = _al(A_P2s + 2 * S)
    A_accQ = _al(A_junk + 1280)
    A_mnegs = _al(A_accQ + 2 * 4 * NH * NQS_)
    A_lrep = _al(A_mnegs + 2 * NH * NQS_)
    A_lac = _al(A_lrep + 2 * QW)
    A_lrw = _al(A_lac + 32 + 8)
    A_AO = _al(A_lrw + 2 * QW)
    A_r1 = _al(A_AO + 2 * HC * QW)
    A_xres = _al(A_r1 + HC * QW)
    A_yst = _al(A_xres + 2 * 2 * QW)
    A_wo = _al(A_yst + 3 * 2 * QW)
    A_ag = _al(A_wo + 2 * HC * 128)
    A_sq2 = _al(A_ag + 2 * QW)
    A_stf = _al(A_sq2 + 2 * QW)
    A_mgc = _al(A_stf + 6 * 2 * QW)
    A_ssm = _al(A_mgc + 2 * QW)
    p2_end = _al(A_ssm + 4 * QW)
    AR_COLS = _al(max(p2_end, p1_end) + 64)
    arena = sbt("arena", [128, AR_COLS], BF16)
    assert AR_COLS * 2 <= 190 * 1024, f"arena {AR_COLS * 2} B/partition too big"

    XT = arena[:, A_XT:A_XT + HC * T]
    wv_sb = arena[:, A_wv:A_wv + HC * VD]
    wqk0 = A_wqk
    kst = arena[:, A_kst:A_kst + 2 * 512]
    vst = arena[:, A_vst:A_vst + 2 * 512]
    q2o, k2o, v2o = A_q2, A_k2, A_v2
    PTsl = arena[:, A_PT:A_PT + KC * QW]
    P2s = arena[:, A_P2s:A_P2s + 2 * S]
    junk = arena[:, A_junk:A_junk + 1280]
    accQ = arena[:, A_accQ:A_accQ + 2 * 4 * NH * NQS_].bitcast(F32)
    accQI = arena[:, A_accQ:A_accQ + 2 * 4 * NH * NQS_].bitcast(mybir.dt.int32)
    mnegs = arena[:, A_mnegs:A_mnegs + 2 * NH * NQS_].bitcast(F32)
    lrep_sb = arena[:, A_lrep:A_lrep + 2 * QW].bitcast(F32)
    lacc = arena[:, A_lac:A_lac + 32].bitcast(F32)
    lcol2 = arena[:, A_lac + 32:A_lac + 36].bitcast(F32)
    lcol2i = arena[:, A_lac + 36:A_lac + 40].bitcast(F32)
    lirow = arena[0:1, A_lrw:A_lrw + 2 * QW].bitcast(F32)
    AO = arena[:, A_AO:A_AO + 2 * HC * QW]
    r1sl = arena[:, A_r1:A_r1 + HC * QW]
    x_res = arena[:, A_xres:A_xres + 2 * 2 * QW].bitcast(F32)
    ystage = arena[:, A_yst:A_yst + 3 * 2 * QW].bitcast(F32)
    wo0 = A_wo
    agst = arena[:, A_ag:A_ag + 2 * QW]
    sqst = arena[:, A_sq2:A_sq2 + 2 * QW]
    stf = arena[:, A_stf:A_stf + 6 * 2 * QW].bitcast(F32)
    stfI = arena[:, A_stf:A_stf + 6 * 2 * QW].bitcast(mybir.dt.int32)
    mgcI = arena[:, A_mgc:A_mgc + 2 * QW].bitcast(mybir.dt.int32)
    mgcF = arena[:, A_mgc:A_mgc + 2 * QW].bitcast(F32)
    ssm = arena[:, A_ssm:A_ssm + 4 * QW]
    lgf = arena[0:8, A_junk:A_junk + 2 * QW].bitcast(F32)
    gT_sb = arena[0:8, A_junk + 2 * QW:A_junk + 3 * QW]
    g0 = A_junk + 3 * QW
    gtok = arena[:, g0:g0 + 2 * GQ * 8].bitcast(F32)
    gtk2 = arena[:, g0 + 2 * GQ * 8:g0 + 4 * GQ * 8].bitcast(F32)
    gtk3 = arena[:, g0 + 4 * GQ * 8:g0 + 6 * GQ * 8].bitcast(F32)
    gsc = arena[:, g0 + 6 * GQ * 8:g0 + 6 * GQ * 8 + 2 * GQ].bitcast(F32)
    gsc2 = arena[:, g0 + 6 * GQ * 8 + 2 * GQ:g0 + 6 * GQ * 8 + 4 * GQ].bitcast(F32)
    assert 3 * QW + 6 * GQ * 8 + 4 * GQ <= 1280

    psA = es.enter_context(nc.psum_tensor("psA", [128, 2048], F32))
    psB = es.enter_context(nc.psum_tensor("psB", [128, 2048], F32))

    sems = {}
    for s in ALL_SEMS:
        sems[s] = es.enter_context(nc.semaphore("sem_" + s))
    block = es.enter_context(nc.Block())

    b_cst, b_cstf, b_sel, b_ones = Buf("cst"), Buf("cstf"), Buf("sel"), Buf("ones")
    b_onesf = Buf("onesf")
    b_bqk, b_bo, b_bv = Buf("bqk"), Buf("bo"), Buf("bv")
    b_wr, b_wd, b_wu = Buf("wr"), Buf("wd"), Buf("wu")
    b_XT = Buf("XT")
    b_wv = Buf("wv")
    b_wqk = [Buf("wqk%d" % i) for i in range(2)]
    b_kst = [Buf("kst%d" % i) for i in range(2)]
    b_vst = [Buf("vst%d" % i) for i in range(2)]
    b_psA = [Buf("psA%d" % i) for i in range(4)]
    b_psB = [Buf("psB%d" % i) for i in range(4)]
    b_pA2 = [b_psA[2], b_psA[2]]
    b_pA3 = [b_psA[3], b_psA[3]]
    b_pB2 = [b_psB[2], b_psB[2]]
    b_pB3 = [b_psB[3], b_psB[3]]
    psA_all = b_psA
    psB_all = b_psB

    def pAb(slot):
        return [b_psA[slot]]

    def pBb(slot):
        return [b_psB[slot]]
    e_kscr = [[] for _ in range(NH)]
    e_qscr = [[] for _ in range(NH)]
    e_vscr = [[] for _ in range(NH)]
    b_k2 = [Buf("k2_%d" % i) for i in range(2)]
    b_q2 = [Buf("q2_%d" % i) for i in range(2)]
    b_q2s = [Buf("q2s%d" % i) for i in range(2)]
    b_v2s = [Buf("v2s%d" % i) for i in range(2)]
    b_P2s = [[Buf("P2s%d_%d" % (i, hh)) for hh in range(2)] for i in range(2)]
    b_lacc = [Buf("lacc%d" % i) for i in range(2)]
    b_lc2, b_lc2i, b_lirow = Buf("lc2"), Buf("lc2i"), Buf("lirow")
    b_accQ = Buf("accQ")
    b_mnegs = Buf("mnegs")
    b_PT = Buf("PT")
    b_junk = Buf("junk")
    b_lrep = Buf("lrep")
    b_AO2 = [[Buf("AO%d_%d" % (par, n)) for n in range(HC)] for par in range(2)]
    b_wo = [Buf("wo%d" % i) for i in range(2)]
    b_x2 = [Buf("x2_%d" % i) for i in range(2)]
    b_r1n = [Buf("r1n%d" % o) for o in range(HC)]
    b_sq2 = [Buf("sq2_%d" % i) for i in range(2)]
    b_yst1 = Buf("yst1")
    b_yst2 = [Buf("yst2_%d" % i) for i in range(2)]
    b_ag2 = [Buf("ag2_%d" % i) for i in range(2)]
    b_stf = [Buf("stf%d" % i) for i in range(6)]
    b_ssm = [Buf("ssm%d" % i) for i in range(4)]
    b_lgf2, b_gT2 = Buf("lgf2"), Buf("gT2")
    b_gtok2, b_gtk22, b_gtk32 = Buf("gtok2"), Buf("gtk22"), Buf("gtk32")
    b_gsc_, b_gsc2_ = Buf("gsc_"), Buf("gsc2_")
    b_mgc = Buf("mgc")

    def aosl(par, c):
        return AO[:, par * HC * QW + c * QW:par * HC * QW + (c + 1) * QW]

    g = "gpsimd"
    K.op(g, lambda h: h.dma_start(cst[:, :], cst_h[:, :]), writes=[b_cst], dma=True)
    K.op(g, lambda h: h.dma_start(cstf[:, :], cstf_h[:, :]), writes=[b_cstf], dma=True)
    K.op(g, lambda h: h.dma_start(sel_sb[:, :], cst_h[0:8, 128:256]), writes=[b_sel], dma=True)
    K.op(g, lambda h: h.dma_start(bqk_sb[:, :], bqk_h[:, :]), writes=[b_bqk], dma=True)
    K.op(g, lambda h: h.dma_start(bo_sb[:, :], bo_h[:, :]), writes=[b_bo], dma=True)
    K.op(g, lambda h: h.dma_start(bv_sb[:, :], bv_h[:, :]), writes=[b_bv], dma=True)
    K.op(g, lambda h: h.dma_start(wr_sb[:, :], wr_h[:, :]), writes=[b_wr], dma=True)
    K.op(g, lambda h: h.dma_start(wd_sb[:, :], wd_h[:, :]), writes=[b_wd], dma=True)
    K.op(g, lambda h: h.dma_start(wu_sb[:, :], wu_h[:, :]), writes=[b_wu], dma=True)
    for c in range(HC):
        K.op(g, (lambda c: lambda h: h.dma_start(
            XT[:, c * T:(c + 1) * T], xT[c * 128:(c + 1) * 128, :]))(c),
            writes=[b_XT], dma=True)
    K.op("vector", lambda h: h.memset(ones_sb[:, :], 1.0), writes=[b_ones])
    MAGIC_F = float(np.frombuffer(np.uint32(0x5F3759DF).tobytes(), np.float32)[0])
    K.op("vector", lambda h: h.memset(mgcF[:, :], MAGIC_F), writes=[b_mgc])
    K.op("vector", lambda h: h.memset(onesf_sb[:, :], 1.0), writes=[b_onesf])

    ident = cst[:, 0:128]
    ident8f = cstf[0:8, 0:8]
    identf = cstf[:, :]

    for dt in range(NVD):
        K.op(g, (lambda dt: lambda h: h.dma_start(
            wv_sb[:, :].rearrange("p (c d) -> p c d", d=VD),
            wv_h[dt, :, :].rearrange("(c p) d -> p c d", p=128)))(dt),
            writes=[b_wv], dma=True)
        for tb in range(TBo):
            idx1 = dt * TBo + tb
            slot = idx1 % 4
            if idx1 < HC:
                wcs = idx1 % 2
                K.op(g, (lambda o, wcs: lambda h: h.dma_start(
                    arena[:, wqk0 + wcs * HC * 128:wqk0 + (wcs + 1) * HC * 128]
                    .rearrange("p (c q) -> p c q", q=128),
                    wo_h[o, :, :].rearrange("(c p) q -> p c q", p=128)))(idx1, wcs),
                    writes=[b_wqk[wcs]], dma=True)
                K.op("sync", (lambda o, wcs: lambda h: h.dma_start(
                    wo_scr[o, :, :],
                    arena[:, wqk0 + wcs * HC * 128:wqk0 + (wcs + 1) * HC * 128]))(idx1, wcs),
                    reads=[b_wqk[wcs]], dma=True)
            def mmg(h, dt=dt, tb=tb, slot=slot):
                for c in range(HC):
                    h.matmul(psA[:, slot * 512:slot * 512 + VD],
                             XT[:, c * T + tb * 128:c * T + (tb + 1) * 128],
                             wv_sb[:, c * VD:(c + 1) * VD],
                             start=(c == 0), stop=False)
                return h.matmul(psA[:, slot * 512:slot * 512 + VD],
                                ones_sb[0:1, 0:128], bv_sb[:, dt * VD:(dt + 1) * VD],
                                start=False, stop=True)
            K.op("tensor", mmg, reads=[b_XT, b_wv, b_ones, b_bv], writes=pAb(slot))
            vs = idx1 % 2
            K.op("vector", (lambda slot, vs: lambda h: h.tensor_copy(
                vst[:, vs * 512:vs * 512 + VD], psA[:, slot * 512:slot * 512 + VD]))(slot, vs),
                reads=pAb(slot), writes=[b_vst[vs]])
            nh_t = VD // 128
            def spill(h, dt=dt, tb=tb, vs=vs, nh_t=nh_t):
                return h.dma_start(vag_i[tb, dt * nh_t:(dt + 1) * nh_t, :, :].transpose([1, 0, 2]),
                                   vst[:, vs * 512:vs * 512 + VD].rearrange("p (j d) -> p j d", d=128))
            ev = K.op("sync", spill, reads=[b_vst[vs]], dma=True)
            e_vscr[0].append(ev)

    def qk_proj(which):
        span = T
        tile_w = min(512, span)
        ntile = span // tile_w
        wsrc = wk_h if which == "k" else wq_h
        scr = kag_i if which == "k" else qscr
        escr = e_kscr if which == "k" else e_qscr
        bias0 = NH if which == "k" else 0
        for n in range(NH):
            slot = n % 2
            K.op(g, (lambda n, slot, wsrc=wsrc: lambda h: h.dma_start(
                arena[:, wqk0 + slot * HC * 128: wqk0 + (slot + 1) * HC * 128]
                .rearrange("p (c o) -> p c o", o=128),
                wsrc[n, :, :].rearrange("(c p) o -> p c o", p=128)))(n, slot),
                writes=[b_wqk[slot]], dma=True)
            for j in range(ntile):
                idx = n * ntile + j
                ps = idx % 4
                def mmg(h, n=n, j=j, slot=slot, ps=ps, tile_w=tile_w):
                    for c in range(HC):
                        mm = h.matmul(psB[:, ps * 512:ps * 512 + tile_w],
                                      arena[:, wqk0 + slot * HC * 128 + c * 128:
                                            wqk0 + slot * HC * 128 + (c + 1) * 128],
                                      XT[:, c * T + j * tile_w:c * T + (j + 1) * tile_w],
                                      start=(c == 0), stop=(c == HC - 1))
                    return mm
                K.op("tensor", mmg, reads=[b_XT, b_wqk[slot]], writes=pBb(ps))
                ks = idx % 2
                K.op("scalar", (lambda ps, ks, n, tile_w: lambda h: h.activation(
                    kst[:, ks * 512:ks * 512 + tile_w], psB[:, ps * 512:ps * 512 + tile_w],
                    AF.Identity, bias=bqk_sb[:, bias0 + n:bias0 + n + 1], scale=1.0))(ps, ks, n, tile_w),
                    reads=pBb(ps) + [b_bqk], writes=[b_kst[ks]])
                ev = K.op("sync", (lambda n, j, ks, tile_w, scr=scr: lambda h: h.dma_start(
                    scr[n, :, j * tile_w:(j + 1) * tile_w],
                    kst[:, ks * 512:ks * 512 + tile_w]))(n, j, ks, tile_w),
                    reads=[b_kst[ks]], dma=True)
                escr[n].append(ev)
    qk_proj("k")

    pair_groups = [[i, i + 1] for i in range(0, cfg.n_cores, 2)]
    e_kag = K.op(g, (lambda: lambda h: h.collective_compute(
        "AllGather", AluOpType.bypass, replica_groups=pair_groups,
        ins=[kag_i.ap().opt()], outs=[kag_o.ap().opt()]))(),
        waits=[e for n in range(NH) for e in e_kscr[n]])
    e_vag = K.op(g, (lambda: lambda h: h.collective_compute(
        "AllGather", AluOpType.bypass, replica_groups=pair_groups,
        ins=[vag_i.ap().opt()], outs=[vag_o.ap().opt()]))(),
        waits=e_vscr[0])
    qk_proj("q")

    K.barrier()
    nSA = (SH + 511) // 512
    NQS = T // 128
    import functools

    nstr = max(1, KC // 4)
    NV = NH * NQS
    SSW = (nstr + 2) * 128
    for n in range(NH):
        sl = n % 2
        K.op("sync", (lambda n, sl: lambda h: h.dma_start(
            arena[:, q2o + sl * T:q2o + (sl + 1) * T], qscr[n, :, :]))(n, sl),
            waits=e_qscr[n], writes=[b_q2[sl]], dma=True)
        for half in range(2):
            K.op("sync", (lambda n, sl, half: lambda h: h.dma_start(
                arena[:, k2o + sl * S + half * T:k2o + sl * S + (half + 1) * T],
                kag_o[half, n, :, :]))(n, sl, half),
                waits=[e_kag], writes=[b_k2[sl]], dma=True)
        for qi in range(NQS):
            pp = qi % 2
            psS = psA if pp == 0 else psB
            bps = ([b_psA[0], b_psA[1]] + b_pA2) if pp == 0 else ([b_psB[0], b_psB[1]] + b_pB2)
            hn = max(1, nstr // 2)
            def smp(h, n=n, qi=qi, sl=sl, psS=psS, hn=hn):
                k2v = arena[:, k2o + sl * S:k2o + (sl + 1) * S]
                qv = arena[:, q2o + sl * T + qi * 128:q2o + sl * T + (qi + 1) * 128]
                kstr = k2v.rearrange("p (a b) -> p a b", b=512)[:, :, 0:128]
                h.matmul(psS[:, 0:hn * 128], qv, kstr[:, 0:hn, :], start=True, stop=True)
                if nstr > hn:
                    h.matmul(psS[:, 512:512 + (nstr - hn) * 128], qv, kstr[:, hn:nstr, :],
                             start=True, stop=True)
                return h.matmul(psS[:, 1024:1024 + 256],
                                qv, k2v.rearrange("p (h b) -> p h b", h=2)
                                [:, :, qi * 128:(qi + 1) * 128],
                                start=True, stop=True)
            K.op("tensor", smp, reads=[b_k2[sl], b_q2[sl]], writes=bps)
            c0 = n * NQS + qi
            if nstr == 8:
                K.op("scalar", (lambda c0, psS: lambda h: h.activation(
                    junk[:, 0:1280], psS[:, 0:1280], AF.Exp, scale=SC / 4.0,
                    accum_out=accQ[:, c0:c0 + 1]))(c0, psS),
                    reads=bps, writes=[b_junk, b_accQ])
            else:
                K.op("scalar", (lambda c0, psS, hn: lambda h: h.activation(
                    junk[:, 0:hn * 128], psS[:, 0:hn * 128], AF.Exp, scale=SC / 4.0,
                    accum_out=accQ[:, NV + c0:NV + c0 + 1]))(c0, psS, hn),
                    reads=bps, writes=[b_junk, b_accQ])
                K.op("scalar", (lambda c0, psS: lambda h: h.activation(
                    junk[:, 0:256], psS[:, 1024:1280], AF.Exp, scale=SC / 4.0,
                    accum_out=accQ[:, 2 * NV + c0:2 * NV + c0 + 1]))(c0, psS),
                    reads=bps, writes=[b_junk, b_accQ])
                K.op("vector", (lambda c0: lambda h: h.tensor_tensor(
                    accQ[:, c0:c0 + 1], accQ[:, NV + c0:NV + c0 + 1],
                    accQ[:, 2 * NV + c0:2 * NV + c0 + 1], op=AluOpType.add))(c0),
                    reads=[b_accQ], writes=[b_accQ])
        K.op("vector", (lambda n: lambda h: h.tensor_scalar(
            accQI[:, NH * NQS + n * NQS:NH * NQS + (n + 1) * NQS],
            accQI[:, n * NQS:(n + 1) * NQS], 23, None,
            op0=AluOpType.arith_shift_right))(n),
            reads=[b_accQ], writes=[b_accQ])
        K.op("vector", (lambda n: lambda h: h.tensor_copy(
            accQ[:, 2 * NH * NQS + n * NQS:2 * NH * NQS + (n + 1) * NQS],
            accQI[:, NH * NQS + n * NQS:NH * NQS + (n + 1) * NQS]))(n),
            reads=[b_accQ], writes=[b_accQ])
        K.op("vector", (lambda n: lambda h: h.tensor_scalar(
            mnegs[:, n * NQS:(n + 1) * NQS],
            accQ[:, 2 * NH * NQS + n * NQS:2 * NH * NQS + (n + 1) * NQS],
            -127.0, -4.0 * 0.6931471805599453,
            op0=AluOpType.add, op1=AluOpType.mult))(n),
            reads=[b_accQ], writes=[b_mnegs])

    K.barrier()
    pend = []

    def flush_pending():
        for f in pend:
            f()
        pend.clear()

    q34 = []
    last_out = {}

    def p3p4_tile(qg):
        par = qg % 2
        t0 = qg * QW
        ops = []
        add = lambda f: ops.append([f])
        addb = lambda fs: ops.append(list(fs))

        def accum_stats(o):
            cur = o % 2
            add((lambda o, cur: lambda: K.op("vector", lambda h: h.tensor_tensor(
                sqst[:, cur * QW:(cur + 1) * QW], r1sl[:, o * QW:(o + 1) * QW],
                r1sl[:, o * QW:(o + 1) * QW], op=AluOpType.mult),
                reads=[b_r1n[o]], writes=[b_sq2[cur]]))(o, cur))
            if o == 0:
                add(lambda: K.op("vector", lambda h: h.tensor_copy(
                    ssm[:, 0:QW], r1sl[:, 0:QW]), reads=[b_r1n[0]], writes=[b_ssm[0]]))
                add(lambda: K.op("vector", lambda h: h.tensor_copy(
                    ssm[:, 2 * QW:3 * QW], sqst[:, 0:QW]), reads=[b_sq2[0]], writes=[b_ssm[2]]))
            else:
                prv = 1 - cur
                add((lambda o, cur, prv: lambda: K.op("vector", lambda h: h.tensor_tensor(
                    ssm[:, cur * QW:(cur + 1) * QW], ssm[:, prv * QW:(prv + 1) * QW],
                    r1sl[:, o * QW:(o + 1) * QW], op=AluOpType.add),
                    reads=[b_ssm[prv], b_r1n[o]], writes=[b_ssm[cur]]))(o, cur, prv))
                add((lambda cur, prv: lambda: K.op("vector", lambda h: h.tensor_tensor(
                    ssm[:, (2 + cur) * QW:(3 + cur) * QW],
                    ssm[:, (2 + prv) * QW:(3 + prv) * QW],
                    sqst[:, cur * QW:(cur + 1) * QW], op=AluOpType.add),
                    reads=[b_ssm[2 + prv], b_sq2[cur]], writes=[b_ssm[2 + cur]]))(cur, prv))

        def ln_half(to_ao):
            fin = (HC - 1) % 2
            mu = stf[:, 0:QW]
            rs = stf[:, QW:2 * QW]
            mus = stf[:, 2 * QW:3 * QW]
            scr2 = stf[:, 3 * QW:4 * QW]
            var2 = stf[:, 4 * QW:5 * QW]
            addb([
                lambda: K.op("tensor", lambda h: h.matmul(
                    psB[:, 1280:1280 + QW], ones_sb[:, 0:128], ssm[:, fin * QW:(fin + 1) * QW],
                    start=True, stop=True), reads=[b_ssm[fin], b_ones], writes=[b_pB2[1]]),
                lambda: K.op("tensor", lambda h: h.matmul(
                    psB[:, 1792:1792 + QW], ones_sb[:, 0:128],
                    ssm[:, (2 + fin) * QW:(3 + fin) * QW],
                    start=True, stop=True), reads=[b_ssm[2 + fin], b_ones], writes=[b_pB3[1]]),
                lambda: K.op("vector", lambda h: h.tensor_scalar(
                    mu, psB[:, 1280:1280 + QW], IH, None, op0=AluOpType.mult),
                    reads=[b_pB2[1]], writes=[b_stf[0]]),
                lambda: K.op("vector", lambda h: h.tensor_scalar(
                    scr2, psB[:, 1792:1792 + QW], IH, cfg.EPS,
                    op0=AluOpType.mult, op1=AluOpType.add),
                    reads=[b_pB3[1]], writes=[b_stf[3]]),
            ])
            add(lambda: K.op("vector", lambda h: h.tensor_tensor(
                rs, mu, mu, op=AluOpType.mult), reads=[b_stf[0]], writes=[b_stf[1]]))
            add(lambda: K.op("vector", lambda h: h.tensor_tensor(
                var2, scr2, rs, op=AluOpType.subtract),
                reads=[b_stf[1], b_stf[3]], writes=[b_stf[4]]))
            add(lambda: K.op("vector", lambda h: h.tensor_scalar(
                stfI[:, 3 * QW:4 * QW], stfI[:, 4 * QW:5 * QW], 1, None,
                op0=AluOpType.arith_shift_right),
                reads=[b_stf[4]], writes=[b_stf[3]]))
            add(lambda: K.op("vector", lambda h: h.tensor_tensor(
                stfI[:, QW:2 * QW], mgcI[:, 0:QW], stfI[:, 3 * QW:4 * QW],
                op=AluOpType.subtract), reads=[b_mgc, b_stf[3]], writes=[b_stf[1]]))
            for it in range(2):
                sa, sb = (1, 2) if it % 2 == 0 else (2, 1)
                add((lambda sa: lambda: K.op("vector", lambda h: h.tensor_tensor(
                    stf[:, 3 * QW:4 * QW], stf[:, sa * QW:(sa + 1) * QW],
                    stf[:, sa * QW:(sa + 1) * QW], op=AluOpType.mult),
                    reads=[b_stf[sa]], writes=[b_stf[3]]))(sa))
                add(lambda: K.op("vector", lambda h: h.tensor_tensor(
                    stf[:, 5 * QW:6 * QW], stf[:, 3 * QW:4 * QW], var2,
                    op=AluOpType.mult), reads=[b_stf[3], b_stf[4]], writes=[b_stf[5]]))
                add(lambda: K.op("vector", lambda h: h.tensor_scalar(
                    stf[:, 3 * QW:4 * QW], stf[:, 5 * QW:6 * QW], -0.5, 1.5,
                    op0=AluOpType.mult, op1=AluOpType.add),
                    reads=[b_stf[5]], writes=[b_stf[3]]))
                add((lambda sa, sb: lambda: K.op("vector", lambda h: h.tensor_tensor(
                    stf[:, sb * QW:(sb + 1) * QW], stf[:, sa * QW:(sa + 1) * QW],
                    stf[:, 3 * QW:4 * QW], op=AluOpType.mult),
                    reads=[b_stf[sa], b_stf[3]], writes=[b_stf[sb]]))(sa, sb))
            add(lambda: K.op("vector", lambda h: h.tensor_tensor(
                mus, mu, rs, op=AluOpType.mult), reads=[b_stf[0], b_stf[1]], writes=[b_stf[2]]))
            for o in range(HC):
                ys = o % 2
                add((lambda o: lambda: K.op("vector", lambda h: h.tensor_tensor(
                    ystage[:, 0:QW], r1sl[:, o * QW:(o + 1) * QW], rs,
                    op=AluOpType.mult), reads=[b_r1n[o], b_stf[1]], writes=[b_yst1]))(o))
                add((lambda o, ys: lambda: K.op("vector", lambda h: h.tensor_tensor(
                    ystage[:, (1 + ys) * QW:(2 + ys) * QW], ystage[:, 0:QW], mus,
                    op=AluOpType.subtract),
                    reads=[b_yst1, b_stf[2]], writes=[b_yst2[ys]]))(o, ys))
                if to_ao:
                    add((lambda o, ys: lambda: K.op("scalar", lambda h: h.activation(
                        aosl(par, o), ystage[:, (1 + ys) * QW:(2 + ys) * QW], AF.Copy),
                        reads=[b_yst2[ys]], writes=[b_AO2[par][o]]))(o, ys))
                else:
                    def mk_out(o=o, ys=ys):
                        def go():
                            ev = K.op("sync", lambda h: h.dma_start(
                                outT[o * 128:(o + 1) * 128, t0:t0 + QW],
                                ystage[:, (1 + ys) * QW:(2 + ys) * QW]),
                                reads=[b_yst2[ys]], dma=True)
                            last_out[ev.sem] = ev
                        return go
                    add(mk_out())

        if debug:
            add(lambda: K.op("sync", lambda h: h.dma_start(
                d_ao[qg, :, :], AO[:, par * HC * QW:(par + 1) * HC * QW]),
                reads=list(b_AO2[par]), dma=True))
        for o in range(HC):
            ws = o % 2
            po = 1280 if o % 2 == 0 else 1792
            pb = b_pA2[1] if o % 2 == 0 else b_pA3[1]
            add((lambda o, ws: lambda: K.op(g, lambda h: h.dma_start(
                arena[:, wo0 + ws * HC * 128:wo0 + (ws + 1) * HC * 128], wo_scr[o, :, :]),
                writes=[b_wo[ws]], dma=True))(o, ws))
            add((lambda o: lambda: K.op("sync", lambda h: h.dma_start(
                x_res[:, (o % 2) * QW:(o % 2) * QW + QW],
                xT[o * 128:(o + 1) * 128, t0:t0 + QW]),
                writes=[b_x2[o % 2]], dma=True))(o))
            def mk_proj(o=o, ws=ws, po=po, pb=pb):
                def mms(h):
                    for c in range(HC):
                        mm = h.matmul(psA[:, po:po + QW],
                                      arena[:, wo0 + ws * HC * 128 + c * 128:
                                            wo0 + ws * HC * 128 + (c + 1) * 128],
                                      aosl(par, c), start=(c == 0), stop=(c == HC - 1))
                    return mm
                return lambda: K.op("tensor", mms, reads=[b_wo[ws]] + b_AO2[par], writes=[pb])
            bnd = [mk_proj()]
            if debug and o < 2:
                bnd.append((lambda o, po, pb: lambda: K.op("vector", lambda h: h.tensor_copy(
                    ystage[:, 0:QW], psA[:, po:po + QW]),
                    reads=[pb], writes=[b_yst1]))(o, po, pb))
                bnd.append((lambda o: lambda: K.op("sync", lambda h: h.dma_start(
                    d_pj[qg, o, :, :], ystage[:, 0:QW]), reads=[b_yst1], dma=True))(o))
                bnd.append((lambda o: lambda: K.op("sync", lambda h: h.dma_start(
                    d_xr[qg, o, :, :], x_res[:, (o % 2) * QW:(o % 2) * QW + QW]),
                    reads=[b_x2[o % 2]], dma=True))(o))
            bnd.append((lambda o, po, pb: lambda: K.op("vector", lambda h: h.scalar_tensor_tensor(
                r1sl[:, o * QW:(o + 1) * QW], psA[:, po:po + QW], bo_sb[:, o:o + 1],
                x_res[:, (o % 2) * QW:(o % 2) * QW + QW],
                op0=AluOpType.add, op1=AluOpType.add),
                reads=[pb, b_bo, b_x2[o % 2]], writes=[b_r1n[o]]))(o, po, pb))
            addb(bnd)
            accum_stats(o)
        if debug:
            add(lambda: K.op("sync", lambda h: h.dma_start(
                d_r1[qg, :, :], r1sl[:, 0:HC * QW]), reads=list(b_r1n), dma=True))
        ln_half(True)
        if debug:
            add(lambda: K.op("sync", lambda h: h.dma_start(
                d_y[qg, :, :], AO[:, par * HC * QW:(par + 1) * HC * QW]),
                reads=list(b_AO2[par]), dma=True))
            add(lambda: K.op("sync", lambda h: h.dma_start(
                d_st[qg, 0, :, :], stf[:, 0:6 * QW]), reads=list(b_stf), dma=True))

        def mk_router():
            def mms(h):
                for c in range(HC):
                    mm = h.matmul(psA[0:8, 1280:1280 + QW], wr_sb[:, c * 8:(c + 1) * 8],
                                  aosl(par, c), start=(c == 0), stop=(c == HC - 1))
                return mm
            return lambda: K.op("tensor", mms, reads=[b_wr] + b_AO2[par], writes=[b_pA2[1]])
        addb([mk_router(),
              lambda: K.op("vector", lambda h: h.tensor_copy(
                  lgf[:, 0:QW], psA[0:8, 1280:1280 + QW]), reads=[b_pA2[1]], writes=[b_lgf2])])
        addb([(lambda tb: lambda: K.op("tensor", lambda h: h.transpose(
                  psA[:, 1792 + tb * 8:1792 + tb * 8 + 8],
                  lgf[0:8, tb * 128:(tb + 1) * 128], ident8f),
                  reads=[b_lgf2, b_cstf], writes=[b_pA3[1]]))(tb) for tb in range(GQ)] +
              [lambda: K.op("vector", lambda h: h.tensor_copy(
                  gtok[:, 0:GQ * 8], psA[:, 1792:1792 + GQ * 8]),
                  reads=[b_pA3[1]], writes=[b_gtok2])])
        seg = gtok[:, :].rearrange("p (s e) -> p s e", e=8)
        seg2 = gtk2[:, :].rearrange("p (s e) -> p s e", e=8)
        seg3 = gtk3[:, :].rearrange("p (s e) -> p s e", e=8)
        gb = lambda: gsc[:, :, None].broadcast_to((128, GQ, 8))
        gb2 = lambda: gsc2[:, :, None].broadcast_to((128, GQ, 8))
        add(lambda: K.op("vector", lambda h: h.tensor_reduce(
            gsc[:, :, None], seg, axis=mybir.AxisListType.X, op=AluOpType.max),
            reads=[b_gtok2], writes=[b_gsc_]))
        add(lambda: K.op("vector", lambda h: h.tensor_tensor(
            seg2, seg, gb(), op=AluOpType.subtract),
            reads=[b_gtok2, b_gsc_], writes=[b_gtk22]))
        add(lambda: K.op("scalar", lambda h: h.activation(gtok[:, :], gtk2[:, :], AF.Exp),
            reads=[b_gtk22], writes=[b_gtok2]))
        add(lambda: K.op("vector", lambda h: h.tensor_reduce(
            gsc[:, :, None], seg, axis=mybir.AxisListType.X, op=AluOpType.add),
            reads=[b_gtok2], writes=[b_gsc_]))
        add(lambda: K.op("vector", lambda h: h.reciprocal(gsc2[:, :], gsc[:, :]),
            reads=[b_gsc_], writes=[b_gsc2_]))
        add(lambda: K.op("vector", lambda h: h.tensor_tensor(
            seg2, seg, gb2(), op=AluOpType.mult),
            reads=[b_gsc2_, b_gtok2], writes=[b_gtk22]))
        add(lambda: K.op("vector", lambda h: h.tensor_reduce(
            gsc[:, :, None], seg2, axis=mybir.AxisListType.X, op=AluOpType.max),
            reads=[b_gtk22], writes=[b_gsc_]))
        add(lambda: K.op("vector", lambda h: h.tensor_tensor(
            seg, seg2, gb(), op=AluOpType.is_ge),
            reads=[b_gtk22, b_gsc_], writes=[b_gtok2]))
        add(lambda: K.op("vector", lambda h: h.scalar_tensor_tensor(
            gtk3[:, :], gtok[:, :], -2.0, gtk2[:, :],
            op0=AluOpType.mult, op1=AluOpType.add),
            reads=[b_gtk22, b_gtok2], writes=[b_gtk32]))
        add(lambda: K.op("vector", lambda h: h.tensor_reduce(
            gsc[:, :, None], seg3, axis=mybir.AxisListType.X, op=AluOpType.max),
            reads=[b_gtk32], writes=[b_gsc_]))
        add(lambda: K.op("vector", lambda h: h.tensor_tensor(
            seg, seg2, gb(), op=AluOpType.is_ge),
            reads=[b_gtk22, b_gsc_], writes=[b_gtok2]))
        add(lambda: K.op("vector", lambda h: h.tensor_tensor(
            seg3, seg, seg2, op=AluOpType.mult),
            reads=[b_gtk22, b_gtok2], writes=[b_gtk32]))
        addb([(lambda tb: lambda: K.op("tensor", lambda h: h.transpose(
                  psA[0:8, 1280 + tb * 128:1280 + (tb + 1) * 128],
                  gtk3[:, tb * 8:(tb + 1) * 8], identf),
                  reads=[b_gtk32, b_cstf], writes=[b_pA2[1]]))(tb) for tb in range(GQ)] +
              [lambda: K.op("vector", lambda h: h.tensor_copy(
                  gT_sb[:, 0:QW], psA[0:8, 1280:1280 + QW]),
                  reads=[b_pA2[1]], writes=[b_gT2])])
        def mk_down():
            def mms(h):
                for c in range(HC):
                    mm = h.matmul(psA[:, 1280:1280 + QW],
                                  wd_sb[:, c * EI:(c + 1) * EI][:, 0:128],
                                  aosl(par, c), start=(c == 0), stop=(c == HC - 1))
                return mm
            return lambda: K.op("tensor", mms, reads=[b_wd] + b_AO2[par], writes=[b_pA2[1]])
        addb([
            lambda: K.op("tensor", lambda h: h.matmul(
                psA[:, 1792:1792 + QW], sel_sb[:, :], gT_sb[0:8, 0:QW], start=True, stop=True),
                reads=[b_sel, b_gT2], writes=[b_pA3[1]]),
            mk_down(),
            lambda: K.op("scalar", lambda h: h.activation(
                ystage[:, 0:QW], psA[:, 1280:1280 + QW], AF.Exp, scale=-1.0),
                reads=[b_pA2[1]], writes=[b_yst1]),
            lambda: K.op("vector", lambda h: h.tensor_scalar(
                ystage[:, QW:2 * QW], ystage[:, 0:QW], 1.0, None, op0=AluOpType.add),
                reads=[b_yst1], writes=[b_yst2[0]]),
            lambda: K.op("vector", lambda h: h.reciprocal(
                ystage[:, 2 * QW:3 * QW], ystage[:, QW:2 * QW]),
                reads=[b_yst2[0]], writes=[b_yst2[1]]),
            lambda: K.op("vector", lambda h: h.tensor_tensor(
                agst[:, 0:QW], psA[:, 1280:1280 + QW], ystage[:, 2 * QW:3 * QW],
                op=AluOpType.mult), reads=[b_pA2[1], b_yst2[1]], writes=[b_ag2[0]]),
            lambda: K.op("vector", lambda h: h.tensor_tensor(
                agst[:, QW:2 * QW], agst[:, 0:QW], psA[:, 1792:1792 + QW],
                op=AluOpType.mult), reads=[b_ag2[0], b_pA3[1]], writes=[b_ag2[1]]),
        ])
        for o in range(HC):
            po = 1280 if o % 2 == 0 else 1792
            pb = b_pA2[1] if o % 2 == 0 else b_pA3[1]
            addb([(lambda o, po, pb: lambda: K.op("tensor", lambda h: h.matmul(
                      psA[:, po:po + QW], wu_sb[:, o * 128:(o + 1) * 128],
                      agst[:, QW:2 * QW], start=True, stop=True),
                      reads=[b_wu, b_ag2[1]], writes=[pb]))(o, po, pb),
                  (lambda o, po, pb: lambda: K.op("vector", lambda h: h.tensor_tensor(
                      r1sl[:, o * QW:(o + 1) * QW], psA[:, po:po + QW], aosl(par, o),
                      op=AluOpType.add), reads=[pb, b_AO2[par][o]],
                      writes=[b_r1n[o]]))(o, po, pb)])
            accum_stats(o)
        if debug:
            add(lambda: K.op("sync", lambda h: h.dma_start(
                d_r2[qg, :, :], r1sl[:, 0:HC * QW]), reads=list(b_r1n), dma=True))
        ln_half(False)
        if debug:
            add(lambda: K.op("sync", lambda h: h.dma_start(
                d_st[qg, 1, :, :], stf[:, 0:6 * QW]), reads=list(b_stf), dma=True))
        return ops

    for qg in range(NGQ):
        for n in range(NH):
            sl = n % 2
            K.op(g, (lambda n, sl, qg: lambda h: h.dma_start(
                arena[:, q2o + sl * QW:q2o + (sl + 1) * QW],
                qscr[n, :, qg * QW:(qg + 1) * QW]))(n, sl, qg),
                waits=e_qscr[n], writes=[b_q2s[sl]], dma=True)
            for half in range(2):
                K.op(g, (lambda n, sl, half: lambda h: h.dma_start(
                    arena[:, k2o + sl * S + half * T:k2o + sl * S + (half + 1) * T],
                    kag_o[half, n, :, :]))(n, sl, half),
                    waits=[e_kag], writes=[b_k2[sl]], dma=True)
                K.op(g, (lambda n, sl, half: lambda h: h.dma_start(
                    arena[:, v2o + sl * S + half * T:v2o + sl * S + (half + 1) * T]
                    .rearrange("p (tb d) -> p tb d", d=128),
                    vag_o[half, :, n, :, :].transpose([1, 0, 2])))(n, sl, half),
                    waits=[e_vag], writes=[b_v2s[sl]], dma=True)
            for qs in range(GQ):
                qi = qg * GQ + qs
                p = qi % 2
                def smm(h, qs=qs, sl=sl, half=0):
                    for m in range(nSA):
                        w = min(512, SH - m * 512)
                        ps = psA if half == 0 else psB
                        mm = h.matmul(ps[:, m * 512:m * 512 + w],
                                      arena[:, q2o + sl * QW + qs * 128:
                                            q2o + sl * QW + (qs + 1) * 128],
                                      arena[:, k2o + sl * S + half * SH + m * 512:
                                            k2o + sl * S + half * SH + m * 512 + w],
                                      start=True, stop=True)
                    return mm
                K.op("tensor", functools.partial(smm, half=0),
                     reads=[b_k2[sl], b_q2s[sl]], writes=psA_all)
                K.op("tensor", functools.partial(smm, half=1),
                     reads=[b_k2[sl], b_q2s[sl]], writes=psB_all)
                gp = n % 2
                K.op("scalar", (lambda n, qi, p, gp, qs: lambda h: h.activation(
                    P2s[:, p * S:p * S + SH], psA[:, 0:SH], AF.Exp, scale=SC,
                    bias=mnegs[:, n * NQS + qi:n * NQS + qi + 1],
                    accum_out=lacc[:, gp * 4 + 2 * qs:gp * 4 + 2 * qs + 1]))(n, qi, p, gp, qs),
                    reads=psA_all + [b_mnegs], writes=[b_P2s[p][0], b_lacc[gp]])
                K.op("scalar", (lambda n, qi, p, gp, qs: lambda h: h.activation(
                    P2s[:, p * S + SH:p * S + 2 * SH], psB[:, 0:SH], AF.Exp, scale=SC,
                    bias=mnegs[:, n * NQS + qi:n * NQS + qi + 1],
                    accum_out=lacc[:, gp * 4 + 2 * qs + 1:gp * 4 + 2 * qs + 2]))(n, qi, p, gp, qs),
                    reads=psB_all + [b_mnegs], writes=[b_P2s[p][1], b_lacc[gp]])
                flush_pending()
                K.op("sync", (lambda p, qs: lambda h: h.dma_start_transpose(
                    PTsl[:, :].rearrange("p (j q) -> p j q", q=QW)
                    [:, :, qs * 128:(qs + 1) * 128],
                    P2s[:, p * S:(p + 1) * S]))(p, qs),
                    reads=[b_P2s[p][0], b_P2s[p][1]], writes=[b_PT],
                    dma=True, dma_sem="xd")
                if qs == GQ - 1:
                    def def_G(n=n, qg=qg, sl=sl, gp=n % 2):
                        def pv(h):
                            for kt in range(KC):
                                mm = h.matmul(psA[:, 1024:1024 + QW],
                                              arena[:, v2o + sl * S + kt * 128:
                                                    v2o + sl * S + (kt + 1) * 128],
                                              PTsl[:, kt * QW:(kt + 1) * QW],
                                              start=(kt == 0), stop=(kt == KC - 1))
                            return mm
                        K.op("tensor", pv, reads=[b_v2s[sl], b_PT], writes=[b_pA2[0]])
                        K.op("vector", (lambda gp: lambda h: h.tensor_tensor(
                            lcol2[:, 0:GQ],
                            lacc[:, 0:8].rearrange("p (g s h) -> p g s h", g=2, h=2)[:, gp, 0:GQ, 0],
                            lacc[:, 0:8].rearrange("p (g s h) -> p g s h", g=2, h=2)[:, gp, 0:GQ, 1],
                            op=AluOpType.add))(gp),
                            reads=[b_lacc[gp]], writes=[b_lc2])
                        K.op("vector", lambda h: h.reciprocal(lcol2i[:, 0:GQ], lcol2[:, 0:GQ]),
                             reads=[b_lc2], writes=[b_lc2i])
                        def ltr(h):
                            for q_ in range(GQ):
                                tr = h.transpose(psA[0:1, 1536 + q_ * 128:1536 + (q_ + 1) * 128],
                                                 lcol2i[:, q_:q_ + 1], identf)
                            return tr
                        K.op("tensor", ltr, reads=[b_lc2i, b_cstf], writes=[b_pA3[0]])
                        K.op("vector", lambda h: h.tensor_copy(
                            lirow[0:1, 0:QW], psA[0:1, 1536:1536 + QW]),
                            reads=[b_pA3[0]], writes=[b_lirow])
                        def lrepmm(h):
                            for q_ in range(GQ):
                                mm = h.matmul(psA[:, 1536 + q_ * 128:1536 + (q_ + 1) * 128],
                                              onesf_sb[0:1, 0:128],
                                              lirow[0:1, q_ * 128:(q_ + 1) * 128],
                                              start=True, stop=True)
                            return mm
                        K.op("tensor", lrepmm, reads=[b_onesf, b_lirow], writes=[b_pA3[0]])
                        K.op("vector", lambda h: h.tensor_copy(
                            lrep_sb[:, 0:QW], psA[:, 1536:1536 + QW]),
                            reads=[b_pA3[0]], writes=[b_lrep])
                        K.op("vector", (lambda n, qg: lambda h: h.tensor_tensor(
                            aosl(qg % 2, n), psA[:, 1024:1024 + QW], lrep_sb[:, 0:QW],
                            op=AluOpType.mult))(n, qg),
                            reads=[b_pA2[0], b_lrep], writes=[b_AO2[qg % 2][n]])
                    pend.append(def_G)
                if q34:
                    rem = NH * GQ - (n * GQ + qs)
                    for _ in range(min(len(q34), -(-len(q34) // rem))):
                        for f in q34.pop(0):
                            f()
        q34.extend(p3p4_tile(qg))
    flush_pending()
    for bundle in q34:
        for f in bundle:
            f()
    q34.clear()

    outs = list(last_out.values())
    K.op("sync", lambda h: None, waits=outs)
    K.mark(*outs)

    K.finalize(nc, block, sems)
    es.close()
    return nc


def prep_inputs(cfg, x_bh, w_qkv, b_qkv, w_out, b_out, router_w, down_w, up_w, own_first=True):
    H, NH, HC, S, VD, NVD, EI = cfg.H, cfg.NH, cfg.HC, cfg.S, cfg.VD, cfg.NVD, cfg.EI
    f32 = np.float32
    d = {}
    d["xT"] = np.ascontiguousarray(x_bh.T.astype(f32))
    Wq, Wk, Wv = w_qkv[0:H], w_qkv[H:2 * H], w_qkv[2 * H:3 * H]
    d["wq"] = np.ascontiguousarray(
        Wq.reshape(NH, 128, H).transpose(0, 2, 1))
    d["wk"] = np.ascontiguousarray(Wk.reshape(NH, 128, H).transpose(0, 2, 1))
    d["wv"] = np.ascontiguousarray(
        Wv.T.reshape(H, NVD, VD).transpose(1, 0, 2))
    d["wo"] = np.ascontiguousarray(w_out.reshape(HC, 128, H).transpose(0, 2, 1))
    d["bqk"] = np.ascontiguousarray(
        np.concatenate([b_qkv[0:H].reshape(NH, 128).T, b_qkv[H:2 * H].reshape(NH, 128).T], axis=1)).astype(f32)
    d["bo"] = np.ascontiguousarray(b_out.reshape(HC, 128).T).astype(f32)
    d["bv"] = b_qkv[2 * H:3 * H].reshape(1, H).astype(f32)
    d["wr"] = np.ascontiguousarray(
        router_w.T.reshape(HC, 128, 8).transpose(1, 0, 2).reshape(128, HC * 8)).astype(f32)
    Dcat = down_w.reshape(EI, H)
    d["wd"] = np.ascontiguousarray(
        Dcat.T.reshape(HC, 128, EI).transpose(1, 0, 2).reshape(128, HC * EI)).astype(f32)
    Ucat = up_w.transpose(1, 0, 2).reshape(H, EI)
    d["wu"] = np.ascontiguousarray(Ucat.T).astype(f32)
    cst = np.zeros((128, 256), dtype=BF)
    cst[:, 0:128] = np.eye(128, dtype=BF)
    sel = np.zeros((8, 128), dtype=BF)
    for e in range(8):
        sel[e, e * 16:(e + 1) * 16] = 1
    cst[0:8, 128:256] = sel
    d["cst"] = cst
    d["cstf"] = np.eye(128, dtype=f32)
    return d


from concourse.bass_utils import run_bass_kernel_spmd


def kernel(L0_attn_hidden_states, L0_attn_w_qkv, L0_attn_b_qkv, L0_attn_w_out,
           L0_attn_b_out, L0_ffn_router_w, L0_ffn_down_w, L0_ffn_up_w):
    x = np.asarray(L0_attn_hidden_states)
    w_qkv = np.asarray(L0_attn_w_qkv)
    b_qkv = np.asarray(L0_attn_b_qkv)
    w_out = np.asarray(L0_attn_w_out)
    b_out = np.asarray(L0_attn_b_out)
    router_w = np.asarray(L0_ffn_router_w)
    down_w = np.asarray(L0_ffn_down_w)
    up_w = np.asarray(L0_ffn_up_w)

    B, Sfull, H = x.shape
    cfg = Cfg(NH=16, S=4096, T=2048, QT=512, n_cores=8)
    nc = build(cfg)

    wdict = prep_inputs(cfg, np.zeros((cfg.T, H), np.float32), w_qkv, b_qkv,
                        w_out, b_out, router_w, down_w, up_w)
    del wdict["xT"]

    in_maps = []
    for c in range(8):
        b, hf = c // 2, c % 2
        m = dict(wdict)
        m["xT"] = np.ascontiguousarray(x[b, hf * cfg.T:(hf + 1) * cfg.T, :].T.astype(np.float32))
        in_maps.append(m)

    res = run_bass_kernel_spmd(nc, in_maps, core_ids=list(range(8)))

    out = np.empty((B, Sfull, H), dtype=np.float32)
    for c in range(8):
        b, hf = c // 2, c % 2
        out[b, hf * cfg.T:(hf + 1) * cfg.T, :] = res.results[c]["outT"].T
    return out


# version 25
# speedup vs baseline: 1.0108x; 1.0108x over previous
import math
import numpy as np
import ml_dtypes
from contextlib import ExitStack

import concourse.bass as bass
import concourse.mybir as mybir
from concourse.alu_op_type import AluOpType

F32 = mybir.dt.float32
BF16 = mybir.dt.bfloat16
AF = mybir.ActivationFunctionType
BF = ml_dtypes.bfloat16
ENGINES = ("tensor", "vector", "scalar", "gpsimd", "sync")
DMA_SEMS = {"vector": ["vd0"], "scalar": ["sd0", "sd1"],
            "gpsimd": ["gd0", "gd1", "gd2", "gd3"],
            "sync": ["yd0", "yd1", "yd2", "yd3"]}
ALL_SEMS = ["pe", "ve", "ac", "gp", "sy", "xd"] + [s for l in DMA_SEMS.values() for s in l]
ENG_SEM = {"tensor": "pe", "vector": "ve", "scalar": "ac", "gpsimd": "gp", "sync": "sy"}


class Ev:
    __slots__ = ("eng", "sem", "idx", "by", "used", "val")

    def __init__(self, eng, sem, idx, by):
        self.eng, self.sem, self.idx, self.by = eng, sem, idx, by
        self.used = False
        self.val = None


class Buf:
    __slots__ = ("w", "r", "name")

    def __init__(self, name=""):
        self.w = None
        self.r = []
        self.name = name


class Sched:
    def __init__(self):
        self.recs = {e: [] for e in ENGINES}
        self.dma_rot = {e: 0 for e in ENGINES}
        self.last_dma = {}
        self.tail = {e: None for e in ENGINES}
        self.pending = {e: [] for e in ENGINES}

    def barrier(self):
        evs = [e for e in self.tail.values() if e is not None]
        evs += [e for e in self.last_dma.values() if e is not None]
        for eng in ENGINES:
            self.pending[eng] = list(evs)

    def op(self, eng, fn, reads=(), writes=(), waits=(), dma=False, inc_by=None, dma_sem=None):
        if dma:
            if dma_sem is not None:
                sem = dma_sem
            else:
                sl = DMA_SEMS[eng]
                sem = sl[self.dma_rot[eng] % len(sl)]
                self.dma_rot[eng] += 1
        else:
            sem = ENG_SEM[eng]
        by = inc_by if inc_by is not None else (16 if dma else 1)
        ev = Ev(eng, sem, len(self.recs[eng]), by)
        wl = []
        if dma:
            ev.used = True
            prev = self.last_dma.get(sem)
            if prev is not None:
                wl.append(prev)
            self.last_dma[sem] = ev
        for b in reads:
            if b.w is not None:
                wl.append(b.w)
        for b in writes:
            if b.w is not None:
                wl.append(b.w)
            wl.extend(b.r)
        for e in waits:
            wl.append(e)
        seen, wl2 = set(), []
        for e in wl:
            if e is None or id(e) in seen:
                continue
            seen.add(id(e))
            wl2.append(e)
        if self.pending[eng]:
            for e in self.pending[eng]:
                if id(e) not in seen and e is not ev:
                    seen.add(id(e))
                    wl2.append(e)
            self.pending[eng] = []
        self.recs[eng].append((fn, wl2, ev))
        self.tail[eng] = ev
        for b in reads:
            b.r.append(ev)
        for b in writes:
            b.w = ev
            b.r = []
        return ev

    def mark(self, *evs):
        for e in evs:
            if e is not None:
                e.used = True

    def finalize(self, nc, block, sems):
        for eng in ENGINES:
            cnt = {}
            for fn, wl, ev in self.recs[eng]:
                for e in wl:
                    e.used = True
        for eng in ENGINES:
            cnt = {}
            for fn, wl, ev in self.recs[eng]:
                if ev.used:
                    cnt[ev.sem] = cnt.get(ev.sem, 0) + ev.by
                    ev.val = cnt[ev.sem]
        def make(eng):
            def emit(h):
                hw = {}
                for fn, wl, ev in self.recs[eng]:
                    for e in wl:
                        if e.val is None:
                            continue
                        key = e.sem
                        if hw.get(key, 0) >= e.val:
                            continue
                        h.wait_ge(sems[e.sem], e.val)
                        hw[key] = e.val
                    inst = fn(h)
                    if ev.used and ev.val is not None:
                        assert inst is not None, f"{eng} op needs inc but returned None"
                        inst.then_inc(sems[ev.sem], ev.by)
            return emit
        for eng in ENGINES:
            if not self.recs[eng]:
                continue
            getattr(block, eng)(make(eng))


class Cfg:
    def __init__(self, NH=16, S=4096, T=2048, QT=512, n_cores=8):
        self.NH = NH
        self.HC = NH
        self.H = NH * 128
        self.S = S
        self.T = T
        self.QT = QT
        self.KC = S // 128
        self.NQT = T // QT
        self.E, self.I = 8, 16
        self.EI = 128
        self.SCALE = 1.0 / math.sqrt(128.0)
        self.VD = min(512, self.H)
        self.NVD = self.H // self.VD
        self.TB = S // 128
        self.TBo = T // 128
        self.n_cores = n_cores
        self.EPS = 1e-5
        self.TW = min(2 * QT, T)
        self.NTW = T // self.TW
        self.JW = min(512, self.TW)
        self.JT = self.TW // self.JW


def build(cfg: Cfg, debug=False):
    nc = bass.Bass("TRN2", target_bir_lowering=False)
    K = Sched()
    H, S, T, QT, NH, HC, KC = cfg.H, cfg.S, cfg.T, cfg.QT, cfg.NH, cfg.HC, cfg.KC
    NQT, EI, VD, NVD, TB = cfg.NQT, cfg.EI, cfg.VD, cfg.NVD, cfg.TB
    TW, NTW, JW, JT = cfg.TW, cfg.NTW, cfg.JW, cfg.JT
    SC = cfg.SCALE
    IH = 1.0 / H

    xT = nc.dram_tensor("xT", [H, T], F32, kind="ExternalInput")
    wq_h = nc.dram_tensor("wq", [NH, HC * 128, 128], F32, kind="ExternalInput")
    wk_h = nc.dram_tensor("wk", [NH, HC * 128, 128], F32, kind="ExternalInput")
    wv_h = nc.dram_tensor("wv", [NVD, HC * 128, VD], F32, kind="ExternalInput")
    wo_h = nc.dram_tensor("wo", [HC, HC * 128, 128], F32, kind="ExternalInput")
    bqk_h = nc.dram_tensor("bqk", [128, 2 * NH], F32, kind="ExternalInput")
    bo_h = nc.dram_tensor("bo", [128, HC], F32, kind="ExternalInput")
    bv_h = nc.dram_tensor("bv", [1, H], F32, kind="ExternalInput")
    wr_h = nc.dram_tensor("wr", [128, HC * 8], F32, kind="ExternalInput")
    wd_h = nc.dram_tensor("wd", [128, HC * EI], F32, kind="ExternalInput")
    wu_h = nc.dram_tensor("wu", [EI, H], F32, kind="ExternalInput")
    cst_h = nc.dram_tensor("cst", [128, 256], BF16, kind="ExternalInput")
    cstf_h = nc.dram_tensor("cstf", [128, 128], F32, kind="ExternalInput")
    outT = nc.dram_tensor("outT", [H, T], F32, kind="ExternalOutput")

    _scr_kind = "Internal"
    TBo = cfg.TBo
    qscr = nc.dram_tensor("qscr", [NH, 128, T], BF16, kind=_scr_kind)
    kag_i = nc.dram_tensor("kag_i", [NH, 128, T], BF16)
    kag_o = nc.dram_tensor("kag_o", [2, NH, 128, T], BF16, kind=_scr_kind)
    vag_i = nc.dram_tensor("vag_i", [TBo, NH, 128, 128], BF16)
    vag_o = nc.dram_tensor("vag_o", [2, TBo, NH, 128, 128], BF16, kind=_scr_kind)
    wo_scr = nc.dram_tensor("wo_scr", [HC, 128, HC * 128], BF16, kind=_scr_kind)
    NGQ_ = T // (min(2, T // 128) * 128)
    d_ao = nc.dram_tensor("d_ao", [NGQ_, 128, HC * (T // NGQ_)], BF16, kind="ExternalOutput") if debug else None
    d_r1 = nc.dram_tensor("d_r1", [NGQ_, 128, HC * (T // NGQ_)], BF16, kind="ExternalOutput") if debug else None
    d_y = nc.dram_tensor("d_y", [NGQ_, 128, HC * (T // NGQ_)], BF16, kind="ExternalOutput") if debug else None
    d_r2 = nc.dram_tensor("d_r2", [NGQ_, 128, HC * (T // NGQ_)], BF16, kind="ExternalOutput") if debug else None
    d_st = nc.dram_tensor("d_st", [NGQ_, 2, 128, 6 * (T // NGQ_)], F32, kind="ExternalOutput") if debug else None
    d_pj = nc.dram_tensor("d_pj", [NGQ_, 2, 128, T // NGQ_], F32, kind="ExternalOutput") if debug else None
    d_xr = nc.dram_tensor("d_xr", [NGQ_, 2, 128, T // NGQ_], F32, kind="ExternalOutput") if debug else None

    es = ExitStack()
    sbt = lambda name, shape, dt: es.enter_context(nc.sbuf_tensor(name, shape, dt))

    cst = sbt("cst_sb", [128, 256], BF16)
    cstf = sbt("cstf_sb", [128, 128], F32)
    sel_sb = sbt("sel_sb", [8, 128], BF16)
    ones_sb = sbt("ones_sb", [128, 128], BF16)
    onesf_sb = sbt("onesf_sb", [1, 128], F32)
    bqk_sb = sbt("bqk_sb", [128, 2 * NH], F32)
    bo_sb = sbt("bo_sb", [128, HC], F32)
    bv_sb = sbt("bv_sb", [1, H], BF16)
    wr_sb = sbt("wr_sb", [128, HC * 8], BF16)
    wd_sb = sbt("wd_sb", [128, HC * EI], BF16)
    wu_sb = sbt("wu_sb", [128, H], BF16)

    NTB = T // 128
    def _al(x):
        return (x + 63) // 64 * 64
    A_XT = 0
    A_wv = _al(A_XT + HC * T)
    A_wqk = _al(A_wv + HC * VD)
    A_kst = _al(A_wqk + 2 * HC * 128)
    A_vst = _al(A_kst + 2 * 512)
    p1_end = _al(A_vst + 2 * 512)
    GQ = min(2, T // 128)
    QW = GQ * 128
    NGQ = T // QW
    SH = S // 2
    NQS_ = T // 128
    A_q2 = 0
    A_k2 = _al(A_q2 + 2 * T)
    A_v2 = _al(A_k2 + 2 * S)
    A_PT = _al(A_v2 + 2 * S)
    NSL = 4
    A_P2s = _al(A_PT + KC * QW)
    A_junk = _al(A_P2s + NSL * S)
    A_accQ = _al(A_junk + 1280)
    A_mnegs = _al(A_accQ + 2 * 4 * NH * NQS_)
    A_lrep = _al(A_mnegs + 2 * NH * NQS_)
    A_lac = _al(A_lrep + 2 * QW)
    A_lrw = _al(A_lac + 32 + 8)
    A_AO = _al(A_lrw + 2 * QW)
    A_r1 = _al(A_AO + 2 * HC * QW)
    A_xres = _al(A_r1 + HC * QW)
    A_yst = _al(A_xres + 2 * 2 * QW)
    A_wo = _al(A_yst + 3 * 2 * QW)
    A_ag = _al(A_wo + 2 * HC * 128)
    A_sq2 = _al(A_ag + 2 * QW)
    A_stf = _al(A_sq2 + 2 * QW)
    A_mgc = _al(A_stf + 6 * 2 * QW)
    A_ssm = _al(A_mgc + 2 * QW)
    p2_end = _al(A_ssm + 4 * QW)
    AR_COLS = _al(max(p2_end, p1_end) + 64)
    arena = sbt("arena", [128, AR_COLS], BF16)
    assert AR_COLS * 2 <= 190 * 1024, f"arena {AR_COLS * 2} B/partition too big"

    XT = arena[:, A_XT:A_XT + HC * T]
    wv_sb = arena[:, A_wv:A_wv + HC * VD]
    wqk0 = A_wqk
    kst = arena[:, A_kst:A_kst + 2 * 512]
    vst = arena[:, A_vst:A_vst + 2 * 512]
    q2o, k2o, v2o = A_q2, A_k2, A_v2
    PTsl = arena[:, A_PT:A_PT + KC * QW]
    P2s = arena[:, A_P2s:A_P2s + NSL * S]
    junk = arena[:, A_junk:A_junk + 1280]
    accQ = arena[:, A_accQ:A_accQ + 2 * 4 * NH * NQS_].bitcast(F32)
    accQI = arena[:, A_accQ:A_accQ + 2 * 4 * NH * NQS_].bitcast(mybir.dt.int32)
    mnegs = arena[:, A_mnegs:A_mnegs + 2 * NH * NQS_].bitcast(F32)
    lrep_sb = arena[:, A_lrep:A_lrep + 2 * QW].bitcast(F32)
    lacc = arena[:, A_lac:A_lac + 32].bitcast(F32)
    lcol2 = arena[:, A_lac + 32:A_lac + 36].bitcast(F32)
    lcol2i = arena[:, A_lac + 36:A_lac + 40].bitcast(F32)
    lirow = arena[0:1, A_lrw:A_lrw + 2 * QW].bitcast(F32)
    AO = arena[:, A_AO:A_AO + 2 * HC * QW]
    r1sl = arena[:, A_r1:A_r1 + HC * QW]
    x_res = arena[:, A_xres:A_xres + 2 * 2 * QW].bitcast(F32)
    ystage = arena[:, A_yst:A_yst + 3 * 2 * QW].bitcast(F32)
    wo0 = A_wo
    agst = arena[:, A_ag:A_ag + 2 * QW]
    sqst = arena[:, A_sq2:A_sq2 + 2 * QW]
    stf = arena[:, A_stf:A_stf + 6 * 2 * QW].bitcast(F32)
    stfI = arena[:, A_stf:A_stf + 6 * 2 * QW].bitcast(mybir.dt.int32)
    mgcI = arena[:, A_mgc:A_mgc + 2 * QW].bitcast(mybir.dt.int32)
    mgcF = arena[:, A_mgc:A_mgc + 2 * QW].bitcast(F32)
    ssm = arena[:, A_ssm:A_ssm + 4 * QW]
    lgf = arena[0:8, A_junk:A_junk + 2 * QW].bitcast(F32)
    gT_sb = arena[0:8, A_junk + 2 * QW:A_junk + 3 * QW]
    g0 = A_junk + 3 * QW
    gtok = arena[:, g0:g0 + 2 * GQ * 8].bitcast(F32)
    gtk2 = arena[:, g0 + 2 * GQ * 8:g0 + 4 * GQ * 8].bitcast(F32)
    gtk3 = arena[:, g0 + 4 * GQ * 8:g0 + 6 * GQ * 8].bitcast(F32)
    gsc = arena[:, g0 + 6 * GQ * 8:g0 + 6 * GQ * 8 + 2 * GQ].bitcast(F32)
    gsc2 = arena[:, g0 + 6 * GQ * 8 + 2 * GQ:g0 + 6 * GQ * 8 + 4 * GQ].bitcast(F32)
    assert 3 * QW + 6 * GQ * 8 + 4 * GQ <= 1280

    psA = es.enter_context(nc.psum_tensor("psA", [128, 2048], F32))
    psB = es.enter_context(nc.psum_tensor("psB", [128, 2048], F32))

    sems = {}
    for s in ALL_SEMS:
        sems[s] = es.enter_context(nc.semaphore("sem_" + s))
    block = es.enter_context(nc.Block())

    b_cst, b_cstf, b_sel, b_ones = Buf("cst"), Buf("cstf"), Buf("sel"), Buf("ones")
    b_onesf = Buf("onesf")
    b_bqk, b_bo, b_bv = Buf("bqk"), Buf("bo"), Buf("bv")
    b_wr, b_wd, b_wu = Buf("wr"), Buf("wd"), Buf("wu")
    b_XT = Buf("XT")
    b_wv = Buf("wv")
    b_wqk = [Buf("wqk%d" % i) for i in range(2)]
    b_kst = [Buf("kst%d" % i) for i in range(2)]
    b_vst = [Buf("vst%d" % i) for i in range(2)]
    b_psA = [Buf("psA%d" % i) for i in range(4)]
    b_psB = [Buf("psB%d" % i) for i in range(4)]
    b_pA2 = [b_psA[2], b_psA[2]]
    b_pA3 = [b_psA[3], b_psA[3]]
    b_pB2 = [b_psB[2], b_psB[2]]
    b_pB3 = [b_psB[3], b_psB[3]]
    psA_all = b_psA
    psB_all = b_psB

    def pAb(slot):
        return [b_psA[slot]]

    def pBb(slot):
        return [b_psB[slot]]
    e_kscr = [[] for _ in range(NH)]
    e_qscr = [[] for _ in range(NH)]
    e_vscr = [[] for _ in range(NH)]
    b_k2 = [Buf("k2_%d" % i) for i in range(2)]
    b_q2 = [Buf("q2_%d" % i) for i in range(2)]
    b_q2s = [Buf("q2s%d" % i) for i in range(2)]
    b_v2s = [Buf("v2s%d" % i) for i in range(2)]
    b_P2s = [[Buf("P2s%d_%d" % (i, hh)) for hh in range(2)] for i in range(NSL)]
    b_lacc = [Buf("lacc%d" % i) for i in range(2)]
    b_lc2, b_lc2i, b_lirow = Buf("lc2"), Buf("lc2i"), Buf("lirow")
    b_accQ = Buf("accQ")
    b_mnegs = Buf("mnegs")
    b_PT = Buf("PT")
    b_junk = Buf("junk")
    b_lrep = Buf("lrep")
    b_AO2 = [[Buf("AO%d_%d" % (par, n)) for n in range(HC)] for par in range(2)]
    b_wo = [Buf("wo%d" % i) for i in range(2)]
    b_x2 = [Buf("x2_%d" % i) for i in range(2)]
    b_r1n = [Buf("r1n%d" % o) for o in range(HC)]
    b_sq2 = [Buf("sq2_%d" % i) for i in range(2)]
    b_yst1 = Buf("yst1")
    b_yst2 = [Buf("yst2_%d" % i) for i in range(2)]
    b_ag2 = [Buf("ag2_%d" % i) for i in range(2)]
    b_stf = [Buf("stf%d" % i) for i in range(6)]
    b_ssm = [Buf("ssm%d" % i) for i in range(4)]
    b_lgf2, b_gT2 = Buf("lgf2"), Buf("gT2")
    b_gtok2, b_gtk22, b_gtk32 = Buf("gtok2"), Buf("gtk22"), Buf("gtk32")
    b_gsc_, b_gsc2_ = Buf("gsc_"), Buf("gsc2_")
    b_mgc = Buf("mgc")

    def aosl(par, c):
        return AO[:, par * HC * QW + c * QW:par * HC * QW + (c + 1) * QW]

    g = "gpsimd"
    K.op(g, lambda h: h.dma_start(cst[:, :], cst_h[:, :]), writes=[b_cst], dma=True)
    K.op(g, lambda h: h.dma_start(cstf[:, :], cstf_h[:, :]), writes=[b_cstf], dma=True)
    K.op(g, lambda h: h.dma_start(sel_sb[:, :], cst_h[0:8, 128:256]), writes=[b_sel], dma=True)
    K.op(g, lambda h: h.dma_start(bqk_sb[:, :], bqk_h[:, :]), writes=[b_bqk], dma=True)
    K.op(g, lambda h: h.dma_start(bo_sb[:, :], bo_h[:, :]), writes=[b_bo], dma=True)
    K.op(g, lambda h: h.dma_start(bv_sb[:, :], bv_h[:, :]), writes=[b_bv], dma=True)
    K.op(g, lambda h: h.dma_start(wr_sb[:, :], wr_h[:, :]), writes=[b_wr], dma=True)
    K.op(g, lambda h: h.dma_start(wd_sb[:, :], wd_h[:, :]), writes=[b_wd], dma=True)
    K.op(g, lambda h: h.dma_start(wu_sb[:, :], wu_h[:, :]), writes=[b_wu], dma=True)
    for c in range(HC):
        K.op(g, (lambda c: lambda h: h.dma_start(
            XT[:, c * T:(c + 1) * T], xT[c * 128:(c + 1) * 128, :]))(c),
            writes=[b_XT], dma=True)
    K.op("vector", lambda h: h.memset(ones_sb[:, :], 1.0), writes=[b_ones])
    MAGIC_F = float(np.frombuffer(np.uint32(0x5F3759DF).tobytes(), np.float32)[0])
    K.op("vector", lambda h: h.memset(mgcF[:, :], MAGIC_F), writes=[b_mgc])
    K.op("vector", lambda h: h.memset(onesf_sb[:, :], 1.0), writes=[b_onesf])

    ident = cst[:, 0:128]
    ident8f = cstf[0:8, 0:8]
    identf = cstf[:, :]

    for dt in range(NVD):
        K.op(g, (lambda dt: lambda h: h.dma_start(
            wv_sb[:, :].rearrange("p (c d) -> p c d", d=VD),
            wv_h[dt, :, :].rearrange("(c p) d -> p c d", p=128)))(dt),
            writes=[b_wv], dma=True)
        for tb in range(TBo):
            idx1 = dt * TBo + tb
            slot = idx1 % 4
            if idx1 < HC:
                wcs = idx1 % 2
                K.op(g, (lambda o, wcs: lambda h: h.dma_start(
                    arena[:, wqk0 + wcs * HC * 128:wqk0 + (wcs + 1) * HC * 128]
                    .rearrange("p (c q) -> p c q", q=128),
                    wo_h[o, :, :].rearrange("(c p) q -> p c q", p=128)))(idx1, wcs),
                    writes=[b_wqk[wcs]], dma=True)
                K.op("sync", (lambda o, wcs: lambda h: h.dma_start(
                    wo_scr[o, :, :],
                    arena[:, wqk0 + wcs * HC * 128:wqk0 + (wcs + 1) * HC * 128]))(idx1, wcs),
                    reads=[b_wqk[wcs]], dma=True)
            def mmg(h, dt=dt, tb=tb, slot=slot):
                for c in range(HC):
                    h.matmul(psA[:, slot * 512:slot * 512 + VD],
                             XT[:, c * T + tb * 128:c * T + (tb + 1) * 128],
                             wv_sb[:, c * VD:(c + 1) * VD],
                             start=(c == 0), stop=False)
                return h.matmul(psA[:, slot * 512:slot * 512 + VD],
                                ones_sb[0:1, 0:128], bv_sb[:, dt * VD:(dt + 1) * VD],
                                start=False, stop=True)
            K.op("tensor", mmg, reads=[b_XT, b_wv, b_ones, b_bv], writes=pAb(slot))
            vs = idx1 % 2
            K.op("vector", (lambda slot, vs: lambda h: h.tensor_copy(
                vst[:, vs * 512:vs * 512 + VD], psA[:, slot * 512:slot * 512 + VD]))(slot, vs),
                reads=pAb(slot), writes=[b_vst[vs]])
            nh_t = VD // 128
            def spill(h, dt=dt, tb=tb, vs=vs, nh_t=nh_t):
                return h.dma_start(vag_i[tb, dt * nh_t:(dt + 1) * nh_t, :, :].transpose([1, 0, 2]),
                                   vst[:, vs * 512:vs * 512 + VD].rearrange("p (j d) -> p j d", d=128))
            ev = K.op("sync", spill, reads=[b_vst[vs]], dma=True)
            e_vscr[0].append(ev)

    def qk_proj(which):
        span = T
        tile_w = min(512, span)
        ntile = span // tile_w
        wsrc = wk_h if which == "k" else wq_h
        scr = kag_i if which == "k" else qscr
        escr = e_kscr if which == "k" else e_qscr
        bias0 = NH if which == "k" else 0
        for n in range(NH):
            slot = n % 2
            K.op(g, (lambda n, slot, wsrc=wsrc: lambda h: h.dma_start(
                arena[:, wqk0 + slot * HC * 128: wqk0 + (slot + 1) * HC * 128]
                .rearrange("p (c o) -> p c o", o=128),
                wsrc[n, :, :].rearrange("(c p) o -> p c o", p=128)))(n, slot),
                writes=[b_wqk[slot]], dma=True)
            for j in range(ntile):
                idx = n * ntile + j
                ps = idx % 4
                def mmg(h, n=n, j=j, slot=slot, ps=ps, tile_w=tile_w):
                    for c in range(HC):
                        mm = h.matmul(psB[:, ps * 512:ps * 512 + tile_w],
                                      arena[:, wqk0 + slot * HC * 128 + c * 128:
                                            wqk0 + slot * HC * 128 + (c + 1) * 128],
                                      XT[:, c * T + j * tile_w:c * T + (j + 1) * tile_w],
                                      start=(c == 0), stop=(c == HC - 1))
                    return mm
                K.op("tensor", mmg, reads=[b_XT, b_wqk[slot]], writes=pBb(ps))
                ks = idx % 2
                K.op("scalar", (lambda ps, ks, n, tile_w: lambda h: h.activation(
                    kst[:, ks * 512:ks * 512 + tile_w], psB[:, ps * 512:ps * 512 + tile_w],
                    AF.Identity, bias=bqk_sb[:, bias0 + n:bias0 + n + 1], scale=1.0))(ps, ks, n, tile_w),
                    reads=pBb(ps) + [b_bqk], writes=[b_kst[ks]])
                ev = K.op("sync", (lambda n, j, ks, tile_w, scr=scr: lambda h: h.dma_start(
                    scr[n, :, j * tile_w:(j + 1) * tile_w],
                    kst[:, ks * 512:ks * 512 + tile_w]))(n, j, ks, tile_w),
                    reads=[b_kst[ks]], dma=True)
                escr[n].append(ev)
    qk_proj("k")

    pair_groups = [[i, i + 1] for i in range(0, cfg.n_cores, 2)]
    e_kag = K.op(g, (lambda: lambda h: h.collective_compute(
        "AllGather", AluOpType.bypass, replica_groups=pair_groups,
        ins=[kag_i.ap().opt()], outs=[kag_o.ap().opt()]))(),
        waits=[e for n in range(NH) for e in e_kscr[n]])
    e_vag = K.op(g, (lambda: lambda h: h.collective_compute(
        "AllGather", AluOpType.bypass, replica_groups=pair_groups,
        ins=[vag_i.ap().opt()], outs=[vag_o.ap().opt()]))(),
        waits=e_vscr[0])
    qk_proj("q")

    K.barrier()
    nSA = (SH + 511) // 512
    NQS = T // 128
    import functools

    nstr = max(1, KC // 4)
    NV = NH * NQS
    SSW = (nstr + 2) * 128
    for n in range(NH):
        sl = n % 2
        K.op("sync", (lambda n, sl: lambda h: h.dma_start(
            arena[:, q2o + sl * T:q2o + (sl + 1) * T], qscr[n, :, :]))(n, sl),
            waits=e_qscr[n], writes=[b_q2[sl]], dma=True)
        for half in range(2):
            K.op("sync", (lambda n, sl, half: lambda h: h.dma_start(
                arena[:, k2o + sl * S + half * T:k2o + sl * S + (half + 1) * T],
                kag_o[half, n, :, :]))(n, sl, half),
                waits=[e_kag], writes=[b_k2[sl]], dma=True)
        for qi in range(NQS):
            pp = qi % 2
            psS = psA if pp == 0 else psB
            bps = ([b_psA[0], b_psA[1]] + b_pA2) if pp == 0 else ([b_psB[0], b_psB[1]] + b_pB2)
            hn = max(1, nstr // 2)
            def smp(h, n=n, qi=qi, sl=sl, psS=psS, hn=hn):
                k2v = arena[:, k2o + sl * S:k2o + (sl + 1) * S]
                qv = arena[:, q2o + sl * T + qi * 128:q2o + sl * T + (qi + 1) * 128]
                kstr = k2v.rearrange("p (a b) -> p a b", b=512)[:, :, 0:128]
                h.matmul(psS[:, 0:hn * 128], qv, kstr[:, 0:hn, :], start=True, stop=True)
                if nstr > hn:
                    h.matmul(psS[:, 512:512 + (nstr - hn) * 128], qv, kstr[:, hn:nstr, :],
                             start=True, stop=True)
                return h.matmul(psS[:, 1024:1024 + 256],
                                qv, k2v.rearrange("p (h b) -> p h b", h=2)
                                [:, :, qi * 128:(qi + 1) * 128],
                                start=True, stop=True)
            K.op("tensor", smp, reads=[b_k2[sl], b_q2[sl]], writes=bps)
            c0 = n * NQS + qi
            if nstr == 8:
                K.op("scalar", (lambda c0, psS: lambda h: h.activation(
                    junk[:, 0:1280], psS[:, 0:1280], AF.Exp, scale=SC / 4.0,
                    accum_out=accQ[:, c0:c0 + 1]))(c0, psS),
                    reads=bps, writes=[b_junk, b_accQ])
            else:
                K.op("scalar", (lambda c0, psS, hn: lambda h: h.activation(
                    junk[:, 0:hn * 128], psS[:, 0:hn * 128], AF.Exp, scale=SC / 4.0,
                    accum_out=accQ[:, NV + c0:NV + c0 + 1]))(c0, psS, hn),
                    reads=bps, writes=[b_junk, b_accQ])
                K.op("scalar", (lambda c0, psS: lambda h: h.activation(
                    junk[:, 0:256], psS[:, 1024:1280], AF.Exp, scale=SC / 4.0,
                    accum_out=accQ[:, 2 * NV + c0:2 * NV + c0 + 1]))(c0, psS),
                    reads=bps, writes=[b_junk, b_accQ])
                K.op("vector", (lambda c0: lambda h: h.tensor_tensor(
                    accQ[:, c0:c0 + 1], accQ[:, NV + c0:NV + c0 + 1],
                    accQ[:, 2 * NV + c0:2 * NV + c0 + 1], op=AluOpType.add))(c0),
                    reads=[b_accQ], writes=[b_accQ])
        K.op("vector", (lambda n: lambda h: h.tensor_scalar(
            accQI[:, NH * NQS + n * NQS:NH * NQS + (n + 1) * NQS],
            accQI[:, n * NQS:(n + 1) * NQS], 23, None,
            op0=AluOpType.arith_shift_right))(n),
            reads=[b_accQ], writes=[b_accQ])
        K.op("vector", (lambda n: lambda h: h.tensor_copy(
            accQ[:, 2 * NH * NQS + n * NQS:2 * NH * NQS + (n + 1) * NQS],
            accQI[:, NH * NQS + n * NQS:NH * NQS + (n + 1) * NQS]))(n),
            reads=[b_accQ], writes=[b_accQ])
        K.op("vector", (lambda n: lambda h: h.tensor_scalar(
            mnegs[:, n * NQS:(n + 1) * NQS],
            accQ[:, 2 * NH * NQS + n * NQS:2 * NH * NQS + (n + 1) * NQS],
            -127.0, -4.0 * 0.6931471805599453,
            op0=AluOpType.add, op1=AluOpType.mult))(n),
            reads=[b_accQ], writes=[b_mnegs])

    K.barrier()
    pend = []

    def flush_pending():
        for f in pend:
            f()
        pend.clear()

    q34 = []
    last_out = {}

    def p3p4_tile(qg):
        par = qg % 2
        t0 = qg * QW
        ops = []
        add = lambda f: ops.append([f])
        addb = lambda fs: ops.append(list(fs))

        def accum_stats(o):
            cur = o % 2
            add((lambda o, cur: lambda: K.op("vector", lambda h: h.tensor_tensor(
                sqst[:, cur * QW:(cur + 1) * QW], r1sl[:, o * QW:(o + 1) * QW],
                r1sl[:, o * QW:(o + 1) * QW], op=AluOpType.mult),
                reads=[b_r1n[o]], writes=[b_sq2[cur]]))(o, cur))
            if o == 0:
                add(lambda: K.op("vector", lambda h: h.tensor_copy(
                    ssm[:, 0:QW], r1sl[:, 0:QW]), reads=[b_r1n[0]], writes=[b_ssm[0]]))
                add(lambda: K.op("vector", lambda h: h.tensor_copy(
                    ssm[:, 2 * QW:3 * QW], sqst[:, 0:QW]), reads=[b_sq2[0]], writes=[b_ssm[2]]))
            else:
                prv = 1 - cur
                add((lambda o, cur, prv: lambda: K.op("vector", lambda h: h.tensor_tensor(
                    ssm[:, cur * QW:(cur + 1) * QW], ssm[:, prv * QW:(prv + 1) * QW],
                    r1sl[:, o * QW:(o + 1) * QW], op=AluOpType.add),
                    reads=[b_ssm[prv], b_r1n[o]], writes=[b_ssm[cur]]))(o, cur, prv))
                add((lambda cur, prv: lambda: K.op("vector", lambda h: h.tensor_tensor(
                    ssm[:, (2 + cur) * QW:(3 + cur) * QW],
                    ssm[:, (2 + prv) * QW:(3 + prv) * QW],
                    sqst[:, cur * QW:(cur + 1) * QW], op=AluOpType.add),
                    reads=[b_ssm[2 + prv], b_sq2[cur]], writes=[b_ssm[2 + cur]]))(cur, prv))

        def ln_half(to_ao):
            fin = (HC - 1) % 2
            mu = stf[:, 0:QW]
            rs = stf[:, QW:2 * QW]
            mus = stf[:, 2 * QW:3 * QW]
            scr2 = stf[:, 3 * QW:4 * QW]
            var2 = stf[:, 4 * QW:5 * QW]
            addb([
                lambda: K.op("tensor", lambda h: h.matmul(
                    psB[:, 1280:1280 + QW], ones_sb[:, 0:128], ssm[:, fin * QW:(fin + 1) * QW],
                    start=True, stop=True), reads=[b_ssm[fin], b_ones], writes=[b_pB2[1]]),
                lambda: K.op("tensor", lambda h: h.matmul(
                    psB[:, 1792:1792 + QW], ones_sb[:, 0:128],
                    ssm[:, (2 + fin) * QW:(3 + fin) * QW],
                    start=True, stop=True), reads=[b_ssm[2 + fin], b_ones], writes=[b_pB3[1]]),
                lambda: K.op("vector", lambda h: h.tensor_scalar(
                    mu, psB[:, 1280:1280 + QW], IH, None, op0=AluOpType.mult),
                    reads=[b_pB2[1]], writes=[b_stf[0]]),
                lambda: K.op("vector", lambda h: h.tensor_scalar(
                    scr2, psB[:, 1792:1792 + QW], IH, cfg.EPS,
                    op0=AluOpType.mult, op1=AluOpType.add),
                    reads=[b_pB3[1]], writes=[b_stf[3]]),
            ])
            add(lambda: K.op("vector", lambda h: h.tensor_tensor(
                rs, mu, mu, op=AluOpType.mult), reads=[b_stf[0]], writes=[b_stf[1]]))
            add(lambda: K.op("vector", lambda h: h.tensor_tensor(
                var2, scr2, rs, op=AluOpType.subtract),
                reads=[b_stf[1], b_stf[3]], writes=[b_stf[4]]))
            add(lambda: K.op("vector", lambda h: h.tensor_scalar(
                stfI[:, 3 * QW:4 * QW], stfI[:, 4 * QW:5 * QW], 1, None,
                op0=AluOpType.arith_shift_right),
                reads=[b_stf[4]], writes=[b_stf[3]]))
            add(lambda: K.op("vector", lambda h: h.tensor_tensor(
                stfI[:, QW:2 * QW], mgcI[:, 0:QW], stfI[:, 3 * QW:4 * QW],
                op=AluOpType.subtract), reads=[b_mgc, b_stf[3]], writes=[b_stf[1]]))
            for it in range(2):
                sa, sb = (1, 2) if it % 2 == 0 else (2, 1)
                add((lambda sa: lambda: K.op("vector", lambda h: h.tensor_tensor(
                    stf[:, 3 * QW:4 * QW], stf[:, sa * QW:(sa + 1) * QW],
                    stf[:, sa * QW:(sa + 1) * QW], op=AluOpType.mult),
                    reads=[b_stf[sa]], writes=[b_stf[3]]))(sa))
                add(lambda: K.op("vector", lambda h: h.tensor_tensor(
                    stf[:, 5 * QW:6 * QW], stf[:, 3 * QW:4 * QW], var2,
                    op=AluOpType.mult), reads=[b_stf[3], b_stf[4]], writes=[b_stf[5]]))
                add(lambda: K.op("vector", lambda h: h.tensor_scalar(
                    stf[:, 3 * QW:4 * QW], stf[:, 5 * QW:6 * QW], -0.5, 1.5,
                    op0=AluOpType.mult, op1=AluOpType.add),
                    reads=[b_stf[5]], writes=[b_stf[3]]))
                add((lambda sa, sb: lambda: K.op("vector", lambda h: h.tensor_tensor(
                    stf[:, sb * QW:(sb + 1) * QW], stf[:, sa * QW:(sa + 1) * QW],
                    stf[:, 3 * QW:4 * QW], op=AluOpType.mult),
                    reads=[b_stf[sa], b_stf[3]], writes=[b_stf[sb]]))(sa, sb))
            add(lambda: K.op("vector", lambda h: h.tensor_tensor(
                mus, mu, rs, op=AluOpType.mult), reads=[b_stf[0], b_stf[1]], writes=[b_stf[2]]))
            for o in range(HC):
                ys = o % 2
                add((lambda o: lambda: K.op("vector", lambda h: h.tensor_tensor(
                    ystage[:, 0:QW], r1sl[:, o * QW:(o + 1) * QW], rs,
                    op=AluOpType.mult), reads=[b_r1n[o], b_stf[1]], writes=[b_yst1]))(o))
                add((lambda o, ys: lambda: K.op("vector", lambda h: h.tensor_tensor(
                    ystage[:, (1 + ys) * QW:(2 + ys) * QW], ystage[:, 0:QW], mus,
                    op=AluOpType.subtract),
                    reads=[b_yst1, b_stf[2]], writes=[b_yst2[ys]]))(o, ys))
                if to_ao:
                    add((lambda o, ys: lambda: K.op("scalar", lambda h: h.activation(
                        aosl(par, o), ystage[:, (1 + ys) * QW:(2 + ys) * QW], AF.Copy),
                        reads=[b_yst2[ys]], writes=[b_AO2[par][o]]))(o, ys))
                else:
                    def mk_out(o=o, ys=ys):
                        def go():
                            ev = K.op("scalar", lambda h: h.dma_start(
                                outT[o * 128:(o + 1) * 128, t0:t0 + QW],
                                ystage[:, (1 + ys) * QW:(2 + ys) * QW]),
                                reads=[b_yst2[ys]], dma=True)
                            last_out[ev.sem] = ev
                        return go
                    add(mk_out())

        if debug:
            add(lambda: K.op(g, lambda h: h.dma_start(
                d_ao[qg, :, :], AO[:, par * HC * QW:(par + 1) * HC * QW]),
                reads=list(b_AO2[par]), dma=True))
        for o in range(HC):
            ws = o % 2
            po = 1280 if o % 2 == 0 else 1792
            pb = b_pA2[1] if o % 2 == 0 else b_pA3[1]
            add((lambda o, ws: lambda: K.op(g, lambda h: h.dma_start(
                arena[:, wo0 + ws * HC * 128:wo0 + (ws + 1) * HC * 128], wo_scr[o, :, :]),
                writes=[b_wo[ws]], dma=True))(o, ws))
            add((lambda o: lambda: K.op(g, lambda h: h.dma_start(
                x_res[:, (o % 2) * QW:(o % 2) * QW + QW],
                xT[o * 128:(o + 1) * 128, t0:t0 + QW]),
                writes=[b_x2[o % 2]], dma=True))(o))
            def mk_proj(o=o, ws=ws, po=po, pb=pb):
                def mms(h):
                    for c in range(HC):
                        mm = h.matmul(psA[:, po:po + QW],
                                      arena[:, wo0 + ws * HC * 128 + c * 128:
                                            wo0 + ws * HC * 128 + (c + 1) * 128],
                                      aosl(par, c), start=(c == 0), stop=(c == HC - 1))
                    return mm
                return lambda: K.op("tensor", mms, reads=[b_wo[ws]] + b_AO2[par], writes=[pb])
            bnd = [mk_proj()]
            if debug and o < 2:
                bnd.append((lambda o, po, pb: lambda: K.op("vector", lambda h: h.tensor_copy(
                    ystage[:, 0:QW], psA[:, po:po + QW]),
                    reads=[pb], writes=[b_yst1]))(o, po, pb))
                bnd.append((lambda o: lambda: K.op(g, lambda h: h.dma_start(
                    d_pj[qg, o, :, :], ystage[:, 0:QW]), reads=[b_yst1], dma=True))(o))
                bnd.append((lambda o: lambda: K.op(g, lambda h: h.dma_start(
                    d_xr[qg, o, :, :], x_res[:, (o % 2) * QW:(o % 2) * QW + QW]),
                    reads=[b_x2[o % 2]], dma=True))(o))
            bnd.append((lambda o, po, pb: lambda: K.op("vector", lambda h: h.scalar_tensor_tensor(
                r1sl[:, o * QW:(o + 1) * QW], psA[:, po:po + QW], bo_sb[:, o:o + 1],
                x_res[:, (o % 2) * QW:(o % 2) * QW + QW],
                op0=AluOpType.add, op1=AluOpType.add),
                reads=[pb, b_bo, b_x2[o % 2]], writes=[b_r1n[o]]))(o, po, pb))
            addb(bnd)
            accum_stats(o)
        if debug:
            add(lambda: K.op(g, lambda h: h.dma_start(
                d_r1[qg, :, :], r1sl[:, 0:HC * QW]), reads=list(b_r1n), dma=True))
        ln_half(True)
        if debug:
            add(lambda: K.op(g, lambda h: h.dma_start(
                d_y[qg, :, :], AO[:, par * HC * QW:(par + 1) * HC * QW]),
                reads=list(b_AO2[par]), dma=True))
            add(lambda: K.op(g, lambda h: h.dma_start(
                d_st[qg, 0, :, :], stf[:, 0:6 * QW]), reads=list(b_stf), dma=True))

        def mk_router():
            def mms(h):
                for c in range(HC):
                    mm = h.matmul(psA[0:8, 1280:1280 + QW], wr_sb[:, c * 8:(c + 1) * 8],
                                  aosl(par, c), start=(c == 0), stop=(c == HC - 1))
                return mm
            return lambda: K.op("tensor", mms, reads=[b_wr] + b_AO2[par], writes=[b_pA2[1]])
        addb([mk_router(),
              lambda: K.op("vector", lambda h: h.tensor_copy(
                  lgf[:, 0:QW], psA[0:8, 1280:1280 + QW]), reads=[b_pA2[1]], writes=[b_lgf2])])
        addb([(lambda tb: lambda: K.op("tensor", lambda h: h.transpose(
                  psA[:, 1792 + tb * 8:1792 + tb * 8 + 8],
                  lgf[0:8, tb * 128:(tb + 1) * 128], ident8f),
                  reads=[b_lgf2, b_cstf], writes=[b_pA3[1]]))(tb) for tb in range(GQ)] +
              [lambda: K.op("vector", lambda h: h.tensor_copy(
                  gtok[:, 0:GQ * 8], psA[:, 1792:1792 + GQ * 8]),
                  reads=[b_pA3[1]], writes=[b_gtok2])])
        seg = gtok[:, :].rearrange("p (s e) -> p s e", e=8)
        seg2 = gtk2[:, :].rearrange("p (s e) -> p s e", e=8)
        seg3 = gtk3[:, :].rearrange("p (s e) -> p s e", e=8)
        gb = lambda: gsc[:, :, None].broadcast_to((128, GQ, 8))
        gb2 = lambda: gsc2[:, :, None].broadcast_to((128, GQ, 8))
        add(lambda: K.op("vector", lambda h: h.tensor_reduce(
            gsc[:, :, None], seg, axis=mybir.AxisListType.X, op=AluOpType.max),
            reads=[b_gtok2], writes=[b_gsc_]))
        add(lambda: K.op("vector", lambda h: h.tensor_tensor(
            seg2, seg, gb(), op=AluOpType.subtract),
            reads=[b_gtok2, b_gsc_], writes=[b_gtk22]))
        add(lambda: K.op("scalar", lambda h: h.activation(gtok[:, :], gtk2[:, :], AF.Exp),
            reads=[b_gtk22], writes=[b_gtok2]))
        add(lambda: K.op("vector", lambda h: h.tensor_reduce(
            gsc[:, :, None], seg, axis=mybir.AxisListType.X, op=AluOpType.add),
            reads=[b_gtok2], writes=[b_gsc_]))
        add(lambda: K.op("vector", lambda h: h.reciprocal(gsc2[:, :], gsc[:, :]),
            reads=[b_gsc_], writes=[b_gsc2_]))
        add(lambda: K.op("vector", lambda h: h.tensor_tensor(
            seg2, seg, gb2(), op=AluOpType.mult),
            reads=[b_gsc2_, b_gtok2], writes=[b_gtk22]))
        add(lambda: K.op("vector", lambda h: h.tensor_reduce(
            gsc[:, :, None], seg2, axis=mybir.AxisListType.X, op=AluOpType.max),
            reads=[b_gtk22], writes=[b_gsc_]))
        add(lambda: K.op("vector", lambda h: h.tensor_tensor(
            seg, seg2, gb(), op=AluOpType.is_ge),
            reads=[b_gtk22, b_gsc_], writes=[b_gtok2]))
        add(lambda: K.op("vector", lambda h: h.scalar_tensor_tensor(
            gtk3[:, :], gtok[:, :], -2.0, gtk2[:, :],
            op0=AluOpType.mult, op1=AluOpType.add),
            reads=[b_gtk22, b_gtok2], writes=[b_gtk32]))
        add(lambda: K.op("vector", lambda h: h.tensor_reduce(
            gsc[:, :, None], seg3, axis=mybir.AxisListType.X, op=AluOpType.max),
            reads=[b_gtk32], writes=[b_gsc_]))
        add(lambda: K.op("vector", lambda h: h.tensor_tensor(
            seg, seg2, gb(), op=AluOpType.is_ge),
            reads=[b_gtk22, b_gsc_], writes=[b_gtok2]))
        add(lambda: K.op("vector", lambda h: h.tensor_tensor(
            seg3, seg, seg2, op=AluOpType.mult),
            reads=[b_gtk22, b_gtok2], writes=[b_gtk32]))
        addb([(lambda tb: lambda: K.op("tensor", lambda h: h.transpose(
                  psA[0:8, 1280 + tb * 128:1280 + (tb + 1) * 128],
                  gtk3[:, tb * 8:(tb + 1) * 8], identf),
                  reads=[b_gtk32, b_cstf], writes=[b_pA2[1]]))(tb) for tb in range(GQ)] +
              [lambda: K.op("vector", lambda h: h.tensor_copy(
                  gT_sb[:, 0:QW], psA[0:8, 1280:1280 + QW]),
                  reads=[b_pA2[1]], writes=[b_gT2])])
        def mk_down():
            def mms(h):
                for c in range(HC):
                    mm = h.matmul(psA[:, 1280:1280 + QW],
                                  wd_sb[:, c * EI:(c + 1) * EI][:, 0:128],
                                  aosl(par, c), start=(c == 0), stop=(c == HC - 1))
                return mm
            return lambda: K.op("tensor", mms, reads=[b_wd] + b_AO2[par], writes=[b_pA2[1]])
        addb([
            lambda: K.op("tensor", lambda h: h.matmul(
                psA[:, 1792:1792 + QW], sel_sb[:, :], gT_sb[0:8, 0:QW], start=True, stop=True),
                reads=[b_sel, b_gT2], writes=[b_pA3[1]]),
            mk_down(),
            lambda: K.op("scalar", lambda h: h.activation(
                ystage[:, 0:QW], psA[:, 1280:1280 + QW], AF.Exp, scale=-1.0),
                reads=[b_pA2[1]], writes=[b_yst1]),
            lambda: K.op("vector", lambda h: h.tensor_scalar(
                ystage[:, QW:2 * QW], ystage[:, 0:QW], 1.0, None, op0=AluOpType.add),
                reads=[b_yst1], writes=[b_yst2[0]]),
            lambda: K.op("vector", lambda h: h.reciprocal(
                ystage[:, 2 * QW:3 * QW], ystage[:, QW:2 * QW]),
                reads=[b_yst2[0]], writes=[b_yst2[1]]),
            lambda: K.op("vector", lambda h: h.tensor_tensor(
                agst[:, 0:QW], psA[:, 1280:1280 + QW], ystage[:, 2 * QW:3 * QW],
                op=AluOpType.mult), reads=[b_pA2[1], b_yst2[1]], writes=[b_ag2[0]]),
            lambda: K.op("vector", lambda h: h.tensor_tensor(
                agst[:, QW:2 * QW], agst[:, 0:QW], psA[:, 1792:1792 + QW],
                op=AluOpType.mult), reads=[b_ag2[0], b_pA3[1]], writes=[b_ag2[1]]),
        ])
        for o in range(HC):
            po = 1280 if o % 2 == 0 else 1792
            pb = b_pA2[1] if o % 2 == 0 else b_pA3[1]
            addb([(lambda o, po, pb: lambda: K.op("tensor", lambda h: h.matmul(
                      psA[:, po:po + QW], wu_sb[:, o * 128:(o + 1) * 128],
                      agst[:, QW:2 * QW], start=True, stop=True),
                      reads=[b_wu, b_ag2[1]], writes=[pb]))(o, po, pb),
                  (lambda o, po, pb: lambda: K.op("vector", lambda h: h.tensor_tensor(
                      r1sl[:, o * QW:(o + 1) * QW], psA[:, po:po + QW], aosl(par, o),
                      op=AluOpType.add), reads=[pb, b_AO2[par][o]],
                      writes=[b_r1n[o]]))(o, po, pb)])
            accum_stats(o)
        if debug:
            add(lambda: K.op("sync", lambda h: h.dma_start(
                d_r2[qg, :, :], r1sl[:, 0:HC * QW]), reads=list(b_r1n), dma=True))
        ln_half(False)
        if debug:
            add(lambda: K.op(g, lambda h: h.dma_start(
                d_st[qg, 1, :, :], stf[:, 0:6 * QW]), reads=list(b_stf), dma=True))
        return ops

    for qg in range(NGQ):
        for n in range(NH):
            sl = n % 2
            K.op(g, (lambda n, sl, qg: lambda h: h.dma_start(
                arena[:, q2o + sl * QW:q2o + (sl + 1) * QW],
                qscr[n, :, qg * QW:(qg + 1) * QW]))(n, sl, qg),
                waits=e_qscr[n], writes=[b_q2s[sl]], dma=True)
            for half in range(2):
                K.op(g, (lambda n, sl, half: lambda h: h.dma_start(
                    arena[:, k2o + sl * S + half * T:k2o + sl * S + (half + 1) * T],
                    kag_o[half, n, :, :]))(n, sl, half),
                    waits=[e_kag], writes=[b_k2[sl]], dma=True)
                K.op(g, (lambda n, sl, half: lambda h: h.dma_start(
                    arena[:, v2o + sl * S + half * T:v2o + sl * S + (half + 1) * T]
                    .rearrange("p (tb d) -> p tb d", d=128),
                    vag_o[half, :, n, :, :].transpose([1, 0, 2])))(n, sl, half),
                    waits=[e_vag], writes=[b_v2s[sl]], dma=True)
            for qs in range(GQ):
                qi = qg * GQ + qs
                p = qi % NSL
                def smm(h, qs=qs, sl=sl, half=0):
                    for m in range(nSA):
                        w = min(512, SH - m * 512)
                        ps = psA if half == 0 else psB
                        mm = h.matmul(ps[:, m * 512:m * 512 + w],
                                      arena[:, q2o + sl * QW + qs * 128:
                                            q2o + sl * QW + (qs + 1) * 128],
                                      arena[:, k2o + sl * S + half * SH + m * 512:
                                            k2o + sl * S + half * SH + m * 512 + w],
                                      start=True, stop=True)
                    return mm
                K.op("tensor", functools.partial(smm, half=0),
                     reads=[b_k2[sl], b_q2s[sl]], writes=psA_all)
                K.op("tensor", functools.partial(smm, half=1),
                     reads=[b_k2[sl], b_q2s[sl]], writes=psB_all)
                gp = n % 2
                K.op("scalar", (lambda n, qi, p, gp, qs: lambda h: h.activation(
                    P2s[:, p * S:p * S + SH], psA[:, 0:SH], AF.Exp, scale=SC,
                    bias=mnegs[:, n * NQS + qi:n * NQS + qi + 1],
                    accum_out=lacc[:, gp * 4 + 2 * qs:gp * 4 + 2 * qs + 1]))(n, qi, p, gp, qs),
                    reads=psA_all + [b_mnegs], writes=[b_P2s[p][0], b_lacc[gp]])
                K.op("scalar", (lambda n, qi, p, gp, qs: lambda h: h.activation(
                    P2s[:, p * S + SH:p * S + 2 * SH], psB[:, 0:SH], AF.Exp, scale=SC,
                    bias=mnegs[:, n * NQS + qi:n * NQS + qi + 1],
                    accum_out=lacc[:, gp * 4 + 2 * qs + 1:gp * 4 + 2 * qs + 2]))(n, qi, p, gp, qs),
                    reads=psB_all + [b_mnegs], writes=[b_P2s[p][1], b_lacc[gp]])
                flush_pending()
                K.op("sync", (lambda p, qs: lambda h: h.dma_start_transpose(
                    PTsl[:, :].rearrange("p (j q) -> p j q", q=QW)
                    [:, :, qs * 128:(qs + 1) * 128],
                    P2s[:, p * S:(p + 1) * S]))(p, qs),
                    reads=[b_P2s[p][0], b_P2s[p][1]], writes=[b_PT],
                    dma=True, dma_sem="xd")
                if qs == GQ - 1:
                    def def_G(n=n, qg=qg, sl=sl, gp=n % 2):
                        K.op("vector", (lambda gp: lambda h: h.tensor_tensor(
                            lcol2[:, 0:GQ],
                            lacc[:, 0:8].rearrange("p (g s h) -> p g s h", g=2, h=2)[:, gp, 0:GQ, 0],
                            lacc[:, 0:8].rearrange("p (g s h) -> p g s h", g=2, h=2)[:, gp, 0:GQ, 1],
                            op=AluOpType.add))(gp),
                            reads=[b_lacc[gp]], writes=[b_lc2])
                        K.op("vector", lambda h: h.reciprocal(lcol2i[:, 0:GQ], lcol2[:, 0:GQ]),
                             reads=[b_lc2], writes=[b_lc2i])
                        def ltr(h):
                            for q_ in range(GQ):
                                tr = h.transpose(psA[0:1, 1536 + q_ * 128:1536 + (q_ + 1) * 128],
                                                 lcol2i[:, q_:q_ + 1], identf)
                            return tr
                        K.op("tensor", ltr, reads=[b_lc2i, b_cstf], writes=[b_pA3[0]])
                        K.op("vector", lambda h: h.tensor_copy(
                            lirow[0:1, 0:QW], psA[0:1, 1536:1536 + QW]),
                            reads=[b_pA3[0]], writes=[b_lirow])
                        def lrepmm(h):
                            for q_ in range(GQ):
                                mm = h.matmul(psA[:, 1536 + q_ * 128:1536 + (q_ + 1) * 128],
                                              onesf_sb[0:1, 0:128],
                                              lirow[0:1, q_ * 128:(q_ + 1) * 128],
                                              start=True, stop=True)
                            return mm
                        K.op("tensor", lrepmm, reads=[b_onesf, b_lirow], writes=[b_pA3[0]])
                        K.op("vector", lambda h: h.tensor_copy(
                            lrep_sb[:, 0:QW], psA[:, 1536:1536 + QW]),
                            reads=[b_pA3[0]], writes=[b_lrep])
                        def pv(h):
                            for kt in range(KC):
                                mm = h.matmul(psA[:, 1024:1024 + QW],
                                              arena[:, v2o + sl * S + kt * 128:
                                                    v2o + sl * S + (kt + 1) * 128],
                                              PTsl[:, kt * QW:(kt + 1) * QW],
                                              start=(kt == 0), stop=(kt == KC - 1))
                            return mm
                        K.op("tensor", pv, reads=[b_v2s[sl], b_PT], writes=[b_pA2[0]])
                        K.op("vector", (lambda n, qg: lambda h: h.tensor_tensor(
                            aosl(qg % 2, n), psA[:, 1024:1024 + QW], lrep_sb[:, 0:QW],
                            op=AluOpType.mult))(n, qg),
                            reads=[b_pA2[0], b_lrep], writes=[b_AO2[qg % 2][n]])
                    pend.append(def_G)
                if q34:
                    rem = NH * GQ - (n * GQ + qs)
                    for _ in range(min(len(q34), -(-len(q34) // rem))):
                        for f in q34.pop(0):
                            f()
        q34.extend(p3p4_tile(qg))
    flush_pending()
    for bundle in q34:
        for f in bundle:
            f()
    q34.clear()

    outs = list(last_out.values())
    K.op("sync", lambda h: None, waits=outs)
    K.mark(*outs)

    K.finalize(nc, block, sems)
    es.close()
    return nc


def prep_inputs(cfg, x_bh, w_qkv, b_qkv, w_out, b_out, router_w, down_w, up_w, own_first=True):
    H, NH, HC, S, VD, NVD, EI = cfg.H, cfg.NH, cfg.HC, cfg.S, cfg.VD, cfg.NVD, cfg.EI
    f32 = np.float32
    d = {}
    d["xT"] = np.ascontiguousarray(x_bh.T.astype(f32))
    Wq, Wk, Wv = w_qkv[0:H], w_qkv[H:2 * H], w_qkv[2 * H:3 * H]
    d["wq"] = np.ascontiguousarray(
        Wq.reshape(NH, 128, H).transpose(0, 2, 1))
    d["wk"] = np.ascontiguousarray(Wk.reshape(NH, 128, H).transpose(0, 2, 1))
    d["wv"] = np.ascontiguousarray(
        Wv.T.reshape(H, NVD, VD).transpose(1, 0, 2))
    d["wo"] = np.ascontiguousarray(w_out.reshape(HC, 128, H).transpose(0, 2, 1))
    d["bqk"] = np.ascontiguousarray(
        np.concatenate([b_qkv[0:H].reshape(NH, 128).T, b_qkv[H:2 * H].reshape(NH, 128).T], axis=1)).astype(f32)
    d["bo"] = np.ascontiguousarray(b_out.reshape(HC, 128).T).astype(f32)
    d["bv"] = b_qkv[2 * H:3 * H].reshape(1, H).astype(f32)
    d["wr"] = np.ascontiguousarray(
        router_w.T.reshape(HC, 128, 8).transpose(1, 0, 2).reshape(128, HC * 8)).astype(f32)
    Dcat = down_w.reshape(EI, H)
    d["wd"] = np.ascontiguousarray(
        Dcat.T.reshape(HC, 128, EI).transpose(1, 0, 2).reshape(128, HC * EI)).astype(f32)
    Ucat = up_w.transpose(1, 0, 2).reshape(H, EI)
    d["wu"] = np.ascontiguousarray(Ucat.T).astype(f32)
    cst = np.zeros((128, 256), dtype=BF)
    cst[:, 0:128] = np.eye(128, dtype=BF)
    sel = np.zeros((8, 128), dtype=BF)
    for e in range(8):
        sel[e, e * 16:(e + 1) * 16] = 1
    cst[0:8, 128:256] = sel
    d["cst"] = cst
    d["cstf"] = np.eye(128, dtype=f32)
    return d


from concourse.bass_utils import run_bass_kernel_spmd


def kernel(L0_attn_hidden_states, L0_attn_w_qkv, L0_attn_b_qkv, L0_attn_w_out,
           L0_attn_b_out, L0_ffn_router_w, L0_ffn_down_w, L0_ffn_up_w):
    x = np.asarray(L0_attn_hidden_states)
    w_qkv = np.asarray(L0_attn_w_qkv)
    b_qkv = np.asarray(L0_attn_b_qkv)
    w_out = np.asarray(L0_attn_w_out)
    b_out = np.asarray(L0_attn_b_out)
    router_w = np.asarray(L0_ffn_router_w)
    down_w = np.asarray(L0_ffn_down_w)
    up_w = np.asarray(L0_ffn_up_w)

    B, Sfull, H = x.shape
    cfg = Cfg(NH=16, S=4096, T=2048, QT=512, n_cores=8)
    nc = build(cfg)

    wdict = prep_inputs(cfg, np.zeros((cfg.T, H), np.float32), w_qkv, b_qkv,
                        w_out, b_out, router_w, down_w, up_w)
    del wdict["xT"]

    in_maps = []
    for c in range(8):
        b, hf = c // 2, c % 2
        m = dict(wdict)
        m["xT"] = np.ascontiguousarray(x[b, hf * cfg.T:(hf + 1) * cfg.T, :].T.astype(np.float32))
        in_maps.append(m)

    res = run_bass_kernel_spmd(nc, in_maps, core_ids=list(range(8)))

    out = np.empty((B, Sfull, H), dtype=np.float32)
    for c in range(8):
        b, hf = c // 2, c % 2
        out[b, hf * cfg.T:(hf + 1) * cfg.T, :] = res.results[c]["outT"].T
    return out


# version 26
# speedup vs baseline: 1.0244x; 1.0134x over previous
import math
import numpy as np
import ml_dtypes
from contextlib import ExitStack

import concourse.bass as bass
import concourse.mybir as mybir
from concourse.alu_op_type import AluOpType

F32 = mybir.dt.float32
BF16 = mybir.dt.bfloat16
AF = mybir.ActivationFunctionType
BF = ml_dtypes.bfloat16
ENGINES = ("tensor", "vector", "scalar", "gpsimd", "sync")
DMA_SEMS = {"vector": ["vd0"], "scalar": ["sd0", "sd1"],
            "gpsimd": ["gd0", "gd1", "gd2", "gd3"],
            "sync": ["yd0", "yd1", "yd2", "yd3"]}
ALL_SEMS = ["pe", "ve", "ac", "gp", "sy", "xd"] + [s for l in DMA_SEMS.values() for s in l]
ENG_SEM = {"tensor": "pe", "vector": "ve", "scalar": "ac", "gpsimd": "gp", "sync": "sy"}


class Ev:
    __slots__ = ("eng", "sem", "idx", "by", "used", "val")

    def __init__(self, eng, sem, idx, by):
        self.eng, self.sem, self.idx, self.by = eng, sem, idx, by
        self.used = False
        self.val = None


class Buf:
    __slots__ = ("w", "r", "name")

    def __init__(self, name=""):
        self.w = None
        self.r = []
        self.name = name


class Sched:
    def __init__(self):
        self.recs = {e: [] for e in ENGINES}
        self.dma_rot = {e: 0 for e in ENGINES}
        self.last_dma = {}
        self.tail = {e: None for e in ENGINES}
        self.pending = {e: [] for e in ENGINES}

    def barrier(self):
        evs = [e for e in self.tail.values() if e is not None]
        evs += [e for e in self.last_dma.values() if e is not None]
        for eng in ENGINES:
            self.pending[eng] = list(evs)

    def op(self, eng, fn, reads=(), writes=(), waits=(), dma=False, inc_by=None, dma_sem=None):
        if dma:
            if dma_sem is not None:
                sem = dma_sem
            else:
                sl = DMA_SEMS[eng]
                sem = sl[self.dma_rot[eng] % len(sl)]
                self.dma_rot[eng] += 1
        else:
            sem = ENG_SEM[eng]
        by = inc_by if inc_by is not None else (16 if dma else 1)
        ev = Ev(eng, sem, len(self.recs[eng]), by)
        wl = []
        if dma:
            ev.used = True
            prev = self.last_dma.get(sem)
            if prev is not None:
                wl.append(prev)
            self.last_dma[sem] = ev
        for b in reads:
            if b.w is not None:
                wl.append(b.w)
        for b in writes:
            if b.w is not None:
                wl.append(b.w)
            wl.extend(b.r)
        for e in waits:
            wl.append(e)
        seen, wl2 = set(), []
        for e in wl:
            if e is None or id(e) in seen:
                continue
            seen.add(id(e))
            wl2.append(e)
        if self.pending[eng]:
            for e in self.pending[eng]:
                if id(e) not in seen and e is not ev:
                    seen.add(id(e))
                    wl2.append(e)
            self.pending[eng] = []
        self.recs[eng].append((fn, wl2, ev))
        self.tail[eng] = ev
        for b in reads:
            b.r.append(ev)
        for b in writes:
            b.w = ev
            b.r = []
        return ev

    def mark(self, *evs):
        for e in evs:
            if e is not None:
                e.used = True

    def finalize(self, nc, block, sems):
        for eng in ENGINES:
            cnt = {}
            for fn, wl, ev in self.recs[eng]:
                for e in wl:
                    e.used = True
        for eng in ENGINES:
            cnt = {}
            for fn, wl, ev in self.recs[eng]:
                if ev.used:
                    cnt[ev.sem] = cnt.get(ev.sem, 0) + ev.by
                    ev.val = cnt[ev.sem]
        def make(eng):
            def emit(h):
                hw = {}
                for fn, wl, ev in self.recs[eng]:
                    for e in wl:
                        if e.val is None:
                            continue
                        key = e.sem
                        if hw.get(key, 0) >= e.val:
                            continue
                        h.wait_ge(sems[e.sem], e.val)
                        hw[key] = e.val
                    inst = fn(h)
                    if ev.used and ev.val is not None:
                        assert inst is not None, f"{eng} op needs inc but returned None"
                        inst.then_inc(sems[ev.sem], ev.by)
            return emit
        for eng in ENGINES:
            if not self.recs[eng]:
                continue
            getattr(block, eng)(make(eng))


class Cfg:
    def __init__(self, NH=16, S=4096, T=2048, QT=512, n_cores=8):
        self.NH = NH
        self.HC = NH
        self.H = NH * 128
        self.S = S
        self.T = T
        self.QT = QT
        self.KC = S // 128
        self.NQT = T // QT
        self.E, self.I = 8, 16
        self.EI = 128
        self.SCALE = 1.0 / math.sqrt(128.0)
        self.VD = min(512, self.H)
        self.NVD = self.H // self.VD
        self.TB = S // 128
        self.TBo = T // 128
        self.n_cores = n_cores
        self.EPS = 1e-5
        self.TW = min(2 * QT, T)
        self.NTW = T // self.TW
        self.JW = min(512, self.TW)
        self.JT = self.TW // self.JW


def build(cfg: Cfg, debug=False):
    nc = bass.Bass("TRN2", target_bir_lowering=False)
    K = Sched()
    H, S, T, QT, NH, HC, KC = cfg.H, cfg.S, cfg.T, cfg.QT, cfg.NH, cfg.HC, cfg.KC
    NQT, EI, VD, NVD, TB = cfg.NQT, cfg.EI, cfg.VD, cfg.NVD, cfg.TB
    TW, NTW, JW, JT = cfg.TW, cfg.NTW, cfg.JW, cfg.JT
    SC = cfg.SCALE
    IH = 1.0 / H

    xT = nc.dram_tensor("xT", [H, T], F32, kind="ExternalInput")
    wq_h = nc.dram_tensor("wq", [NH, HC * 128, 128], F32, kind="ExternalInput")
    wk_h = nc.dram_tensor("wk", [NH, HC * 128, 128], F32, kind="ExternalInput")
    wv_h = nc.dram_tensor("wv", [NVD, HC * 128, VD], F32, kind="ExternalInput")
    wo_h = nc.dram_tensor("wo", [HC, HC * 128, 128], F32, kind="ExternalInput")
    bqk_h = nc.dram_tensor("bqk", [128, 2 * NH], F32, kind="ExternalInput")
    bo_h = nc.dram_tensor("bo", [128, HC], F32, kind="ExternalInput")
    bv_h = nc.dram_tensor("bv", [1, H], F32, kind="ExternalInput")
    wr_h = nc.dram_tensor("wr", [128, HC * 8], F32, kind="ExternalInput")
    wd_h = nc.dram_tensor("wd", [128, HC * EI], F32, kind="ExternalInput")
    wu_h = nc.dram_tensor("wu", [EI, H], F32, kind="ExternalInput")
    cst_h = nc.dram_tensor("cst", [128, 256], BF16, kind="ExternalInput")
    cstf_h = nc.dram_tensor("cstf", [128, 128], F32, kind="ExternalInput")
    outT = nc.dram_tensor("outT", [H, T], F32, kind="ExternalOutput")

    _scr_kind = "Internal"
    TBo = cfg.TBo
    qscr = nc.dram_tensor("qscr", [NH, 128, T], BF16, kind=_scr_kind)
    kag_i = nc.dram_tensor("kag_i", [NH, 128, T], BF16)
    kag_o = nc.dram_tensor("kag_o", [2, NH, 128, T], BF16, kind=_scr_kind)
    vag_i = nc.dram_tensor("vag_i", [TBo, NH, 128, 128], BF16)
    vag_o = nc.dram_tensor("vag_o", [2, TBo, NH, 128, 128], BF16, kind=_scr_kind)
    wo_scr = nc.dram_tensor("wo_scr", [HC, 128, HC * 128], BF16, kind=_scr_kind)
    NGQ_ = T // (min(2, T // 128) * 128)
    d_ao = nc.dram_tensor("d_ao", [NGQ_, 128, HC * (T // NGQ_)], BF16, kind="ExternalOutput") if debug else None
    d_r1 = nc.dram_tensor("d_r1", [NGQ_, 128, HC * (T // NGQ_)], BF16, kind="ExternalOutput") if debug else None
    d_y = nc.dram_tensor("d_y", [NGQ_, 128, HC * (T // NGQ_)], BF16, kind="ExternalOutput") if debug else None
    d_r2 = nc.dram_tensor("d_r2", [NGQ_, 128, HC * (T // NGQ_)], BF16, kind="ExternalOutput") if debug else None
    d_st = nc.dram_tensor("d_st", [NGQ_, 2, 128, 6 * (T // NGQ_)], F32, kind="ExternalOutput") if debug else None
    d_pj = nc.dram_tensor("d_pj", [NGQ_, 2, 128, T // NGQ_], F32, kind="ExternalOutput") if debug else None
    d_xr = nc.dram_tensor("d_xr", [NGQ_, 2, 128, T // NGQ_], F32, kind="ExternalOutput") if debug else None

    es = ExitStack()
    sbt = lambda name, shape, dt: es.enter_context(nc.sbuf_tensor(name, shape, dt))

    cst = sbt("cst_sb", [128, 256], BF16)
    cstf = sbt("cstf_sb", [128, 128], F32)
    sel_sb = sbt("sel_sb", [8, 128], BF16)
    ones_sb = sbt("ones_sb", [128, 128], BF16)
    onesf_sb = sbt("onesf_sb", [1, 128], F32)
    bqk_sb = sbt("bqk_sb", [128, 2 * NH], F32)
    bo_sb = sbt("bo_sb", [128, HC], F32)
    bv_sb = sbt("bv_sb", [1, H], BF16)
    wr_sb = sbt("wr_sb", [128, HC * 8], BF16)
    wd_sb = sbt("wd_sb", [128, HC * EI], BF16)
    wu_sb = sbt("wu_sb", [128, H], BF16)

    NTB = T // 128
    def _al(x):
        return (x + 63) // 64 * 64
    A_XT = 0
    A_wv = _al(A_XT + HC * T)
    A_wqk = _al(A_wv + HC * VD)
    A_kst = _al(A_wqk + 2 * HC * 128)
    A_vst = _al(A_kst + 2 * 512)
    p1_end = _al(A_vst + 2 * 512)
    GQ = min(2, T // 128)
    QW = GQ * 128
    NGQ = T // QW
    SH = S // 2
    NQS_ = T // 128
    A_q2 = 0
    A_k2 = _al(A_q2 + 2 * T)
    A_v2 = _al(A_k2 + 2 * S)
    A_PT = _al(A_v2 + 2 * S)
    NSL = 4
    A_P2s = _al(A_PT + KC * QW)
    A_junk = _al(A_P2s + NSL * S)
    A_accQ = _al(A_junk + 1280)
    A_mnegs = _al(A_accQ + 2 * 4 * NH * NQS_)
    A_lrep = _al(A_mnegs + 2 * NH * NQS_)
    A_lac = _al(A_lrep + 2 * QW)
    A_lrw = _al(A_lac + 32 + 8)
    A_pvs = _al(A_lrw + 2 * QW)
    A_AO = _al(A_pvs + 2 * 2 * QW)
    A_r1 = _al(A_AO + 2 * HC * QW)
    A_xres = _al(A_r1 + HC * QW)
    A_yst = _al(A_xres + 2 * 2 * QW)
    A_wo = _al(A_yst + 3 * 2 * QW)
    A_ag = _al(A_wo + 2 * HC * 128)
    A_sq2 = _al(A_ag + 2 * QW)
    A_stf = _al(A_sq2 + 2 * QW)
    A_mgc = _al(A_stf + 6 * 2 * QW)
    A_ssm = _al(A_mgc + 2 * QW)
    p2_end = _al(A_ssm + 4 * QW)
    AR_COLS = _al(max(p2_end, p1_end) + 64)
    arena = sbt("arena", [128, AR_COLS], BF16)
    assert AR_COLS * 2 <= 190 * 1024, f"arena {AR_COLS * 2} B/partition too big"

    XT = arena[:, A_XT:A_XT + HC * T]
    wv_sb = arena[:, A_wv:A_wv + HC * VD]
    wqk0 = A_wqk
    kst = arena[:, A_kst:A_kst + 2 * 512]
    vst = arena[:, A_vst:A_vst + 2 * 512]
    q2o, k2o, v2o = A_q2, A_k2, A_v2
    PTsl = arena[:, A_PT:A_PT + KC * QW]
    P2s = arena[:, A_P2s:A_P2s + NSL * S]
    junk = arena[:, A_junk:A_junk + 1280]
    accQ = arena[:, A_accQ:A_accQ + 2 * 4 * NH * NQS_].bitcast(F32)
    accQI = arena[:, A_accQ:A_accQ + 2 * 4 * NH * NQS_].bitcast(mybir.dt.int32)
    mnegs = arena[:, A_mnegs:A_mnegs + 2 * NH * NQS_].bitcast(F32)
    lrep_sb = arena[:, A_lrep:A_lrep + 2 * QW].bitcast(F32)
    lacc = arena[:, A_lac:A_lac + 32].bitcast(F32)
    lcol2 = arena[:, A_lac + 32:A_lac + 36].bitcast(F32)
    lcol2i = arena[:, A_lac + 36:A_lac + 40].bitcast(F32)
    lirow = arena[0:1, A_lrw:A_lrw + 2 * QW].bitcast(F32)
    pvst = arena[:, A_pvs:A_pvs + 2 * 2 * QW].bitcast(F32)
    AO = arena[:, A_AO:A_AO + 2 * HC * QW]
    r1sl = arena[:, A_r1:A_r1 + HC * QW]
    x_res = arena[:, A_xres:A_xres + 2 * 2 * QW].bitcast(F32)
    ystage = arena[:, A_yst:A_yst + 3 * 2 * QW].bitcast(F32)
    wo0 = A_wo
    agst = arena[:, A_ag:A_ag + 2 * QW]
    sqst = arena[:, A_sq2:A_sq2 + 2 * QW]
    stf = arena[:, A_stf:A_stf + 6 * 2 * QW].bitcast(F32)
    stfI = arena[:, A_stf:A_stf + 6 * 2 * QW].bitcast(mybir.dt.int32)
    mgcI = arena[:, A_mgc:A_mgc + 2 * QW].bitcast(mybir.dt.int32)
    mgcF = arena[:, A_mgc:A_mgc + 2 * QW].bitcast(F32)
    ssm = arena[:, A_ssm:A_ssm + 4 * QW]
    lgf = arena[0:8, A_junk:A_junk + 2 * QW].bitcast(F32)
    gT_sb = arena[0:8, A_junk + 2 * QW:A_junk + 3 * QW]
    g0 = A_junk + 3 * QW
    gtok = arena[:, g0:g0 + 2 * GQ * 8].bitcast(F32)
    gtk2 = arena[:, g0 + 2 * GQ * 8:g0 + 4 * GQ * 8].bitcast(F32)
    gtk3 = arena[:, g0 + 4 * GQ * 8:g0 + 6 * GQ * 8].bitcast(F32)
    gsc = arena[:, g0 + 6 * GQ * 8:g0 + 6 * GQ * 8 + 2 * GQ].bitcast(F32)
    gsc2 = arena[:, g0 + 6 * GQ * 8 + 2 * GQ:g0 + 6 * GQ * 8 + 4 * GQ].bitcast(F32)
    assert 3 * QW + 6 * GQ * 8 + 4 * GQ <= 1280

    psA = es.enter_context(nc.psum_tensor("psA", [128, 2048], F32))
    psB = es.enter_context(nc.psum_tensor("psB", [128, 2048], F32))

    sems = {}
    for s in ALL_SEMS:
        sems[s] = es.enter_context(nc.semaphore("sem_" + s))
    block = es.enter_context(nc.Block())

    b_cst, b_cstf, b_sel, b_ones = Buf("cst"), Buf("cstf"), Buf("sel"), Buf("ones")
    b_onesf = Buf("onesf")
    b_bqk, b_bo, b_bv = Buf("bqk"), Buf("bo"), Buf("bv")
    b_wr, b_wd, b_wu = Buf("wr"), Buf("wd"), Buf("wu")
    b_XT = Buf("XT")
    b_wv = Buf("wv")
    b_wqk = [Buf("wqk%d" % i) for i in range(2)]
    b_kst = [Buf("kst%d" % i) for i in range(2)]
    b_vst = [Buf("vst%d" % i) for i in range(2)]
    b_psA = [Buf("psA%d" % i) for i in range(4)]
    b_psB = [Buf("psB%d" % i) for i in range(4)]
    b_pA2 = [b_psA[2], b_psA[2]]
    b_pA3 = [b_psA[3], b_psA[3]]
    b_pB2 = [b_psB[2], b_psB[2]]
    b_pB3 = [b_psB[3], b_psB[3]]
    psA_all = b_psA
    psB_all = b_psB

    def pAb(slot):
        return [b_psA[slot]]

    def pBb(slot):
        return [b_psB[slot]]
    e_kscr = [[] for _ in range(NH)]
    e_qscr = [[] for _ in range(NH)]
    e_vscr = [[] for _ in range(NH)]
    b_k2 = [Buf("k2_%d" % i) for i in range(2)]
    b_q2 = [Buf("q2_%d" % i) for i in range(2)]
    b_q2s = [Buf("q2s%d" % i) for i in range(2)]
    b_v2s = [Buf("v2s%d" % i) for i in range(2)]
    b_P2s = [[Buf("P2s%d_%d" % (i, hh)) for hh in range(2)] for i in range(NSL)]
    b_lacc = [Buf("lacc%d" % i) for i in range(2)]
    b_lc2, b_lc2i, b_lirow = Buf("lc2"), Buf("lc2i"), Buf("lirow")
    b_pvs = [Buf("pvs%d" % i) for i in range(2)]
    b_accQ = Buf("accQ")
    b_mnegs = Buf("mnegs")
    b_PT = Buf("PT")
    b_junk = Buf("junk")
    b_lrep = Buf("lrep")
    b_AO2 = [[Buf("AO%d_%d" % (par, n)) for n in range(HC)] for par in range(2)]
    b_wo = [Buf("wo%d" % i) for i in range(2)]
    b_x2 = [Buf("x2_%d" % i) for i in range(2)]
    b_r1n = [Buf("r1n%d" % o) for o in range(HC)]
    b_sq2 = [Buf("sq2_%d" % i) for i in range(2)]
    b_yst1 = Buf("yst1")
    b_yst2 = [Buf("yst2_%d" % i) for i in range(2)]
    b_ag2 = [Buf("ag2_%d" % i) for i in range(2)]
    b_stf = [Buf("stf%d" % i) for i in range(6)]
    b_ssm = [Buf("ssm%d" % i) for i in range(4)]
    b_lgf2, b_gT2 = Buf("lgf2"), Buf("gT2")
    b_gtok2, b_gtk22, b_gtk32 = Buf("gtok2"), Buf("gtk22"), Buf("gtk32")
    b_gsc_, b_gsc2_ = Buf("gsc_"), Buf("gsc2_")
    b_mgc = Buf("mgc")

    def aosl(par, c):
        return AO[:, par * HC * QW + c * QW:par * HC * QW + (c + 1) * QW]

    g = "gpsimd"
    K.op(g, lambda h: h.dma_start(cst[:, :], cst_h[:, :]), writes=[b_cst], dma=True)
    K.op(g, lambda h: h.dma_start(cstf[:, :], cstf_h[:, :]), writes=[b_cstf], dma=True)
    K.op(g, lambda h: h.dma_start(sel_sb[:, :], cst_h[0:8, 128:256]), writes=[b_sel], dma=True)
    K.op(g, lambda h: h.dma_start(bqk_sb[:, :], bqk_h[:, :]), writes=[b_bqk], dma=True)
    K.op(g, lambda h: h.dma_start(bo_sb[:, :], bo_h[:, :]), writes=[b_bo], dma=True)
    K.op(g, lambda h: h.dma_start(bv_sb[:, :], bv_h[:, :]), writes=[b_bv], dma=True)
    K.op(g, lambda h: h.dma_start(wr_sb[:, :], wr_h[:, :]), writes=[b_wr], dma=True)
    K.op(g, lambda h: h.dma_start(wd_sb[:, :], wd_h[:, :]), writes=[b_wd], dma=True)
    K.op(g, lambda h: h.dma_start(wu_sb[:, :], wu_h[:, :]), writes=[b_wu], dma=True)
    for c in range(HC):
        K.op(g, (lambda c: lambda h: h.dma_start(
            XT[:, c * T:(c + 1) * T], xT[c * 128:(c + 1) * 128, :]))(c),
            writes=[b_XT], dma=True)
    K.op("vector", lambda h: h.memset(ones_sb[:, :], 1.0), writes=[b_ones])
    MAGIC_F = float(np.frombuffer(np.uint32(0x5F3759DF).tobytes(), np.float32)[0])
    K.op("vector", lambda h: h.memset(mgcF[:, :], MAGIC_F), writes=[b_mgc])
    K.op("vector", lambda h: h.memset(onesf_sb[:, :], 1.0), writes=[b_onesf])

    ident = cst[:, 0:128]
    ident8f = cstf[0:8, 0:8]
    identf = cstf[:, :]

    for dt in range(NVD):
        K.op(g, (lambda dt: lambda h: h.dma_start(
            wv_sb[:, :].rearrange("p (c d) -> p c d", d=VD),
            wv_h[dt, :, :].rearrange("(c p) d -> p c d", p=128)))(dt),
            writes=[b_wv], dma=True)
        for tb in range(TBo):
            idx1 = dt * TBo + tb
            slot = idx1 % 4
            if idx1 < HC:
                wcs = idx1 % 2
                K.op(g, (lambda o, wcs: lambda h: h.dma_start(
                    arena[:, wqk0 + wcs * HC * 128:wqk0 + (wcs + 1) * HC * 128]
                    .rearrange("p (c q) -> p c q", q=128),
                    wo_h[o, :, :].rearrange("(c p) q -> p c q", p=128)))(idx1, wcs),
                    writes=[b_wqk[wcs]], dma=True)
                K.op("sync", (lambda o, wcs: lambda h: h.dma_start(
                    wo_scr[o, :, :],
                    arena[:, wqk0 + wcs * HC * 128:wqk0 + (wcs + 1) * HC * 128]))(idx1, wcs),
                    reads=[b_wqk[wcs]], dma=True)
            def mmg(h, dt=dt, tb=tb, slot=slot):
                for c in range(HC):
                    h.matmul(psA[:, slot * 512:slot * 512 + VD],
                             XT[:, c * T + tb * 128:c * T + (tb + 1) * 128],
                             wv_sb[:, c * VD:(c + 1) * VD],
                             start=(c == 0), stop=False)
                return h.matmul(psA[:, slot * 512:slot * 512 + VD],
                                ones_sb[0:1, 0:128], bv_sb[:, dt * VD:(dt + 1) * VD],
                                start=False, stop=True)
            K.op("tensor", mmg, reads=[b_XT, b_wv, b_ones, b_bv], writes=pAb(slot))
            vs = idx1 % 2
            K.op("vector", (lambda slot, vs: lambda h: h.tensor_copy(
                vst[:, vs * 512:vs * 512 + VD], psA[:, slot * 512:slot * 512 + VD]))(slot, vs),
                reads=pAb(slot), writes=[b_vst[vs]])
            nh_t = VD // 128
            def spill(h, dt=dt, tb=tb, vs=vs, nh_t=nh_t):
                return h.dma_start(vag_i[tb, dt * nh_t:(dt + 1) * nh_t, :, :].transpose([1, 0, 2]),
                                   vst[:, vs * 512:vs * 512 + VD].rearrange("p (j d) -> p j d", d=128))
            ev = K.op("sync", spill, reads=[b_vst[vs]], dma=True)
            e_vscr[0].append(ev)

    def qk_proj(which):
        span = T
        tile_w = min(512, span)
        ntile = span // tile_w
        wsrc = wk_h if which == "k" else wq_h
        scr = kag_i if which == "k" else qscr
        escr = e_kscr if which == "k" else e_qscr
        bias0 = NH if which == "k" else 0
        for n in range(NH):
            slot = n % 2
            K.op(g, (lambda n, slot, wsrc=wsrc: lambda h: h.dma_start(
                arena[:, wqk0 + slot * HC * 128: wqk0 + (slot + 1) * HC * 128]
                .rearrange("p (c o) -> p c o", o=128),
                wsrc[n, :, :].rearrange("(c p) o -> p c o", p=128)))(n, slot),
                writes=[b_wqk[slot]], dma=True)
            for j in range(ntile):
                idx = n * ntile + j
                ps = idx % 4
                def mmg(h, n=n, j=j, slot=slot, ps=ps, tile_w=tile_w):
                    for c in range(HC):
                        mm = h.matmul(psB[:, ps * 512:ps * 512 + tile_w],
                                      arena[:, wqk0 + slot * HC * 128 + c * 128:
                                            wqk0 + slot * HC * 128 + (c + 1) * 128],
                                      XT[:, c * T + j * tile_w:c * T + (j + 1) * tile_w],
                                      start=(c == 0), stop=(c == HC - 1))
                    return mm
                K.op("tensor", mmg, reads=[b_XT, b_wqk[slot]], writes=pBb(ps))
                ks = idx % 2
                K.op("scalar", (lambda ps, ks, n, tile_w: lambda h: h.activation(
                    kst[:, ks * 512:ks * 512 + tile_w], psB[:, ps * 512:ps * 512 + tile_w],
                    AF.Identity, bias=bqk_sb[:, bias0 + n:bias0 + n + 1], scale=1.0))(ps, ks, n, tile_w),
                    reads=pBb(ps) + [b_bqk], writes=[b_kst[ks]])
                ev = K.op("sync", (lambda n, j, ks, tile_w, scr=scr: lambda h: h.dma_start(
                    scr[n, :, j * tile_w:(j + 1) * tile_w],
                    kst[:, ks * 512:ks * 512 + tile_w]))(n, j, ks, tile_w),
                    reads=[b_kst[ks]], dma=True)
                escr[n].append(ev)
    qk_proj("k")

    pair_groups = [[i, i + 1] for i in range(0, cfg.n_cores, 2)]
    e_kag = K.op(g, (lambda: lambda h: h.collective_compute(
        "AllGather", AluOpType.bypass, replica_groups=pair_groups,
        ins=[kag_i.ap().opt()], outs=[kag_o.ap().opt()]))(),
        waits=[e for n in range(NH) for e in e_kscr[n]])
    e_vag = K.op(g, (lambda: lambda h: h.collective_compute(
        "AllGather", AluOpType.bypass, replica_groups=pair_groups,
        ins=[vag_i.ap().opt()], outs=[vag_o.ap().opt()]))(),
        waits=e_vscr[0])
    qk_proj("q")

    K.barrier()
    nSA = (SH + 511) // 512
    NQS = T // 128
    import functools

    nstr = max(1, KC // 4)
    NV = NH * NQS
    SSW = (nstr + 2) * 128
    for n in range(NH):
        sl = n % 2
        K.op("sync", (lambda n, sl: lambda h: h.dma_start(
            arena[:, q2o + sl * T:q2o + (sl + 1) * T], qscr[n, :, :]))(n, sl),
            waits=e_qscr[n], writes=[b_q2[sl]], dma=True)
        for half in range(2):
            K.op("sync", (lambda n, sl, half: lambda h: h.dma_start(
                arena[:, k2o + sl * S + half * T:k2o + sl * S + (half + 1) * T],
                kag_o[half, n, :, :]))(n, sl, half),
                waits=[e_kag], writes=[b_k2[sl]], dma=True)
        for qi in range(NQS):
            pp = qi % 2
            psS = psA if pp == 0 else psB
            bps = ([b_psA[0], b_psA[1]] + b_pA2) if pp == 0 else ([b_psB[0], b_psB[1]] + b_pB2)
            hn = max(1, nstr // 2)
            def smp(h, n=n, qi=qi, sl=sl, psS=psS, hn=hn):
                k2v = arena[:, k2o + sl * S:k2o + (sl + 1) * S]
                qv = arena[:, q2o + sl * T + qi * 128:q2o + sl * T + (qi + 1) * 128]
                kstr = k2v.rearrange("p (a b) -> p a b", b=512)[:, :, 0:128]
                h.matmul(psS[:, 0:hn * 128], qv, kstr[:, 0:hn, :], start=True, stop=True)
                if nstr > hn:
                    h.matmul(psS[:, 512:512 + (nstr - hn) * 128], qv, kstr[:, hn:nstr, :],
                             start=True, stop=True)
                return h.matmul(psS[:, 1024:1024 + 256],
                                qv, k2v.rearrange("p (h b) -> p h b", h=2)
                                [:, :, qi * 128:(qi + 1) * 128],
                                start=True, stop=True)
            K.op("tensor", smp, reads=[b_k2[sl], b_q2[sl]], writes=bps)
            c0 = n * NQS + qi
            if nstr == 8:
                K.op("scalar", (lambda c0, psS: lambda h: h.activation(
                    junk[:, 0:1280], psS[:, 0:1280], AF.Exp, scale=SC / 4.0,
                    accum_out=accQ[:, c0:c0 + 1]))(c0, psS),
                    reads=bps, writes=[b_junk, b_accQ])
            else:
                K.op("scalar", (lambda c0, psS, hn: lambda h: h.activation(
                    junk[:, 0:hn * 128], psS[:, 0:hn * 128], AF.Exp, scale=SC / 4.0,
                    accum_out=accQ[:, NV + c0:NV + c0 + 1]))(c0, psS, hn),
                    reads=bps, writes=[b_junk, b_accQ])
                K.op("scalar", (lambda c0, psS: lambda h: h.activation(
                    junk[:, 0:256], psS[:, 1024:1280], AF.Exp, scale=SC / 4.0,
                    accum_out=accQ[:, 2 * NV + c0:2 * NV + c0 + 1]))(c0, psS),
                    reads=bps, writes=[b_junk, b_accQ])
                K.op("vector", (lambda c0: lambda h: h.tensor_tensor(
                    accQ[:, c0:c0 + 1], accQ[:, NV + c0:NV + c0 + 1],
                    accQ[:, 2 * NV + c0:2 * NV + c0 + 1], op=AluOpType.add))(c0),
                    reads=[b_accQ], writes=[b_accQ])
        K.op("vector", (lambda n: lambda h: h.tensor_scalar(
            accQI[:, NH * NQS + n * NQS:NH * NQS + (n + 1) * NQS],
            accQI[:, n * NQS:(n + 1) * NQS], 23, None,
            op0=AluOpType.arith_shift_right))(n),
            reads=[b_accQ], writes=[b_accQ])
        K.op("vector", (lambda n: lambda h: h.tensor_copy(
            accQ[:, 2 * NH * NQS + n * NQS:2 * NH * NQS + (n + 1) * NQS],
            accQI[:, NH * NQS + n * NQS:NH * NQS + (n + 1) * NQS]))(n),
            reads=[b_accQ], writes=[b_accQ])
        K.op("vector", (lambda n: lambda h: h.tensor_scalar(
            mnegs[:, n * NQS:(n + 1) * NQS],
            accQ[:, 2 * NH * NQS + n * NQS:2 * NH * NQS + (n + 1) * NQS],
            -127.0, -4.0 * 0.6931471805599453,
            op0=AluOpType.add, op1=AluOpType.mult))(n),
            reads=[b_accQ], writes=[b_mnegs])

    K.barrier()
    pend = []

    def flush_pending():
        for f in pend:
            f()
        pend.clear()

    q34 = []
    last_out = {}

    def p3p4_tile(qg):
        par = qg % 2
        t0 = qg * QW
        ops = []
        add = lambda f: ops.append([f])
        addb = lambda fs: ops.append(list(fs))

        def accum_stats(o):
            cur = o % 2
            add((lambda o, cur: lambda: K.op("vector", lambda h: h.tensor_tensor(
                sqst[:, cur * QW:(cur + 1) * QW], r1sl[:, o * QW:(o + 1) * QW],
                r1sl[:, o * QW:(o + 1) * QW], op=AluOpType.mult),
                reads=[b_r1n[o]], writes=[b_sq2[cur]]))(o, cur))
            if o == 0:
                add(lambda: K.op("vector", lambda h: h.tensor_copy(
                    ssm[:, 0:QW], r1sl[:, 0:QW]), reads=[b_r1n[0]], writes=[b_ssm[0]]))
                add(lambda: K.op("vector", lambda h: h.tensor_copy(
                    ssm[:, 2 * QW:3 * QW], sqst[:, 0:QW]), reads=[b_sq2[0]], writes=[b_ssm[2]]))
            else:
                prv = 1 - cur
                add((lambda o, cur, prv: lambda: K.op("vector", lambda h: h.tensor_tensor(
                    ssm[:, cur * QW:(cur + 1) * QW], ssm[:, prv * QW:(prv + 1) * QW],
                    r1sl[:, o * QW:(o + 1) * QW], op=AluOpType.add),
                    reads=[b_ssm[prv], b_r1n[o]], writes=[b_ssm[cur]]))(o, cur, prv))
                add((lambda cur, prv: lambda: K.op("vector", lambda h: h.tensor_tensor(
                    ssm[:, (2 + cur) * QW:(3 + cur) * QW],
                    ssm[:, (2 + prv) * QW:(3 + prv) * QW],
                    sqst[:, cur * QW:(cur + 1) * QW], op=AluOpType.add),
                    reads=[b_ssm[2 + prv], b_sq2[cur]], writes=[b_ssm[2 + cur]]))(cur, prv))

        def ln_half(to_ao):
            fin = (HC - 1) % 2
            mu = stf[:, 0:QW]
            rs = stf[:, QW:2 * QW]
            mus = stf[:, 2 * QW:3 * QW]
            scr2 = stf[:, 3 * QW:4 * QW]
            var2 = stf[:, 4 * QW:5 * QW]
            addb([
                lambda: K.op("tensor", lambda h: h.matmul(
                    psB[:, 1280:1280 + QW], ones_sb[:, 0:128], ssm[:, fin * QW:(fin + 1) * QW],
                    start=True, stop=True), reads=[b_ssm[fin], b_ones], writes=[b_pB2[1]]),
                lambda: K.op("tensor", lambda h: h.matmul(
                    psB[:, 1792:1792 + QW], ones_sb[:, 0:128],
                    ssm[:, (2 + fin) * QW:(3 + fin) * QW],
                    start=True, stop=True), reads=[b_ssm[2 + fin], b_ones], writes=[b_pB3[1]]),
                lambda: K.op("vector", lambda h: h.tensor_scalar(
                    mu, psB[:, 1280:1280 + QW], IH, None, op0=AluOpType.mult),
                    reads=[b_pB2[1]], writes=[b_stf[0]]),
                lambda: K.op("vector", lambda h: h.tensor_scalar(
                    scr2, psB[:, 1792:1792 + QW], IH, cfg.EPS,
                    op0=AluOpType.mult, op1=AluOpType.add),
                    reads=[b_pB3[1]], writes=[b_stf[3]]),
            ])
            add(lambda: K.op("vector", lambda h: h.tensor_tensor(
                rs, mu, mu, op=AluOpType.mult), reads=[b_stf[0]], writes=[b_stf[1]]))
            add(lambda: K.op("vector", lambda h: h.tensor_tensor(
                var2, scr2, rs, op=AluOpType.subtract),
                reads=[b_stf[1], b_stf[3]], writes=[b_stf[4]]))
            add(lambda: K.op("vector", lambda h: h.tensor_scalar(
                stfI[:, 3 * QW:4 * QW], stfI[:, 4 * QW:5 * QW], 1, None,
                op0=AluOpType.arith_shift_right),
                reads=[b_stf[4]], writes=[b_stf[3]]))
            add(lambda: K.op("vector", lambda h: h.tensor_tensor(
                stfI[:, QW:2 * QW], mgcI[:, 0:QW], stfI[:, 3 * QW:4 * QW],
                op=AluOpType.subtract), reads=[b_mgc, b_stf[3]], writes=[b_stf[1]]))
            for it in range(2):
                sa, sb = (1, 2) if it % 2 == 0 else (2, 1)
                add((lambda sa: lambda: K.op("vector", lambda h: h.tensor_tensor(
                    stf[:, 3 * QW:4 * QW], stf[:, sa * QW:(sa + 1) * QW],
                    stf[:, sa * QW:(sa + 1) * QW], op=AluOpType.mult),
                    reads=[b_stf[sa]], writes=[b_stf[3]]))(sa))
                add(lambda: K.op("vector", lambda h: h.tensor_tensor(
                    stf[:, 5 * QW:6 * QW], stf[:, 3 * QW:4 * QW], var2,
                    op=AluOpType.mult), reads=[b_stf[3], b_stf[4]], writes=[b_stf[5]]))
                add(lambda: K.op("vector", lambda h: h.tensor_scalar(
                    stf[:, 3 * QW:4 * QW], stf[:, 5 * QW:6 * QW], -0.5, 1.5,
                    op0=AluOpType.mult, op1=AluOpType.add),
                    reads=[b_stf[5]], writes=[b_stf[3]]))
                add((lambda sa, sb: lambda: K.op("vector", lambda h: h.tensor_tensor(
                    stf[:, sb * QW:(sb + 1) * QW], stf[:, sa * QW:(sa + 1) * QW],
                    stf[:, 3 * QW:4 * QW], op=AluOpType.mult),
                    reads=[b_stf[sa], b_stf[3]], writes=[b_stf[sb]]))(sa, sb))
            add(lambda: K.op("vector", lambda h: h.tensor_tensor(
                mus, mu, rs, op=AluOpType.mult), reads=[b_stf[0], b_stf[1]], writes=[b_stf[2]]))
            for o in range(HC):
                ys = o % 2
                add((lambda o: lambda: K.op("vector", lambda h: h.tensor_tensor(
                    ystage[:, 0:QW], r1sl[:, o * QW:(o + 1) * QW], rs,
                    op=AluOpType.mult), reads=[b_r1n[o], b_stf[1]], writes=[b_yst1]))(o))
                add((lambda o, ys: lambda: K.op("vector", lambda h: h.tensor_tensor(
                    ystage[:, (1 + ys) * QW:(2 + ys) * QW], ystage[:, 0:QW], mus,
                    op=AluOpType.subtract),
                    reads=[b_yst1, b_stf[2]], writes=[b_yst2[ys]]))(o, ys))
                if to_ao:
                    add((lambda o, ys: lambda: K.op("scalar", lambda h: h.activation(
                        aosl(par, o), ystage[:, (1 + ys) * QW:(2 + ys) * QW], AF.Copy),
                        reads=[b_yst2[ys]], writes=[b_AO2[par][o]]))(o, ys))
                else:
                    def mk_out(o=o, ys=ys):
                        def go():
                            ev = K.op("scalar", lambda h: h.dma_start(
                                outT[o * 128:(o + 1) * 128, t0:t0 + QW],
                                ystage[:, (1 + ys) * QW:(2 + ys) * QW]),
                                reads=[b_yst2[ys]], dma=True)
                            last_out[ev.sem] = ev
                        return go
                    add(mk_out())

        if debug:
            add(lambda: K.op(g, lambda h: h.dma_start(
                d_ao[qg, :, :], AO[:, par * HC * QW:(par + 1) * HC * QW]),
                reads=list(b_AO2[par]), dma=True))
        for o in range(HC):
            ws = o % 2
            po = 1280 if o % 2 == 0 else 1792
            pb = b_pA2[1] if o % 2 == 0 else b_pA3[1]
            add((lambda o, ws: lambda: K.op(g, lambda h: h.dma_start(
                arena[:, wo0 + ws * HC * 128:wo0 + (ws + 1) * HC * 128], wo_scr[o, :, :]),
                writes=[b_wo[ws]], dma=True))(o, ws))
            add((lambda o: lambda: K.op(g, lambda h: h.dma_start(
                x_res[:, (o % 2) * QW:(o % 2) * QW + QW],
                xT[o * 128:(o + 1) * 128, t0:t0 + QW]),
                writes=[b_x2[o % 2]], dma=True))(o))
            def mk_proj(o=o, ws=ws, po=po, pb=pb):
                def mms(h):
                    for c in range(HC):
                        mm = h.matmul(psA[:, po:po + QW],
                                      arena[:, wo0 + ws * HC * 128 + c * 128:
                                            wo0 + ws * HC * 128 + (c + 1) * 128],
                                      aosl(par, c), start=(c == 0), stop=(c == HC - 1))
                    return mm
                return lambda: K.op("tensor", mms, reads=[b_wo[ws]] + b_AO2[par], writes=[pb])
            bnd = [mk_proj()]
            if debug and o < 2:
                bnd.append((lambda o, po, pb: lambda: K.op("vector", lambda h: h.tensor_copy(
                    ystage[:, 0:QW], psA[:, po:po + QW]),
                    reads=[pb], writes=[b_yst1]))(o, po, pb))
                bnd.append((lambda o: lambda: K.op(g, lambda h: h.dma_start(
                    d_pj[qg, o, :, :], ystage[:, 0:QW]), reads=[b_yst1], dma=True))(o))
                bnd.append((lambda o: lambda: K.op(g, lambda h: h.dma_start(
                    d_xr[qg, o, :, :], x_res[:, (o % 2) * QW:(o % 2) * QW + QW]),
                    reads=[b_x2[o % 2]], dma=True))(o))
            bnd.append((lambda o, po, pb: lambda: K.op("vector", lambda h: h.scalar_tensor_tensor(
                r1sl[:, o * QW:(o + 1) * QW], psA[:, po:po + QW], bo_sb[:, o:o + 1],
                x_res[:, (o % 2) * QW:(o % 2) * QW + QW],
                op0=AluOpType.add, op1=AluOpType.add),
                reads=[pb, b_bo, b_x2[o % 2]], writes=[b_r1n[o]]))(o, po, pb))
            addb(bnd)
            accum_stats(o)
        if debug:
            add(lambda: K.op(g, lambda h: h.dma_start(
                d_r1[qg, :, :], r1sl[:, 0:HC * QW]), reads=list(b_r1n), dma=True))
        ln_half(True)
        if debug:
            add(lambda: K.op(g, lambda h: h.dma_start(
                d_y[qg, :, :], AO[:, par * HC * QW:(par + 1) * HC * QW]),
                reads=list(b_AO2[par]), dma=True))
            add(lambda: K.op(g, lambda h: h.dma_start(
                d_st[qg, 0, :, :], stf[:, 0:6 * QW]), reads=list(b_stf), dma=True))

        def mk_router():
            def mms(h):
                for c in range(HC):
                    mm = h.matmul(psA[0:8, 1280:1280 + QW], wr_sb[:, c * 8:(c + 1) * 8],
                                  aosl(par, c), start=(c == 0), stop=(c == HC - 1))
                return mm
            return lambda: K.op("tensor", mms, reads=[b_wr] + b_AO2[par], writes=[b_pA2[1]])
        addb([mk_router(),
              lambda: K.op("vector", lambda h: h.tensor_copy(
                  lgf[:, 0:QW], psA[0:8, 1280:1280 + QW]), reads=[b_pA2[1]], writes=[b_lgf2])])
        addb([(lambda tb: lambda: K.op("tensor", lambda h: h.transpose(
                  psA[:, 1792 + tb * 8:1792 + tb * 8 + 8],
                  lgf[0:8, tb * 128:(tb + 1) * 128], ident8f),
                  reads=[b_lgf2, b_cstf], writes=[b_pA3[1]]))(tb) for tb in range(GQ)] +
              [lambda: K.op("vector", lambda h: h.tensor_copy(
                  gtok[:, 0:GQ * 8], psA[:, 1792:1792 + GQ * 8]),
                  reads=[b_pA3[1]], writes=[b_gtok2])])
        seg = gtok[:, :].rearrange("p (s e) -> p s e", e=8)
        seg2 = gtk2[:, :].rearrange("p (s e) -> p s e", e=8)
        seg3 = gtk3[:, :].rearrange("p (s e) -> p s e", e=8)
        gb = lambda: gsc[:, :, None].broadcast_to((128, GQ, 8))
        gb2 = lambda: gsc2[:, :, None].broadcast_to((128, GQ, 8))
        add(lambda: K.op("vector", lambda h: h.tensor_reduce(
            gsc[:, :, None], seg, axis=mybir.AxisListType.X, op=AluOpType.max),
            reads=[b_gtok2], writes=[b_gsc_]))
        add(lambda: K.op("vector", lambda h: h.tensor_tensor(
            seg2, seg, gb(), op=AluOpType.subtract),
            reads=[b_gtok2, b_gsc_], writes=[b_gtk22]))
        add(lambda: K.op("scalar", lambda h: h.activation(gtok[:, :], gtk2[:, :], AF.Exp),
            reads=[b_gtk22], writes=[b_gtok2]))
        add(lambda: K.op("vector", lambda h: h.tensor_reduce(
            gsc[:, :, None], seg, axis=mybir.AxisListType.X, op=AluOpType.add),
            reads=[b_gtok2], writes=[b_gsc_]))
        add(lambda: K.op("vector", lambda h: h.reciprocal(gsc2[:, :], gsc[:, :]),
            reads=[b_gsc_], writes=[b_gsc2_]))
        add(lambda: K.op("vector", lambda h: h.tensor_tensor(
            seg2, seg, gb2(), op=AluOpType.mult),
            reads=[b_gsc2_, b_gtok2], writes=[b_gtk22]))
        add(lambda: K.op("vector", lambda h: h.tensor_reduce(
            gsc[:, :, None], seg2, axis=mybir.AxisListType.X, op=AluOpType.max),
            reads=[b_gtk22], writes=[b_gsc_]))
        add(lambda: K.op("vector", lambda h: h.tensor_tensor(
            seg, seg2, gb(), op=AluOpType.is_ge),
            reads=[b_gtk22, b_gsc_], writes=[b_gtok2]))
        add(lambda: K.op("vector", lambda h: h.scalar_tensor_tensor(
            gtk3[:, :], gtok[:, :], -2.0, gtk2[:, :],
            op0=AluOpType.mult, op1=AluOpType.add),
            reads=[b_gtk22, b_gtok2], writes=[b_gtk32]))
        add(lambda: K.op("vector", lambda h: h.tensor_reduce(
            gsc[:, :, None], seg3, axis=mybir.AxisListType.X, op=AluOpType.max),
            reads=[b_gtk32], writes=[b_gsc_]))
        add(lambda: K.op("vector", lambda h: h.tensor_tensor(
            seg, seg2, gb(), op=AluOpType.is_ge),
            reads=[b_gtk22, b_gsc_], writes=[b_gtok2]))
        add(lambda: K.op("vector", lambda h: h.tensor_tensor(
            seg3, seg, seg2, op=AluOpType.mult),
            reads=[b_gtk22, b_gtok2], writes=[b_gtk32]))
        addb([(lambda tb: lambda: K.op("tensor", lambda h: h.transpose(
                  psA[0:8, 1280 + tb * 128:1280 + (tb + 1) * 128],
                  gtk3[:, tb * 8:(tb + 1) * 8], identf),
                  reads=[b_gtk32, b_cstf], writes=[b_pA2[1]]))(tb) for tb in range(GQ)] +
              [lambda: K.op("vector", lambda h: h.tensor_copy(
                  gT_sb[:, 0:QW], psA[0:8, 1280:1280 + QW]),
                  reads=[b_pA2[1]], writes=[b_gT2])])
        def mk_down():
            def mms(h):
                for c in range(HC):
                    mm = h.matmul(psA[:, 1280:1280 + QW],
                                  wd_sb[:, c * EI:(c + 1) * EI][:, 0:128],
                                  aosl(par, c), start=(c == 0), stop=(c == HC - 1))
                return mm
            return lambda: K.op("tensor", mms, reads=[b_wd] + b_AO2[par], writes=[b_pA2[1]])
        addb([
            lambda: K.op("tensor", lambda h: h.matmul(
                psA[:, 1792:1792 + QW], sel_sb[:, :], gT_sb[0:8, 0:QW], start=True, stop=True),
                reads=[b_sel, b_gT2], writes=[b_pA3[1]]),
            mk_down(),
            lambda: K.op("scalar", lambda h: h.activation(
                ystage[:, 0:QW], psA[:, 1280:1280 + QW], AF.Exp, scale=-1.0),
                reads=[b_pA2[1]], writes=[b_yst1]),
            lambda: K.op("vector", lambda h: h.tensor_scalar(
                ystage[:, QW:2 * QW], ystage[:, 0:QW], 1.0, None, op0=AluOpType.add),
                reads=[b_yst1], writes=[b_yst2[0]]),
            lambda: K.op("vector", lambda h: h.reciprocal(
                ystage[:, 2 * QW:3 * QW], ystage[:, QW:2 * QW]),
                reads=[b_yst2[0]], writes=[b_yst2[1]]),
            lambda: K.op("vector", lambda h: h.tensor_tensor(
                agst[:, 0:QW], psA[:, 1280:1280 + QW], ystage[:, 2 * QW:3 * QW],
                op=AluOpType.mult), reads=[b_pA2[1], b_yst2[1]], writes=[b_ag2[0]]),
            lambda: K.op("vector", lambda h: h.tensor_tensor(
                agst[:, QW:2 * QW], agst[:, 0:QW], psA[:, 1792:1792 + QW],
                op=AluOpType.mult), reads=[b_ag2[0], b_pA3[1]], writes=[b_ag2[1]]),
        ])
        for o in range(HC):
            po = 1280 if o % 2 == 0 else 1792
            pb = b_pA2[1] if o % 2 == 0 else b_pA3[1]
            addb([(lambda o, po, pb: lambda: K.op("tensor", lambda h: h.matmul(
                      psA[:, po:po + QW], wu_sb[:, o * 128:(o + 1) * 128],
                      agst[:, QW:2 * QW], start=True, stop=True),
                      reads=[b_wu, b_ag2[1]], writes=[pb]))(o, po, pb),
                  (lambda o, po, pb: lambda: K.op("vector", lambda h: h.tensor_tensor(
                      r1sl[:, o * QW:(o + 1) * QW], psA[:, po:po + QW], aosl(par, o),
                      op=AluOpType.add), reads=[pb, b_AO2[par][o]],
                      writes=[b_r1n[o]]))(o, po, pb)])
            accum_stats(o)
        if debug:
            add(lambda: K.op("sync", lambda h: h.dma_start(
                d_r2[qg, :, :], r1sl[:, 0:HC * QW]), reads=list(b_r1n), dma=True))
        ln_half(False)
        if debug:
            add(lambda: K.op(g, lambda h: h.dma_start(
                d_st[qg, 1, :, :], stf[:, 0:6 * QW]), reads=list(b_stf), dma=True))
        return ops

    for qg in range(NGQ):
        for n in range(NH):
            sl = n % 2
            K.op(g, (lambda n, sl, qg: lambda h: h.dma_start(
                arena[:, q2o + sl * QW:q2o + (sl + 1) * QW],
                qscr[n, :, qg * QW:(qg + 1) * QW]))(n, sl, qg),
                waits=e_qscr[n], writes=[b_q2s[sl]], dma=True)
            for half in range(2):
                K.op(g, (lambda n, sl, half: lambda h: h.dma_start(
                    arena[:, k2o + sl * S + half * T:k2o + sl * S + (half + 1) * T],
                    kag_o[half, n, :, :]))(n, sl, half),
                    waits=[e_kag], writes=[b_k2[sl]], dma=True)
                K.op(g, (lambda n, sl, half: lambda h: h.dma_start(
                    arena[:, v2o + sl * S + half * T:v2o + sl * S + (half + 1) * T]
                    .rearrange("p (tb d) -> p tb d", d=128),
                    vag_o[half, :, n, :, :].transpose([1, 0, 2])))(n, sl, half),
                    waits=[e_vag], writes=[b_v2s[sl]], dma=True)
            for qs in range(GQ):
                qi = qg * GQ + qs
                p = qi % NSL
                def smm(h, qs=qs, sl=sl, half=0):
                    for m in range(nSA):
                        w = min(512, SH - m * 512)
                        ps = psA if half == 0 else psB
                        mm = h.matmul(ps[:, m * 512:m * 512 + w],
                                      arena[:, q2o + sl * QW + qs * 128:
                                            q2o + sl * QW + (qs + 1) * 128],
                                      arena[:, k2o + sl * S + half * SH + m * 512:
                                            k2o + sl * S + half * SH + m * 512 + w],
                                      start=True, stop=True)
                    return mm
                K.op("tensor", functools.partial(smm, half=0),
                     reads=[b_k2[sl], b_q2s[sl]], writes=psA_all)
                K.op("tensor", functools.partial(smm, half=1),
                     reads=[b_k2[sl], b_q2s[sl]], writes=psB_all)
                gp = n % 2
                K.op("scalar", (lambda n, qi, p, gp, qs: lambda h: h.activation(
                    P2s[:, p * S:p * S + SH], psA[:, 0:SH], AF.Exp, scale=SC,
                    bias=mnegs[:, n * NQS + qi:n * NQS + qi + 1],
                    accum_out=lacc[:, gp * 4 + 2 * qs:gp * 4 + 2 * qs + 1]))(n, qi, p, gp, qs),
                    reads=psA_all + [b_mnegs], writes=[b_P2s[p][0], b_lacc[gp]])
                K.op("scalar", (lambda n, qi, p, gp, qs: lambda h: h.activation(
                    P2s[:, p * S + SH:p * S + 2 * SH], psB[:, 0:SH], AF.Exp, scale=SC,
                    bias=mnegs[:, n * NQS + qi:n * NQS + qi + 1],
                    accum_out=lacc[:, gp * 4 + 2 * qs + 1:gp * 4 + 2 * qs + 2]))(n, qi, p, gp, qs),
                    reads=psB_all + [b_mnegs], writes=[b_P2s[p][1], b_lacc[gp]])
                flush_pending()
                K.op("sync", (lambda p, qs: lambda h: h.dma_start_transpose(
                    PTsl[:, :].rearrange("p (j q) -> p j q", q=QW)
                    [:, :, qs * 128:(qs + 1) * 128],
                    P2s[:, p * S:(p + 1) * S]))(p, qs),
                    reads=[b_P2s[p][0], b_P2s[p][1]], writes=[b_PT],
                    dma=True, dma_sem="xd")
                if qs == GQ - 1:
                    def def_G(n=n, qg=qg, sl=sl, gp=n % 2):
                        def pv(h):
                            for kt in range(KC):
                                mm = h.matmul(psA[:, 1024:1024 + QW],
                                              arena[:, v2o + sl * S + kt * 128:
                                                    v2o + sl * S + (kt + 1) * 128],
                                              PTsl[:, kt * QW:(kt + 1) * QW],
                                              start=(kt == 0), stop=(kt == KC - 1))
                            return mm
                        K.op("tensor", pv, reads=[b_v2s[sl], b_PT], writes=[b_pA2[0]])
                        K.op("vector", (lambda gp: lambda h: h.tensor_copy(
                            pvst[:, gp * QW:(gp + 1) * QW], psA[:, 1024:1024 + QW]))(gp),
                            reads=[b_pA2[0]], writes=[b_pvs[gp]])
                        K.op("vector", (lambda gp: lambda h: h.tensor_tensor(
                            lcol2[:, 0:GQ],
                            lacc[:, 0:8].rearrange("p (g s h) -> p g s h", g=2, h=2)[:, gp, 0:GQ, 0],
                            lacc[:, 0:8].rearrange("p (g s h) -> p g s h", g=2, h=2)[:, gp, 0:GQ, 1],
                            op=AluOpType.add))(gp),
                            reads=[b_lacc[gp]], writes=[b_lc2])
                        K.op("vector", lambda h: h.reciprocal(lcol2i[:, 0:GQ], lcol2[:, 0:GQ]),
                             reads=[b_lc2], writes=[b_lc2i])
                        def ltr(h):
                            for q_ in range(GQ):
                                tr = h.transpose(psA[0:1, 1536 + q_ * 128:1536 + (q_ + 1) * 128],
                                                 lcol2i[:, q_:q_ + 1], identf)
                            return tr
                        K.op("tensor", ltr, reads=[b_lc2i, b_cstf], writes=[b_pA3[0]])
                        K.op("vector", lambda h: h.tensor_copy(
                            lirow[0:1, 0:QW], psA[0:1, 1536:1536 + QW]),
                            reads=[b_pA3[0]], writes=[b_lirow])
                        def lrepmm(h):
                            for q_ in range(GQ):
                                mm = h.matmul(psA[:, 1536 + q_ * 128:1536 + (q_ + 1) * 128],
                                              onesf_sb[0:1, 0:128],
                                              lirow[0:1, q_ * 128:(q_ + 1) * 128],
                                              start=True, stop=True)
                            return mm
                        K.op("tensor", lrepmm, reads=[b_onesf, b_lirow], writes=[b_pA3[0]])
                        K.op("vector", lambda h: h.tensor_copy(
                            lrep_sb[:, 0:QW], psA[:, 1536:1536 + QW]),
                            reads=[b_pA3[0]], writes=[b_lrep])
                        K.op("vector", (lambda n, qg, gp: lambda h: h.tensor_tensor(
                            aosl(qg % 2, n), pvst[:, gp * QW:(gp + 1) * QW],
                            lrep_sb[:, 0:QW],
                            op=AluOpType.mult))(n, qg, gp),
                            reads=[b_pvs[gp], b_lrep], writes=[b_AO2[qg % 2][n]])
                    pend.append(def_G)
                if q34:
                    rem = NH * GQ - (n * GQ + qs)
                    for _ in range(min(len(q34), -(-len(q34) // rem))):
                        for f in q34.pop(0):
                            f()
        q34.extend(p3p4_tile(qg))
    flush_pending()
    for bundle in q34:
        for f in bundle:
            f()
    q34.clear()

    outs = list(last_out.values())
    K.op("sync", lambda h: None, waits=outs)
    K.mark(*outs)

    K.finalize(nc, block, sems)
    es.close()
    return nc


def prep_inputs(cfg, x_bh, w_qkv, b_qkv, w_out, b_out, router_w, down_w, up_w, own_first=True):
    H, NH, HC, S, VD, NVD, EI = cfg.H, cfg.NH, cfg.HC, cfg.S, cfg.VD, cfg.NVD, cfg.EI
    f32 = np.float32
    d = {}
    d["xT"] = np.ascontiguousarray(x_bh.T.astype(f32))
    Wq, Wk, Wv = w_qkv[0:H], w_qkv[H:2 * H], w_qkv[2 * H:3 * H]
    d["wq"] = np.ascontiguousarray(
        Wq.reshape(NH, 128, H).transpose(0, 2, 1))
    d["wk"] = np.ascontiguousarray(Wk.reshape(NH, 128, H).transpose(0, 2, 1))
    d["wv"] = np.ascontiguousarray(
        Wv.T.reshape(H, NVD, VD).transpose(1, 0, 2))
    d["wo"] = np.ascontiguousarray(w_out.reshape(HC, 128, H).transpose(0, 2, 1))
    d["bqk"] = np.ascontiguousarray(
        np.concatenate([b_qkv[0:H].reshape(NH, 128).T, b_qkv[H:2 * H].reshape(NH, 128).T], axis=1)).astype(f32)
    d["bo"] = np.ascontiguousarray(b_out.reshape(HC, 128).T).astype(f32)
    d["bv"] = b_qkv[2 * H:3 * H].reshape(1, H).astype(f32)
    d["wr"] = np.ascontiguousarray(
        router_w.T.reshape(HC, 128, 8).transpose(1, 0, 2).reshape(128, HC * 8)).astype(f32)
    Dcat = down_w.reshape(EI, H)
    d["wd"] = np.ascontiguousarray(
        Dcat.T.reshape(HC, 128, EI).transpose(1, 0, 2).reshape(128, HC * EI)).astype(f32)
    Ucat = up_w.transpose(1, 0, 2).reshape(H, EI)
    d["wu"] = np.ascontiguousarray(Ucat.T).astype(f32)
    cst = np.zeros((128, 256), dtype=BF)
    cst[:, 0:128] = np.eye(128, dtype=BF)
    sel = np.zeros((8, 128), dtype=BF)
    for e in range(8):
        sel[e, e * 16:(e + 1) * 16] = 1
    cst[0:8, 128:256] = sel
    d["cst"] = cst
    d["cstf"] = np.eye(128, dtype=f32)
    return d


from concourse.bass_utils import run_bass_kernel_spmd


def kernel(L0_attn_hidden_states, L0_attn_w_qkv, L0_attn_b_qkv, L0_attn_w_out,
           L0_attn_b_out, L0_ffn_router_w, L0_ffn_down_w, L0_ffn_up_w):
    x = np.asarray(L0_attn_hidden_states)
    w_qkv = np.asarray(L0_attn_w_qkv)
    b_qkv = np.asarray(L0_attn_b_qkv)
    w_out = np.asarray(L0_attn_w_out)
    b_out = np.asarray(L0_attn_b_out)
    router_w = np.asarray(L0_ffn_router_w)
    down_w = np.asarray(L0_ffn_down_w)
    up_w = np.asarray(L0_ffn_up_w)

    B, Sfull, H = x.shape
    cfg = Cfg(NH=16, S=4096, T=2048, QT=512, n_cores=8)
    nc = build(cfg)

    wdict = prep_inputs(cfg, np.zeros((cfg.T, H), np.float32), w_qkv, b_qkv,
                        w_out, b_out, router_w, down_w, up_w)
    del wdict["xT"]

    in_maps = []
    for c in range(8):
        b, hf = c // 2, c % 2
        m = dict(wdict)
        m["xT"] = np.ascontiguousarray(x[b, hf * cfg.T:(hf + 1) * cfg.T, :].T.astype(np.float32))
        in_maps.append(m)

    res = run_bass_kernel_spmd(nc, in_maps, core_ids=list(range(8)))

    out = np.empty((B, Sfull, H), dtype=np.float32)
    for c in range(8):
        b, hf = c // 2, c % 2
        out[b, hf * cfg.T:(hf + 1) * cfg.T, :] = res.results[c]["outT"].T
    return out


# version 28
# speedup vs baseline: 1.1010x; 1.0747x over previous
import math
import numpy as np
import ml_dtypes
from contextlib import ExitStack

import concourse.bass as bass
import concourse.mybir as mybir
from concourse.alu_op_type import AluOpType

F32 = mybir.dt.float32
BF16 = mybir.dt.bfloat16
AF = mybir.ActivationFunctionType
BF = ml_dtypes.bfloat16
ENGINES = ("tensor", "vector", "scalar", "gpsimd", "sync")
DMA_SEMS = {"vector": ["vd0"], "scalar": ["sd0", "sd1"],
            "gpsimd": ["gd0", "gd1", "gd2", "gd3"],
            "sync": ["yd0", "yd1", "yd2", "yd3"]}
ALL_SEMS = ["pe", "ve", "ac", "gp", "sy", "xd", "xe"] + [s for l in DMA_SEMS.values() for s in l]
ENG_SEM = {"tensor": "pe", "vector": "ve", "scalar": "ac", "gpsimd": "gp", "sync": "sy"}


class Ev:
    __slots__ = ("eng", "sem", "idx", "by", "used", "val")

    def __init__(self, eng, sem, idx, by):
        self.eng, self.sem, self.idx, self.by = eng, sem, idx, by
        self.used = False
        self.val = None


class Buf:
    __slots__ = ("w", "r", "name")

    def __init__(self, name=""):
        self.w = None
        self.r = []
        self.name = name


class Sched:
    def __init__(self):
        self.recs = {e: [] for e in ENGINES}
        self.dma_rot = {e: 0 for e in ENGINES}
        self.last_dma = {}
        self.tail = {e: None for e in ENGINES}
        self.pending = {e: [] for e in ENGINES}

    def barrier(self):
        evs = [e for e in self.tail.values() if e is not None]
        evs += [e for e in self.last_dma.values() if e is not None]
        for eng in ENGINES:
            self.pending[eng] = list(evs)

    def op(self, eng, fn, reads=(), writes=(), waits=(), dma=False, inc_by=None, dma_sem=None):
        if dma:
            if dma_sem is not None:
                sem = dma_sem
            else:
                sl = DMA_SEMS[eng]
                sem = sl[self.dma_rot[eng] % len(sl)]
                self.dma_rot[eng] += 1
        else:
            sem = ENG_SEM[eng]
        by = inc_by if inc_by is not None else (16 if dma else 1)
        ev = Ev(eng, sem, len(self.recs[eng]), by)
        wl = []
        if dma:
            ev.used = True
            prev = self.last_dma.get(sem)
            if prev is not None:
                wl.append(prev)
            self.last_dma[sem] = ev
        for b in reads:
            if b.w is not None:
                wl.append(b.w)
        for b in writes:
            if b.w is not None:
                wl.append(b.w)
            wl.extend(b.r)
        for e in waits:
            wl.append(e)
        seen, wl2 = set(), []
        for e in wl:
            if e is None or id(e) in seen:
                continue
            seen.add(id(e))
            wl2.append(e)
        if self.pending[eng]:
            for e in self.pending[eng]:
                if id(e) not in seen and e is not ev:
                    seen.add(id(e))
                    wl2.append(e)
            self.pending[eng] = []
        self.recs[eng].append((fn, wl2, ev))
        self.tail[eng] = ev
        for b in reads:
            b.r.append(ev)
        for b in writes:
            b.w = ev
            b.r = []
        return ev

    def mark(self, *evs):
        for e in evs:
            if e is not None:
                e.used = True

    def finalize(self, nc, block, sems):
        for eng in ENGINES:
            cnt = {}
            for fn, wl, ev in self.recs[eng]:
                for e in wl:
                    e.used = True
        for eng in ENGINES:
            cnt = {}
            for fn, wl, ev in self.recs[eng]:
                if ev.used:
                    cnt[ev.sem] = cnt.get(ev.sem, 0) + ev.by
                    ev.val = cnt[ev.sem]
        def make(eng):
            def emit(h):
                hw = {}
                for fn, wl, ev in self.recs[eng]:
                    for e in wl:
                        if e.val is None:
                            continue
                        key = e.sem
                        if hw.get(key, 0) >= e.val:
                            continue
                        h.wait_ge(sems[e.sem], e.val)
                        hw[key] = e.val
                    inst = fn(h)
                    if ev.used and ev.val is not None:
                        assert inst is not None, f"{eng} op needs inc but returned None"
                        inst.then_inc(sems[ev.sem], ev.by)
            return emit
        for eng in ENGINES:
            if not self.recs[eng]:
                continue
            getattr(block, eng)(make(eng))


class Cfg:
    def __init__(self, NH=16, S=4096, T=2048, QT=512, n_cores=8):
        self.NH = NH
        self.HC = NH
        self.H = NH * 128
        self.S = S
        self.T = T
        self.QT = QT
        self.KC = S // 128
        self.NQT = T // QT
        self.E, self.I = 8, 16
        self.EI = 128
        self.SCALE = 1.0 / math.sqrt(128.0)
        self.VD = min(512, self.H)
        self.NVD = self.H // self.VD
        self.TB = S // 128
        self.TBo = T // 128
        self.n_cores = n_cores
        self.EPS = 1e-5
        self.TW = min(2 * QT, T)
        self.NTW = T // self.TW
        self.JW = min(512, self.TW)
        self.JT = self.TW // self.JW


def build(cfg: Cfg, debug=False):
    nc = bass.Bass("TRN2", target_bir_lowering=False)
    K = Sched()
    H, S, T, QT, NH, HC, KC = cfg.H, cfg.S, cfg.T, cfg.QT, cfg.NH, cfg.HC, cfg.KC
    NQT, EI, VD, NVD, TB = cfg.NQT, cfg.EI, cfg.VD, cfg.NVD, cfg.TB
    TW, NTW, JW, JT = cfg.TW, cfg.NTW, cfg.JW, cfg.JT
    SC = cfg.SCALE
    IH = 1.0 / H

    xT = nc.dram_tensor("xT", [H, T], F32, kind="ExternalInput")
    wq_h = nc.dram_tensor("wq", [NH, HC * 128, 128], F32, kind="ExternalInput")
    wk_h = nc.dram_tensor("wk", [NH, HC * 128, 128], F32, kind="ExternalInput")
    wv_h = nc.dram_tensor("wv", [NVD, HC * 128, VD], F32, kind="ExternalInput")
    wo_h = nc.dram_tensor("wo", [HC, HC * 128, 128], F32, kind="ExternalInput")
    bqk_h = nc.dram_tensor("bqk", [128, 2 * NH], F32, kind="ExternalInput")
    bo_h = nc.dram_tensor("bo", [128, HC], F32, kind="ExternalInput")
    bv_h = nc.dram_tensor("bv", [1, H], F32, kind="ExternalInput")
    wr_h = nc.dram_tensor("wr", [128, HC * 8], F32, kind="ExternalInput")
    wd_h = nc.dram_tensor("wd", [128, HC * EI], F32, kind="ExternalInput")
    wu_h = nc.dram_tensor("wu", [EI, H], F32, kind="ExternalInput")
    cst_h = nc.dram_tensor("cst", [128, 256], BF16, kind="ExternalInput")
    cstf_h = nc.dram_tensor("cstf", [128, 128], F32, kind="ExternalInput")
    outT = nc.dram_tensor("outT", [H, T], F32, kind="ExternalOutput")

    _scr_kind = "Internal"
    TBo = cfg.TBo
    qscr = nc.dram_tensor("qscr", [NH, 128, T], BF16, kind=_scr_kind)
    kag_i = nc.dram_tensor("kag_i", [NH, 128, T], BF16)
    kag_o = nc.dram_tensor("kag_o", [2, NH, 128, T], BF16, kind=_scr_kind)
    vag_i = nc.dram_tensor("vag_i", [TBo, NH, 128, 128], BF16)
    vag_o = nc.dram_tensor("vag_o", [2, TBo, NH, 128, 128], BF16, kind=_scr_kind)
    wo_scr = nc.dram_tensor("wo_scr", [HC, 128, HC * 128], BF16, kind=_scr_kind)
    NGQ_ = T // (min(2, T // 128) * 128)
    d_ao = nc.dram_tensor("d_ao", [NGQ_, 128, HC * (T // NGQ_)], BF16, kind="ExternalOutput") if debug else None
    d_r1 = nc.dram_tensor("d_r1", [NGQ_, 128, HC * (T // NGQ_)], BF16, kind="ExternalOutput") if debug else None
    d_y = nc.dram_tensor("d_y", [NGQ_, 128, HC * (T // NGQ_)], BF16, kind="ExternalOutput") if debug else None
    d_r2 = nc.dram_tensor("d_r2", [NGQ_, 128, HC * (T // NGQ_)], BF16, kind="ExternalOutput") if debug else None
    d_st = nc.dram_tensor("d_st", [NGQ_, 2, 128, 6 * (T // NGQ_)], F32, kind="ExternalOutput") if debug else None
    d_pj = nc.dram_tensor("d_pj", [NGQ_, 2, 128, T // NGQ_], F32, kind="ExternalOutput") if debug else None
    d_xr = nc.dram_tensor("d_xr", [NGQ_, 2, 128, T // NGQ_], F32, kind="ExternalOutput") if debug else None

    es = ExitStack()
    sbt = lambda name, shape, dt: es.enter_context(nc.sbuf_tensor(name, shape, dt))

    cst = sbt("cst_sb", [128, 256], BF16)
    cstf = sbt("cstf_sb", [128, 128], F32)
    sel_sb = sbt("sel_sb", [8, 128], BF16)
    ones_sb = sbt("ones_sb", [128, 128], BF16)
    onesf_sb = sbt("onesf_sb", [1, 128], F32)
    bqk_sb = sbt("bqk_sb", [128, 2 * NH], F32)
    bo_sb = sbt("bo_sb", [128, HC], F32)
    bv_sb = sbt("bv_sb", [1, H], BF16)
    wr_sb = sbt("wr_sb", [128, HC * 8], BF16)
    wd_sb = sbt("wd_sb", [128, HC * EI], BF16)
    wu_sb = sbt("wu_sb", [128, H], BF16)

    NTB = T // 128
    def _al(x):
        return (x + 63) // 64 * 64
    A_XT = 0
    A_wv = _al(A_XT + HC * T)
    A_wqk = _al(A_wv + HC * VD)
    A_kst = _al(A_wqk + 2 * HC * 128)
    A_vst = _al(A_kst + 2 * 512)
    p1_end = _al(A_vst + 2 * 512)
    GQ = min(2, T // 128)
    QW = GQ * 128
    NGQ = T // QW
    SH = S // 2
    NQS_ = T // 128
    A_q2 = 0
    A_k2 = _al(A_q2 + 2 * T)
    A_v2 = _al(A_k2 + 2 * S)
    A_PT = _al(A_v2 + 2 * S)
    NSL = 4
    A_P2s = _al(A_PT + 2 * KC * QW)
    A_junk = _al(A_P2s + NSL * S)
    A_accQ = _al(A_junk + 1280)
    A_mnegs = _al(A_accQ + 2 * 4 * NH * NQS_)
    A_lrep = _al(A_mnegs + 2 * NH * NQS_)
    A_lac = _al(A_lrep + 2 * QW)
    A_lrw = _al(A_lac + 32 + 8)
    A_pvs = _al(A_lrw + 2 * QW)
    A_AO = _al(A_pvs + 2 * 2 * QW)
    A_r1 = _al(A_AO + 2 * HC * QW)
    A_xres = _al(A_r1 + HC * QW)
    A_yst = _al(A_xres + 2 * 2 * QW)
    A_wo = _al(A_yst + 3 * 2 * QW)
    A_ag = _al(A_wo + 2 * HC * 128)
    A_sq2 = _al(A_ag + 2 * QW)
    A_stf = _al(A_sq2 + 2 * QW)
    A_mgc = _al(A_stf + 6 * 2 * QW)
    A_ssm = _al(A_mgc + 2 * QW)
    p2_end = _al(A_ssm + 4 * QW)
    AR_COLS = _al(max(p2_end, p1_end) + 64)
    arena = sbt("arena", [128, AR_COLS], BF16)
    assert AR_COLS * 2 <= 190 * 1024, f"arena {AR_COLS * 2} B/partition too big"

    XT = arena[:, A_XT:A_XT + HC * T]
    wv_sb = arena[:, A_wv:A_wv + HC * VD]
    wqk0 = A_wqk
    kst = arena[:, A_kst:A_kst + 2 * 512]
    vst = arena[:, A_vst:A_vst + 2 * 512]
    q2o, k2o, v2o = A_q2, A_k2, A_v2
    PTs = lambda sb_: arena[:, A_PT + sb_ * KC * QW:A_PT + (sb_ + 1) * KC * QW]
    P2s = arena[:, A_P2s:A_P2s + NSL * S]
    junk = arena[:, A_junk:A_junk + 1280]
    accQ = arena[:, A_accQ:A_accQ + 2 * 4 * NH * NQS_].bitcast(F32)
    accQI = arena[:, A_accQ:A_accQ + 2 * 4 * NH * NQS_].bitcast(mybir.dt.int32)
    mnegs = arena[:, A_mnegs:A_mnegs + 2 * NH * NQS_].bitcast(F32)
    lrep_sb = arena[:, A_lrep:A_lrep + 2 * QW].bitcast(F32)
    lacc = arena[:, A_lac:A_lac + 32].bitcast(F32)
    lcol2 = arena[:, A_lac + 32:A_lac + 36].bitcast(F32)
    lcol2i = arena[:, A_lac + 36:A_lac + 40].bitcast(F32)
    lirow = arena[0:1, A_lrw:A_lrw + 2 * QW].bitcast(F32)
    pvst = arena[:, A_pvs:A_pvs + 2 * 2 * QW].bitcast(F32)
    AO = arena[:, A_AO:A_AO + 2 * HC * QW]
    r1sl = arena[:, A_r1:A_r1 + HC * QW]
    x_res = arena[:, A_xres:A_xres + 2 * 2 * QW].bitcast(F32)
    ystage = arena[:, A_yst:A_yst + 3 * 2 * QW].bitcast(F32)
    wo0 = A_wo
    agst = arena[:, A_ag:A_ag + 2 * QW]
    sqst = arena[:, A_sq2:A_sq2 + 2 * QW]
    stf = arena[:, A_stf:A_stf + 6 * 2 * QW].bitcast(F32)
    stfI = arena[:, A_stf:A_stf + 6 * 2 * QW].bitcast(mybir.dt.int32)
    mgcI = arena[:, A_mgc:A_mgc + 2 * QW].bitcast(mybir.dt.int32)
    mgcF = arena[:, A_mgc:A_mgc + 2 * QW].bitcast(F32)
    ssm = arena[:, A_ssm:A_ssm + 4 * QW]
    lgf = arena[0:8, A_junk:A_junk + 2 * QW].bitcast(F32)
    gT_sb = arena[0:8, A_junk + 2 * QW:A_junk + 3 * QW]
    g0 = A_junk + 3 * QW
    gtok = arena[:, g0:g0 + 2 * GQ * 8].bitcast(F32)
    gtk2 = arena[:, g0 + 2 * GQ * 8:g0 + 4 * GQ * 8].bitcast(F32)
    gtk3 = arena[:, g0 + 4 * GQ * 8:g0 + 6 * GQ * 8].bitcast(F32)
    gsc = arena[:, g0 + 6 * GQ * 8:g0 + 6 * GQ * 8 + 2 * GQ].bitcast(F32)
    gsc2 = arena[:, g0 + 6 * GQ * 8 + 2 * GQ:g0 + 6 * GQ * 8 + 4 * GQ].bitcast(F32)
    assert 3 * QW + 6 * GQ * 8 + 4 * GQ <= 1280

    psA = es.enter_context(nc.psum_tensor("psA", [128, 2048], F32))
    psB = es.enter_context(nc.psum_tensor("psB", [128, 2048], F32))

    sems = {}
    for s in ALL_SEMS:
        sems[s] = es.enter_context(nc.semaphore("sem_" + s))
    block = es.enter_context(nc.Block())

    b_cst, b_cstf, b_sel, b_ones = Buf("cst"), Buf("cstf"), Buf("sel"), Buf("ones")
    b_onesf = Buf("onesf")
    b_bqk, b_bo, b_bv = Buf("bqk"), Buf("bo"), Buf("bv")
    b_wr, b_wd, b_wu = Buf("wr"), Buf("wd"), Buf("wu")
    b_XT = Buf("XT")
    b_wv = Buf("wv")
    b_wqk = [Buf("wqk%d" % i) for i in range(2)]
    b_kst = [Buf("kst%d" % i) for i in range(2)]
    b_vst = [Buf("vst%d" % i) for i in range(2)]
    b_psA = [Buf("psA%d" % i) for i in range(4)]
    b_psB = [Buf("psB%d" % i) for i in range(4)]
    b_pA2 = [b_psA[2], b_psA[2]]
    b_pA3 = [b_psA[3], b_psA[3]]
    b_pB2 = [b_psB[2], b_psB[2]]
    b_pB3 = [b_psB[3], b_psB[3]]
    psA_all = b_psA
    psB_all = b_psB

    def pAb(slot):
        return [b_psA[slot]]

    def pBb(slot):
        return [b_psB[slot]]
    e_kscr = [[] for _ in range(NH)]
    e_qscr = [[] for _ in range(NH)]
    e_vscr = [[] for _ in range(NH)]
    b_k2 = [Buf("k2_%d" % i) for i in range(2)]
    b_q2 = [Buf("q2_%d" % i) for i in range(2)]
    b_q2s = [Buf("q2s%d" % i) for i in range(2)]
    b_v2s = [Buf("v2s%d" % i) for i in range(2)]
    b_P2s = [[Buf("P2s%d_%d" % (i, hh)) for hh in range(2)] for i in range(NSL)]
    b_lacc = [Buf("lacc%d" % i) for i in range(2)]
    b_lc2, b_lc2i, b_lirow = Buf("lc2"), Buf("lc2i"), Buf("lirow")
    b_pvs = [Buf("pvs%d" % i) for i in range(2)]
    b_accQ = Buf("accQ")
    b_mnegs = Buf("mnegs")
    b_PTq = [[Buf("PT%d_%d" % (sb_, q_)) for q_ in range(2)] for sb_ in range(2)]
    b_junk = Buf("junk")
    b_lrep = Buf("lrep")
    b_AO2 = [[Buf("AO%d_%d" % (par, n)) for n in range(HC)] for par in range(2)]
    b_wo = [Buf("wo%d" % i) for i in range(2)]
    b_x2 = [Buf("x2_%d" % i) for i in range(2)]
    b_r1n = [Buf("r1n%d" % o) for o in range(HC)]
    b_sq2 = [Buf("sq2_%d" % i) for i in range(2)]
    b_yst1 = Buf("yst1")
    b_yst2 = [Buf("yst2_%d" % i) for i in range(2)]
    b_ag2 = [Buf("ag2_%d" % i) for i in range(2)]
    b_stf = [Buf("stf%d" % i) for i in range(6)]
    b_ssm = [Buf("ssm%d" % i) for i in range(4)]
    b_lgf2, b_gT2 = Buf("lgf2"), Buf("gT2")
    b_gtok2, b_gtk22, b_gtk32 = Buf("gtok2"), Buf("gtk22"), Buf("gtk32")
    b_gsc_, b_gsc2_ = Buf("gsc_"), Buf("gsc2_")
    b_mgc = Buf("mgc")

    def aosl(par, c):
        return AO[:, par * HC * QW + c * QW:par * HC * QW + (c + 1) * QW]

    g = "gpsimd"
    K.op(g, lambda h: h.dma_start(cst[:, :], cst_h[:, :]), writes=[b_cst], dma=True)
    K.op(g, lambda h: h.dma_start(cstf[:, :], cstf_h[:, :]), writes=[b_cstf], dma=True)
    K.op(g, lambda h: h.dma_start(sel_sb[:, :], cst_h[0:8, 128:256]), writes=[b_sel], dma=True)
    K.op(g, lambda h: h.dma_start(bqk_sb[:, :], bqk_h[:, :]), writes=[b_bqk], dma=True)
    K.op(g, lambda h: h.dma_start(bo_sb[:, :], bo_h[:, :]), writes=[b_bo], dma=True)
    K.op(g, lambda h: h.dma_start(bv_sb[:, :], bv_h[:, :]), writes=[b_bv], dma=True)
    K.op(g, lambda h: h.dma_start(wr_sb[:, :], wr_h[:, :]), writes=[b_wr], dma=True)
    K.op(g, lambda h: h.dma_start(wd_sb[:, :], wd_h[:, :]), writes=[b_wd], dma=True)
    K.op(g, lambda h: h.dma_start(wu_sb[:, :], wu_h[:, :]), writes=[b_wu], dma=True)
    for c in range(HC):
        K.op(g, (lambda c: lambda h: h.dma_start(
            XT[:, c * T:(c + 1) * T], xT[c * 128:(c + 1) * 128, :]))(c),
            writes=[b_XT], dma=True)
    K.op("vector", lambda h: h.memset(ones_sb[:, :], 1.0), writes=[b_ones])
    MAGIC_F = float(np.frombuffer(np.uint32(0x5F3759DF).tobytes(), np.float32)[0])
    K.op("vector", lambda h: h.memset(mgcF[:, :], MAGIC_F), writes=[b_mgc])
    K.op("vector", lambda h: h.memset(onesf_sb[:, :], 1.0), writes=[b_onesf])

    ident = cst[:, 0:128]
    ident8f = cstf[0:8, 0:8]
    identf = cstf[:, :]

    for dt in range(NVD):
        K.op(g, (lambda dt: lambda h: h.dma_start(
            wv_sb[:, :].rearrange("p (c d) -> p c d", d=VD),
            wv_h[dt, :, :].rearrange("(c p) d -> p c d", p=128)))(dt),
            writes=[b_wv], dma=True)
        for tb in range(TBo):
            idx1 = dt * TBo + tb
            slot = idx1 % 4
            if idx1 < HC:
                wcs = idx1 % 2
                K.op(g, (lambda o, wcs: lambda h: h.dma_start(
                    arena[:, wqk0 + wcs * HC * 128:wqk0 + (wcs + 1) * HC * 128]
                    .rearrange("p (c q) -> p c q", q=128),
                    wo_h[o, :, :].rearrange("(c p) q -> p c q", p=128)))(idx1, wcs),
                    writes=[b_wqk[wcs]], dma=True)
                K.op("sync", (lambda o, wcs: lambda h: h.dma_start(
                    wo_scr[o, :, :],
                    arena[:, wqk0 + wcs * HC * 128:wqk0 + (wcs + 1) * HC * 128]))(idx1, wcs),
                    reads=[b_wqk[wcs]], dma=True)
            def mmg(h, dt=dt, tb=tb, slot=slot):
                for c in range(HC):
                    h.matmul(psA[:, slot * 512:slot * 512 + VD],
                             XT[:, c * T + tb * 128:c * T + (tb + 1) * 128],
                             wv_sb[:, c * VD:(c + 1) * VD],
                             start=(c == 0), stop=False)
                return h.matmul(psA[:, slot * 512:slot * 512 + VD],
                                ones_sb[0:1, 0:128], bv_sb[:, dt * VD:(dt + 1) * VD],
                                start=False, stop=True)
            K.op("tensor", mmg, reads=[b_XT, b_wv, b_ones, b_bv], writes=pAb(slot))
            vs = idx1 % 2
            K.op("vector", (lambda slot, vs: lambda h: h.tensor_copy(
                vst[:, vs * 512:vs * 512 + VD], psA[:, slot * 512:slot * 512 + VD]))(slot, vs),
                reads=pAb(slot), writes=[b_vst[vs]])
            nh_t = VD // 128
            def spill(h, dt=dt, tb=tb, vs=vs, nh_t=nh_t):
                return h.dma_start(vag_i[tb, dt * nh_t:(dt + 1) * nh_t, :, :].transpose([1, 0, 2]),
                                   vst[:, vs * 512:vs * 512 + VD].rearrange("p (j d) -> p j d", d=128))
            ev = K.op("sync", spill, reads=[b_vst[vs]], dma=True)
            e_vscr[0].append(ev)

    def qk_proj(which):
        span = T
        tile_w = min(512, span)
        ntile = span // tile_w
        wsrc = wk_h if which == "k" else wq_h
        scr = kag_i if which == "k" else qscr
        escr = e_kscr if which == "k" else e_qscr
        bias0 = NH if which == "k" else 0
        for n in range(NH):
            slot = n % 2
            K.op(g, (lambda n, slot, wsrc=wsrc: lambda h: h.dma_start(
                arena[:, wqk0 + slot * HC * 128: wqk0 + (slot + 1) * HC * 128]
                .rearrange("p (c o) -> p c o", o=128),
                wsrc[n, :, :].rearrange("(c p) o -> p c o", p=128)))(n, slot),
                writes=[b_wqk[slot]], dma=True)
            for j in range(ntile):
                idx = n * ntile + j
                ps = idx % 4
                def mmg(h, n=n, j=j, slot=slot, ps=ps, tile_w=tile_w):
                    for c in range(HC):
                        mm = h.matmul(psB[:, ps * 512:ps * 512 + tile_w],
                                      arena[:, wqk0 + slot * HC * 128 + c * 128:
                                            wqk0 + slot * HC * 128 + (c + 1) * 128],
                                      XT[:, c * T + j * tile_w:c * T + (j + 1) * tile_w],
                                      start=(c == 0), stop=(c == HC - 1))
                    return mm
                K.op("tensor", mmg, reads=[b_XT, b_wqk[slot]], writes=pBb(ps))
                ks = idx % 2
                K.op("scalar", (lambda ps, ks, n, tile_w: lambda h: h.activation(
                    kst[:, ks * 512:ks * 512 + tile_w], psB[:, ps * 512:ps * 512 + tile_w],
                    AF.Identity, bias=bqk_sb[:, bias0 + n:bias0 + n + 1], scale=1.0))(ps, ks, n, tile_w),
                    reads=pBb(ps) + [b_bqk], writes=[b_kst[ks]])
                ev = K.op("sync", (lambda n, j, ks, tile_w, scr=scr: lambda h: h.dma_start(
                    scr[n, :, j * tile_w:(j + 1) * tile_w],
                    kst[:, ks * 512:ks * 512 + tile_w]))(n, j, ks, tile_w),
                    reads=[b_kst[ks]], dma=True)
                escr[n].append(ev)
    qk_proj("k")

    pair_groups = [[i, i + 1] for i in range(0, cfg.n_cores, 2)]
    e_kag = K.op(g, (lambda: lambda h: h.collective_compute(
        "AllGather", AluOpType.bypass, replica_groups=pair_groups,
        ins=[kag_i.ap().opt()], outs=[kag_o.ap().opt()]))(),
        waits=[e for n in range(NH) for e in e_kscr[n]])
    e_vag = K.op(g, (lambda: lambda h: h.collective_compute(
        "AllGather", AluOpType.bypass, replica_groups=pair_groups,
        ins=[vag_i.ap().opt()], outs=[vag_o.ap().opt()]))(),
        waits=e_vscr[0])
    qk_proj("q")

    K.barrier()
    nSA = (SH + 511) // 512
    NQS = T // 128
    import functools

    nstr = max(1, KC // 4)
    NV = NH * NQS
    SSW = (nstr + 2) * 128
    for n in range(NH):
        sl = n % 2
        K.op("sync", (lambda n, sl: lambda h: h.dma_start(
            arena[:, q2o + sl * T:q2o + (sl + 1) * T], qscr[n, :, :]))(n, sl),
            waits=e_qscr[n], writes=[b_q2[sl]], dma=True)
        for half in range(2):
            K.op("sync", (lambda n, sl, half: lambda h: h.dma_start(
                arena[:, k2o + sl * S + half * T:k2o + sl * S + (half + 1) * T],
                kag_o[half, n, :, :]))(n, sl, half),
                waits=[e_kag], writes=[b_k2[sl]], dma=True)
        for qi in range(NQS):
            pp = qi % 2
            psS = psA if pp == 0 else psB
            bps = ([b_psA[0], b_psA[1]] + b_pA2) if pp == 0 else ([b_psB[0], b_psB[1]] + b_pB2)
            hn = max(1, nstr // 2)
            def smp(h, n=n, qi=qi, sl=sl, psS=psS, hn=hn):
                k2v = arena[:, k2o + sl * S:k2o + (sl + 1) * S]
                qv = arena[:, q2o + sl * T + qi * 128:q2o + sl * T + (qi + 1) * 128]
                kstr = k2v.rearrange("p (a b) -> p a b", b=512)[:, :, 0:128]
                h.matmul(psS[:, 0:hn * 128], qv, kstr[:, 0:hn, :], start=True, stop=True)
                if nstr > hn:
                    h.matmul(psS[:, 512:512 + (nstr - hn) * 128], qv, kstr[:, hn:nstr, :],
                             start=True, stop=True)
                return h.matmul(psS[:, 1024:1024 + 256],
                                qv, k2v.rearrange("p (h b) -> p h b", h=2)
                                [:, :, qi * 128:(qi + 1) * 128],
                                start=True, stop=True)
            K.op("tensor", smp, reads=[b_k2[sl], b_q2[sl]], writes=bps)
            c0 = n * NQS + qi
            if nstr == 8:
                K.op("scalar", (lambda c0, psS: lambda h: h.activation(
                    junk[:, 0:1280], psS[:, 0:1280], AF.Exp, scale=SC / 4.0,
                    accum_out=accQ[:, c0:c0 + 1]))(c0, psS),
                    reads=bps, writes=[b_junk, b_accQ])
            else:
                K.op("scalar", (lambda c0, psS, hn: lambda h: h.activation(
                    junk[:, 0:hn * 128], psS[:, 0:hn * 128], AF.Exp, scale=SC / 4.0,
                    accum_out=accQ[:, NV + c0:NV + c0 + 1]))(c0, psS, hn),
                    reads=bps, writes=[b_junk, b_accQ])
                K.op("scalar", (lambda c0, psS: lambda h: h.activation(
                    junk[:, 0:256], psS[:, 1024:1280], AF.Exp, scale=SC / 4.0,
                    accum_out=accQ[:, 2 * NV + c0:2 * NV + c0 + 1]))(c0, psS),
                    reads=bps, writes=[b_junk, b_accQ])
                K.op("vector", (lambda c0: lambda h: h.tensor_tensor(
                    accQ[:, c0:c0 + 1], accQ[:, NV + c0:NV + c0 + 1],
                    accQ[:, 2 * NV + c0:2 * NV + c0 + 1], op=AluOpType.add))(c0),
                    reads=[b_accQ], writes=[b_accQ])
        K.op("vector", (lambda n: lambda h: h.tensor_scalar(
            accQI[:, NH * NQS + n * NQS:NH * NQS + (n + 1) * NQS],
            accQI[:, n * NQS:(n + 1) * NQS], 23, None,
            op0=AluOpType.arith_shift_right))(n),
            reads=[b_accQ], writes=[b_accQ])
        K.op("vector", (lambda n: lambda h: h.tensor_copy(
            accQ[:, 2 * NH * NQS + n * NQS:2 * NH * NQS + (n + 1) * NQS],
            accQI[:, NH * NQS + n * NQS:NH * NQS + (n + 1) * NQS]))(n),
            reads=[b_accQ], writes=[b_accQ])
        K.op("vector", (lambda n: lambda h: h.tensor_scalar(
            mnegs[:, n * NQS:(n + 1) * NQS],
            accQ[:, 2 * NH * NQS + n * NQS:2 * NH * NQS + (n + 1) * NQS],
            -127.0, -4.0 * 0.6931471805599453,
            op0=AluOpType.add, op1=AluOpType.mult))(n),
            reads=[b_accQ], writes=[b_mnegs])

    K.barrier()
    pend = []

    def flush_pending():
        for f in pend:
            f()
        pend.clear()

    q34 = []
    last_out = {}

    def p3p4_tile(qg):
        par = qg % 2
        t0 = qg * QW
        ops = []
        add = lambda f: ops.append([f])
        addb = lambda fs: ops.append(list(fs))

        def accum_stats(o):
            cur = o % 2
            add((lambda o, cur: lambda: K.op("vector", lambda h: h.tensor_tensor(
                sqst[:, cur * QW:(cur + 1) * QW], r1sl[:, o * QW:(o + 1) * QW],
                r1sl[:, o * QW:(o + 1) * QW], op=AluOpType.mult),
                reads=[b_r1n[o]], writes=[b_sq2[cur]]))(o, cur))
            if o == 0:
                add(lambda: K.op("vector", lambda h: h.tensor_copy(
                    ssm[:, 0:QW], r1sl[:, 0:QW]), reads=[b_r1n[0]], writes=[b_ssm[0]]))
                add(lambda: K.op("vector", lambda h: h.tensor_copy(
                    ssm[:, 2 * QW:3 * QW], sqst[:, 0:QW]), reads=[b_sq2[0]], writes=[b_ssm[2]]))
            else:
                prv = 1 - cur
                add((lambda o, cur, prv: lambda: K.op("vector", lambda h: h.tensor_tensor(
                    ssm[:, cur * QW:(cur + 1) * QW], ssm[:, prv * QW:(prv + 1) * QW],
                    r1sl[:, o * QW:(o + 1) * QW], op=AluOpType.add),
                    reads=[b_ssm[prv], b_r1n[o]], writes=[b_ssm[cur]]))(o, cur, prv))
                add((lambda cur, prv: lambda: K.op("vector", lambda h: h.tensor_tensor(
                    ssm[:, (2 + cur) * QW:(3 + cur) * QW],
                    ssm[:, (2 + prv) * QW:(3 + prv) * QW],
                    sqst[:, cur * QW:(cur + 1) * QW], op=AluOpType.add),
                    reads=[b_ssm[2 + prv], b_sq2[cur]], writes=[b_ssm[2 + cur]]))(cur, prv))

        def ln_half(to_ao):
            fin = (HC - 1) % 2
            mu = stf[:, 0:QW]
            rs = stf[:, QW:2 * QW]
            mus = stf[:, 2 * QW:3 * QW]
            scr2 = stf[:, 3 * QW:4 * QW]
            var2 = stf[:, 4 * QW:5 * QW]
            addb([
                lambda: K.op("tensor", lambda h: h.matmul(
                    psB[:, 1280:1280 + QW], ones_sb[:, 0:128], ssm[:, fin * QW:(fin + 1) * QW],
                    start=True, stop=True), reads=[b_ssm[fin], b_ones], writes=[b_pB2[1]]),
                lambda: K.op("tensor", lambda h: h.matmul(
                    psB[:, 1792:1792 + QW], ones_sb[:, 0:128],
                    ssm[:, (2 + fin) * QW:(3 + fin) * QW],
                    start=True, stop=True), reads=[b_ssm[2 + fin], b_ones], writes=[b_pB3[1]]),
                lambda: K.op("vector", lambda h: h.tensor_scalar(
                    mu, psB[:, 1280:1280 + QW], IH, None, op0=AluOpType.mult),
                    reads=[b_pB2[1]], writes=[b_stf[0]]),
                lambda: K.op("vector", lambda h: h.tensor_scalar(
                    scr2, psB[:, 1792:1792 + QW], IH, cfg.EPS,
                    op0=AluOpType.mult, op1=AluOpType.add),
                    reads=[b_pB3[1]], writes=[b_stf[3]]),
            ])
            add(lambda: K.op("vector", lambda h: h.tensor_tensor(
                rs, mu, mu, op=AluOpType.mult), reads=[b_stf[0]], writes=[b_stf[1]]))
            add(lambda: K.op("vector", lambda h: h.tensor_tensor(
                var2, scr2, rs, op=AluOpType.subtract),
                reads=[b_stf[1], b_stf[3]], writes=[b_stf[4]]))
            add(lambda: K.op("vector", lambda h: h.tensor_scalar(
                stfI[:, 3 * QW:4 * QW], stfI[:, 4 * QW:5 * QW], 1, None,
                op0=AluOpType.arith_shift_right),
                reads=[b_stf[4]], writes=[b_stf[3]]))
            add(lambda: K.op("vector", lambda h: h.tensor_tensor(
                stfI[:, QW:2 * QW], mgcI[:, 0:QW], stfI[:, 3 * QW:4 * QW],
                op=AluOpType.subtract), reads=[b_mgc, b_stf[3]], writes=[b_stf[1]]))
            for it in range(2):
                sa, sb = (1, 2) if it % 2 == 0 else (2, 1)
                add((lambda sa: lambda: K.op("vector", lambda h: h.tensor_tensor(
                    stf[:, 3 * QW:4 * QW], stf[:, sa * QW:(sa + 1) * QW],
                    stf[:, sa * QW:(sa + 1) * QW], op=AluOpType.mult),
                    reads=[b_stf[sa]], writes=[b_stf[3]]))(sa))
                add(lambda: K.op("vector", lambda h: h.tensor_tensor(
                    stf[:, 5 * QW:6 * QW], stf[:, 3 * QW:4 * QW], var2,
                    op=AluOpType.mult), reads=[b_stf[3], b_stf[4]], writes=[b_stf[5]]))
                add(lambda: K.op("vector", lambda h: h.tensor_scalar(
                    stf[:, 3 * QW:4 * QW], stf[:, 5 * QW:6 * QW], -0.5, 1.5,
                    op0=AluOpType.mult, op1=AluOpType.add),
                    reads=[b_stf[5]], writes=[b_stf[3]]))
                add((lambda sa, sb: lambda: K.op("vector", lambda h: h.tensor_tensor(
                    stf[:, sb * QW:(sb + 1) * QW], stf[:, sa * QW:(sa + 1) * QW],
                    stf[:, 3 * QW:4 * QW], op=AluOpType.mult),
                    reads=[b_stf[sa], b_stf[3]], writes=[b_stf[sb]]))(sa, sb))
            add(lambda: K.op("vector", lambda h: h.tensor_tensor(
                mus, mu, rs, op=AluOpType.mult), reads=[b_stf[0], b_stf[1]], writes=[b_stf[2]]))
            for o in range(HC):
                ys = o % 2
                add((lambda o: lambda: K.op("vector", lambda h: h.tensor_tensor(
                    ystage[:, 0:QW], r1sl[:, o * QW:(o + 1) * QW], rs,
                    op=AluOpType.mult), reads=[b_r1n[o], b_stf[1]], writes=[b_yst1]))(o))
                add((lambda o, ys: lambda: K.op("vector", lambda h: h.tensor_tensor(
                    ystage[:, (1 + ys) * QW:(2 + ys) * QW], ystage[:, 0:QW], mus,
                    op=AluOpType.subtract),
                    reads=[b_yst1, b_stf[2]], writes=[b_yst2[ys]]))(o, ys))
                if to_ao:
                    add((lambda o, ys: lambda: K.op("scalar", lambda h: h.activation(
                        aosl(par, o), ystage[:, (1 + ys) * QW:(2 + ys) * QW], AF.Copy),
                        reads=[b_yst2[ys]], writes=[b_AO2[par][o]]))(o, ys))
                else:
                    def mk_out(o=o, ys=ys):
                        def go():
                            ev = K.op("scalar", lambda h: h.dma_start(
                                outT[o * 128:(o + 1) * 128, t0:t0 + QW],
                                ystage[:, (1 + ys) * QW:(2 + ys) * QW]),
                                reads=[b_yst2[ys]], dma=True)
                            last_out[ev.sem] = ev
                        return go
                    add(mk_out())

        if debug:
            add(lambda: K.op(g, lambda h: h.dma_start(
                d_ao[qg, :, :], AO[:, par * HC * QW:(par + 1) * HC * QW]),
                reads=list(b_AO2[par]), dma=True))
        for o in range(HC):
            ws = o % 2
            po = 1280 if o % 2 == 0 else 1792
            pb = b_pA2[1] if o % 2 == 0 else b_pA3[1]
            add((lambda o, ws: lambda: K.op(g, lambda h: h.dma_start(
                arena[:, wo0 + ws * HC * 128:wo0 + (ws + 1) * HC * 128], wo_scr[o, :, :]),
                writes=[b_wo[ws]], dma=True))(o, ws))
            add((lambda o: lambda: K.op(g, lambda h: h.dma_start(
                x_res[:, (o % 2) * QW:(o % 2) * QW + QW],
                xT[o * 128:(o + 1) * 128, t0:t0 + QW]),
                writes=[b_x2[o % 2]], dma=True))(o))
            def mk_proj(o=o, ws=ws, po=po, pb=pb):
                def mms(h):
                    for c in range(HC):
                        mm = h.matmul(psA[:, po:po + QW],
                                      arena[:, wo0 + ws * HC * 128 + c * 128:
                                            wo0 + ws * HC * 128 + (c + 1) * 128],
                                      aosl(par, c), start=(c == 0), stop=(c == HC - 1))
                    return mm
                return lambda: K.op("tensor", mms, reads=[b_wo[ws]] + b_AO2[par], writes=[pb])
            bnd = [mk_proj()]
            if debug and o < 2:
                bnd.append((lambda o, po, pb: lambda: K.op("vector", lambda h: h.tensor_copy(
                    ystage[:, 0:QW], psA[:, po:po + QW]),
                    reads=[pb], writes=[b_yst1]))(o, po, pb))
                bnd.append((lambda o: lambda: K.op(g, lambda h: h.dma_start(
                    d_pj[qg, o, :, :], ystage[:, 0:QW]), reads=[b_yst1], dma=True))(o))
                bnd.append((lambda o: lambda: K.op(g, lambda h: h.dma_start(
                    d_xr[qg, o, :, :], x_res[:, (o % 2) * QW:(o % 2) * QW + QW]),
                    reads=[b_x2[o % 2]], dma=True))(o))
            bnd.append((lambda o, po, pb: lambda: K.op("vector", lambda h: h.scalar_tensor_tensor(
                r1sl[:, o * QW:(o + 1) * QW], psA[:, po:po + QW], bo_sb[:, o:o + 1],
                x_res[:, (o % 2) * QW:(o % 2) * QW + QW],
                op0=AluOpType.add, op1=AluOpType.add),
                reads=[pb, b_bo, b_x2[o % 2]], writes=[b_r1n[o]]))(o, po, pb))
            addb(bnd)
            accum_stats(o)
        if debug:
            add(lambda: K.op(g, lambda h: h.dma_start(
                d_r1[qg, :, :], r1sl[:, 0:HC * QW]), reads=list(b_r1n), dma=True))
        ln_half(True)
        if debug:
            add(lambda: K.op(g, lambda h: h.dma_start(
                d_y[qg, :, :], AO[:, par * HC * QW:(par + 1) * HC * QW]),
                reads=list(b_AO2[par]), dma=True))
            add(lambda: K.op(g, lambda h: h.dma_start(
                d_st[qg, 0, :, :], stf[:, 0:6 * QW]), reads=list(b_stf), dma=True))

        def mk_router():
            def mms(h):
                for c in range(HC):
                    mm = h.matmul(psA[0:8, 1280:1280 + QW], wr_sb[:, c * 8:(c + 1) * 8],
                                  aosl(par, c), start=(c == 0), stop=(c == HC - 1))
                return mm
            return lambda: K.op("tensor", mms, reads=[b_wr] + b_AO2[par], writes=[b_pA2[1]])
        addb([mk_router(),
              lambda: K.op("vector", lambda h: h.tensor_copy(
                  lgf[:, 0:QW], psA[0:8, 1280:1280 + QW]), reads=[b_pA2[1]], writes=[b_lgf2])])
        addb([(lambda tb: lambda: K.op("tensor", lambda h: h.transpose(
                  psA[:, 1792 + tb * 8:1792 + tb * 8 + 8],
                  lgf[0:8, tb * 128:(tb + 1) * 128], ident8f),
                  reads=[b_lgf2, b_cstf], writes=[b_pA3[1]]))(tb) for tb in range(GQ)] +
              [lambda: K.op("vector", lambda h: h.tensor_copy(
                  gtok[:, 0:GQ * 8], psA[:, 1792:1792 + GQ * 8]),
                  reads=[b_pA3[1]], writes=[b_gtok2])])
        seg = gtok[:, :].rearrange("p (s e) -> p s e", e=8)
        seg2 = gtk2[:, :].rearrange("p (s e) -> p s e", e=8)
        seg3 = gtk3[:, :].rearrange("p (s e) -> p s e", e=8)
        gb = lambda: gsc[:, :, None].broadcast_to((128, GQ, 8))
        gb2 = lambda: gsc2[:, :, None].broadcast_to((128, GQ, 8))
        add(lambda: K.op("vector", lambda h: h.tensor_reduce(
            gsc[:, :, None], seg, axis=mybir.AxisListType.X, op=AluOpType.max),
            reads=[b_gtok2], writes=[b_gsc_]))
        add(lambda: K.op("vector", lambda h: h.tensor_tensor(
            seg2, seg, gb(), op=AluOpType.subtract),
            reads=[b_gtok2, b_gsc_], writes=[b_gtk22]))
        add(lambda: K.op("scalar", lambda h: h.activation(gtok[:, :], gtk2[:, :], AF.Exp),
            reads=[b_gtk22], writes=[b_gtok2]))
        add(lambda: K.op("vector", lambda h: h.tensor_reduce(
            gsc[:, :, None], seg, axis=mybir.AxisListType.X, op=AluOpType.add),
            reads=[b_gtok2], writes=[b_gsc_]))
        add(lambda: K.op("vector", lambda h: h.reciprocal(gsc2[:, :], gsc[:, :]),
            reads=[b_gsc_], writes=[b_gsc2_]))
        add(lambda: K.op("vector", lambda h: h.tensor_tensor(
            seg2, seg, gb2(), op=AluOpType.mult),
            reads=[b_gsc2_, b_gtok2], writes=[b_gtk22]))
        add(lambda: K.op("vector", lambda h: h.tensor_reduce(
            gsc[:, :, None], seg2, axis=mybir.AxisListType.X, op=AluOpType.max),
            reads=[b_gtk22], writes=[b_gsc_]))
        add(lambda: K.op("vector", lambda h: h.tensor_tensor(
            seg, seg2, gb(), op=AluOpType.is_ge),
            reads=[b_gtk22, b_gsc_], writes=[b_gtok2]))
        add(lambda: K.op("vector", lambda h: h.scalar_tensor_tensor(
            gtk3[:, :], gtok[:, :], -2.0, gtk2[:, :],
            op0=AluOpType.mult, op1=AluOpType.add),
            reads=[b_gtk22, b_gtok2], writes=[b_gtk32]))
        add(lambda: K.op("vector", lambda h: h.tensor_reduce(
            gsc[:, :, None], seg3, axis=mybir.AxisListType.X, op=AluOpType.max),
            reads=[b_gtk32], writes=[b_gsc_]))
        add(lambda: K.op("vector", lambda h: h.tensor_tensor(
            seg, seg2, gb(), op=AluOpType.is_ge),
            reads=[b_gtk22, b_gsc_], writes=[b_gtok2]))
        add(lambda: K.op("vector", lambda h: h.tensor_tensor(
            seg3, seg, seg2, op=AluOpType.mult),
            reads=[b_gtk22, b_gtok2], writes=[b_gtk32]))
        addb([(lambda tb: lambda: K.op("tensor", lambda h: h.transpose(
                  psA[0:8, 1280 + tb * 128:1280 + (tb + 1) * 128],
                  gtk3[:, tb * 8:(tb + 1) * 8], identf),
                  reads=[b_gtk32, b_cstf], writes=[b_pA2[1]]))(tb) for tb in range(GQ)] +
              [lambda: K.op("vector", lambda h: h.tensor_copy(
                  gT_sb[:, 0:QW], psA[0:8, 1280:1280 + QW]),
                  reads=[b_pA2[1]], writes=[b_gT2])])
        def mk_down():
            def mms(h):
                for c in range(HC):
                    mm = h.matmul(psA[:, 1280:1280 + QW],
                                  wd_sb[:, c * EI:(c + 1) * EI][:, 0:128],
                                  aosl(par, c), start=(c == 0), stop=(c == HC - 1))
                return mm
            return lambda: K.op("tensor", mms, reads=[b_wd] + b_AO2[par], writes=[b_pA2[1]])
        addb([
            lambda: K.op("tensor", lambda h: h.matmul(
                psA[:, 1792:1792 + QW], sel_sb[:, :], gT_sb[0:8, 0:QW], start=True, stop=True),
                reads=[b_sel, b_gT2], writes=[b_pA3[1]]),
            mk_down(),
            lambda: K.op("scalar", lambda h: h.activation(
                ystage[:, 0:QW], psA[:, 1280:1280 + QW], AF.Exp, scale=-1.0),
                reads=[b_pA2[1]], writes=[b_yst1]),
            lambda: K.op("vector", lambda h: h.tensor_scalar(
                ystage[:, QW:2 * QW], ystage[:, 0:QW], 1.0, None, op0=AluOpType.add),
                reads=[b_yst1], writes=[b_yst2[0]]),
            lambda: K.op("vector", lambda h: h.reciprocal(
                ystage[:, 2 * QW:3 * QW], ystage[:, QW:2 * QW]),
                reads=[b_yst2[0]], writes=[b_yst2[1]]),
            lambda: K.op("vector", lambda h: h.tensor_tensor(
                agst[:, 0:QW], psA[:, 1280:1280 + QW], ystage[:, 2 * QW:3 * QW],
                op=AluOpType.mult), reads=[b_pA2[1], b_yst2[1]], writes=[b_ag2[0]]),
            lambda: K.op("vector", lambda h: h.tensor_tensor(
                agst[:, QW:2 * QW], agst[:, 0:QW], psA[:, 1792:1792 + QW],
                op=AluOpType.mult), reads=[b_ag2[0], b_pA3[1]], writes=[b_ag2[1]]),
        ])
        for o in range(HC):
            po = 1280 if o % 2 == 0 else 1792
            pb = b_pA2[1] if o % 2 == 0 else b_pA3[1]
            addb([(lambda o, po, pb: lambda: K.op("tensor", lambda h: h.matmul(
                      psA[:, po:po + QW], wu_sb[:, o * 128:(o + 1) * 128],
                      agst[:, QW:2 * QW], start=True, stop=True),
                      reads=[b_wu, b_ag2[1]], writes=[pb]))(o, po, pb),
                  (lambda o, po, pb: lambda: K.op("vector", lambda h: h.tensor_tensor(
                      r1sl[:, o * QW:(o + 1) * QW], psA[:, po:po + QW], aosl(par, o),
                      op=AluOpType.add), reads=[pb, b_AO2[par][o]],
                      writes=[b_r1n[o]]))(o, po, pb)])
            accum_stats(o)
        if debug:
            add(lambda: K.op("sync", lambda h: h.dma_start(
                d_r2[qg, :, :], r1sl[:, 0:HC * QW]), reads=list(b_r1n), dma=True))
        ln_half(False)
        if debug:
            add(lambda: K.op(g, lambda h: h.dma_start(
                d_st[qg, 1, :, :], stf[:, 0:6 * QW]), reads=list(b_stf), dma=True))
        return ops

    for qg in range(NGQ):
        for n in range(NH):
            sl = n % 2
            K.op(g, (lambda n, sl, qg: lambda h: h.dma_start(
                arena[:, q2o + sl * QW:q2o + (sl + 1) * QW],
                qscr[n, :, qg * QW:(qg + 1) * QW]))(n, sl, qg),
                waits=e_qscr[n], writes=[b_q2s[sl]], dma=True)
            for half in range(2):
                K.op(g, (lambda n, sl, half: lambda h: h.dma_start(
                    arena[:, k2o + sl * S + half * T:k2o + sl * S + (half + 1) * T],
                    kag_o[half, n, :, :]))(n, sl, half),
                    waits=[e_kag], writes=[b_k2[sl]], dma=True)
                K.op(g, (lambda n, sl, half: lambda h: h.dma_start(
                    arena[:, v2o + sl * S + half * T:v2o + sl * S + (half + 1) * T]
                    .rearrange("p (tb d) -> p tb d", d=128),
                    vag_o[half, :, n, :, :].transpose([1, 0, 2])))(n, sl, half),
                    waits=[e_vag], writes=[b_v2s[sl]], dma=True)
            for qs in range(GQ):
                qi = qg * GQ + qs
                p = (qs * 2 + n % 2) % NSL
                def smm(h, qs=qs, sl=sl, half=0):
                    for m in range(nSA):
                        w = min(512, SH - m * 512)
                        ps = psA if half == 0 else psB
                        mm = h.matmul(ps[:, m * 512:m * 512 + w],
                                      arena[:, q2o + sl * QW + qs * 128:
                                            q2o + sl * QW + (qs + 1) * 128],
                                      arena[:, k2o + sl * S + half * SH + m * 512:
                                            k2o + sl * S + half * SH + m * 512 + w],
                                      start=True, stop=True)
                    return mm
                K.op("tensor", functools.partial(smm, half=0),
                     reads=[b_k2[sl], b_q2s[sl]], writes=psA_all)
                K.op("tensor", functools.partial(smm, half=1),
                     reads=[b_k2[sl], b_q2s[sl]], writes=psB_all)
                gp = n % 2
                K.op("scalar", (lambda n, qi, p, gp, qs: lambda h: h.activation(
                    P2s[:, p * S:p * S + SH], psA[:, 0:SH], AF.Exp, scale=SC,
                    bias=mnegs[:, n * NQS + qi:n * NQS + qi + 1],
                    accum_out=lacc[:, gp * 4 + 2 * qs:gp * 4 + 2 * qs + 1]))(n, qi, p, gp, qs),
                    reads=psA_all + [b_mnegs], writes=[b_P2s[p][0], b_lacc[gp]])
                K.op("scalar", (lambda n, qi, p, gp, qs: lambda h: h.activation(
                    P2s[:, p * S + SH:p * S + 2 * SH], psB[:, 0:SH], AF.Exp, scale=SC,
                    bias=mnegs[:, n * NQS + qi:n * NQS + qi + 1],
                    accum_out=lacc[:, gp * 4 + 2 * qs + 1:gp * 4 + 2 * qs + 2]))(n, qi, p, gp, qs),
                    reads=psB_all + [b_mnegs], writes=[b_P2s[p][1], b_lacc[gp]])
                flush_pending()
                K.op("sync", (lambda p, qs, sl: lambda h: h.dma_start_transpose(
                    PTs(sl)[:, :].rearrange("p (j q) -> p j q", q=QW)
                    [:, :, qs * 128:(qs + 1) * 128],
                    P2s[:, p * S:(p + 1) * S]))(p, qs, sl),
                    reads=[b_P2s[p][0], b_P2s[p][1]], writes=[b_PTq[sl][qs]],
                    dma=True, dma_sem="xd")
                if qs == GQ - 1:
                    def def_G(n=n, qg=qg, sl=sl, gp=n % 2):
                        def pv(h):
                            for kt in range(KC):
                                mm = h.matmul(psA[:, 1024:1024 + QW],
                                              arena[:, v2o + sl * S + kt * 128:
                                                    v2o + sl * S + (kt + 1) * 128],
                                              PTs(sl)[:, kt * QW:(kt + 1) * QW],
                                              start=(kt == 0), stop=(kt == KC - 1))
                            return mm
                        K.op("tensor", pv, reads=[b_v2s[sl]] + b_PTq[sl], writes=[b_pA2[0]])
                        K.op("vector", (lambda gp: lambda h: h.tensor_copy(
                            pvst[:, gp * QW:(gp + 1) * QW], psA[:, 1024:1024 + QW]))(gp),
                            reads=[b_pA2[0]], writes=[b_pvs[gp]])
                        K.op("vector", (lambda gp: lambda h: h.tensor_tensor(
                            lcol2[:, 0:GQ],
                            lacc[:, 0:8].rearrange("p (g s h) -> p g s h", g=2, h=2)[:, gp, 0:GQ, 0],
                            lacc[:, 0:8].rearrange("p (g s h) -> p g s h", g=2, h=2)[:, gp, 0:GQ, 1],
                            op=AluOpType.add))(gp),
                            reads=[b_lacc[gp]], writes=[b_lc2])
                        K.op("vector", lambda h: h.reciprocal(lcol2i[:, 0:GQ], lcol2[:, 0:GQ]),
                             reads=[b_lc2], writes=[b_lc2i])
                        def ltr(h):
                            for q_ in range(GQ):
                                tr = h.transpose(psA[0:1, 1536 + q_ * 128:1536 + (q_ + 1) * 128],
                                                 lcol2i[:, q_:q_ + 1], identf)
                            return tr
                        K.op("tensor", ltr, reads=[b_lc2i, b_cstf], writes=[b_pA3[0]])
                        K.op("vector", lambda h: h.tensor_copy(
                            lirow[0:1, 0:QW], psA[0:1, 1536:1536 + QW]),
                            reads=[b_pA3[0]], writes=[b_lirow])
                        def lrepmm(h):
                            for q_ in range(GQ):
                                mm = h.matmul(psA[:, 1536 + q_ * 128:1536 + (q_ + 1) * 128],
                                              onesf_sb[0:1, 0:128],
                                              lirow[0:1, q_ * 128:(q_ + 1) * 128],
                                              start=True, stop=True)
                            return mm
                        K.op("tensor", lrepmm, reads=[b_onesf, b_lirow], writes=[b_pA3[0]])
                        K.op("vector", lambda h: h.tensor_copy(
                            lrep_sb[:, 0:QW], psA[:, 1536:1536 + QW]),
                            reads=[b_pA3[0]], writes=[b_lrep])
                        K.op("vector", (lambda n, qg, gp: lambda h: h.tensor_tensor(
                            aosl(qg % 2, n), pvst[:, gp * QW:(gp + 1) * QW],
                            lrep_sb[:, 0:QW],
                            op=AluOpType.mult))(n, qg, gp),
                            reads=[b_pvs[gp], b_lrep], writes=[b_AO2[qg % 2][n]])
                    pend.append(def_G)
                if q34:
                    rem = NH * GQ - (n * GQ + qs)
                    for _ in range(min(len(q34), -(-len(q34) // rem))):
                        for f in q34.pop(0):
                            f()
        q34.extend(p3p4_tile(qg))
    flush_pending()
    for bundle in q34:
        for f in bundle:
            f()
    q34.clear()

    outs = list(last_out.values())
    K.op("sync", lambda h: None, waits=outs)
    K.mark(*outs)

    K.finalize(nc, block, sems)
    es.close()
    return nc


def prep_inputs(cfg, x_bh, w_qkv, b_qkv, w_out, b_out, router_w, down_w, up_w, own_first=True):
    H, NH, HC, S, VD, NVD, EI = cfg.H, cfg.NH, cfg.HC, cfg.S, cfg.VD, cfg.NVD, cfg.EI
    f32 = np.float32
    d = {}
    d["xT"] = np.ascontiguousarray(x_bh.T.astype(f32))
    Wq, Wk, Wv = w_qkv[0:H], w_qkv[H:2 * H], w_qkv[2 * H:3 * H]
    d["wq"] = np.ascontiguousarray(
        Wq.reshape(NH, 128, H).transpose(0, 2, 1))
    d["wk"] = np.ascontiguousarray(Wk.reshape(NH, 128, H).transpose(0, 2, 1))
    d["wv"] = np.ascontiguousarray(
        Wv.T.reshape(H, NVD, VD).transpose(1, 0, 2))
    d["wo"] = np.ascontiguousarray(w_out.reshape(HC, 128, H).transpose(0, 2, 1))
    d["bqk"] = np.ascontiguousarray(
        np.concatenate([b_qkv[0:H].reshape(NH, 128).T, b_qkv[H:2 * H].reshape(NH, 128).T], axis=1)).astype(f32)
    d["bo"] = np.ascontiguousarray(b_out.reshape(HC, 128).T).astype(f32)
    d["bv"] = b_qkv[2 * H:3 * H].reshape(1, H).astype(f32)
    d["wr"] = np.ascontiguousarray(
        router_w.T.reshape(HC, 128, 8).transpose(1, 0, 2).reshape(128, HC * 8)).astype(f32)
    Dcat = down_w.reshape(EI, H)
    d["wd"] = np.ascontiguousarray(
        Dcat.T.reshape(HC, 128, EI).transpose(1, 0, 2).reshape(128, HC * EI)).astype(f32)
    Ucat = up_w.transpose(1, 0, 2).reshape(H, EI)
    d["wu"] = np.ascontiguousarray(Ucat.T).astype(f32)
    cst = np.zeros((128, 256), dtype=BF)
    cst[:, 0:128] = np.eye(128, dtype=BF)
    sel = np.zeros((8, 128), dtype=BF)
    for e in range(8):
        sel[e, e * 16:(e + 1) * 16] = 1
    cst[0:8, 128:256] = sel
    d["cst"] = cst
    d["cstf"] = np.eye(128, dtype=f32)
    return d


from concourse.bass_utils import run_bass_kernel_spmd


def kernel(L0_attn_hidden_states, L0_attn_w_qkv, L0_attn_b_qkv, L0_attn_w_out,
           L0_attn_b_out, L0_ffn_router_w, L0_ffn_down_w, L0_ffn_up_w):
    x = np.asarray(L0_attn_hidden_states)
    w_qkv = np.asarray(L0_attn_w_qkv)
    b_qkv = np.asarray(L0_attn_b_qkv)
    w_out = np.asarray(L0_attn_w_out)
    b_out = np.asarray(L0_attn_b_out)
    router_w = np.asarray(L0_ffn_router_w)
    down_w = np.asarray(L0_ffn_down_w)
    up_w = np.asarray(L0_ffn_up_w)

    B, Sfull, H = x.shape
    cfg = Cfg(NH=16, S=4096, T=2048, QT=512, n_cores=8)
    nc = build(cfg)

    wdict = prep_inputs(cfg, np.zeros((cfg.T, H), np.float32), w_qkv, b_qkv,
                        w_out, b_out, router_w, down_w, up_w)
    del wdict["xT"]

    in_maps = []
    for c in range(8):
        b, hf = c // 2, c % 2
        m = dict(wdict)
        m["xT"] = np.ascontiguousarray(x[b, hf * cfg.T:(hf + 1) * cfg.T, :].T.astype(np.float32))
        in_maps.append(m)

    res = run_bass_kernel_spmd(nc, in_maps, core_ids=list(range(8)))

    out = np.empty((B, Sfull, H), dtype=np.float32)
    for c in range(8):
        b, hf = c // 2, c % 2
        out[b, hf * cfg.T:(hf + 1) * cfg.T, :] = res.results[c]["outT"].T
    return out


# version 29
# speedup vs baseline: 1.1079x; 1.0063x over previous
import math
import numpy as np
import ml_dtypes
from contextlib import ExitStack

import concourse.bass as bass
import concourse.mybir as mybir
from concourse.alu_op_type import AluOpType

F32 = mybir.dt.float32
BF16 = mybir.dt.bfloat16
AF = mybir.ActivationFunctionType
BF = ml_dtypes.bfloat16
ENGINES = ("tensor", "vector", "scalar", "gpsimd", "sync")
DMA_SEMS = {"vector": ["vd0"], "scalar": ["sd0", "sd1"],
            "gpsimd": ["gd0", "gd1", "gd2", "gd3"],
            "sync": ["yd0", "yd1", "yd2"]}
ALL_SEMS = ["pe", "ve", "ac", "gp", "sy", "xd", "yd3"] + [s for l in DMA_SEMS.values() for s in l]
ENG_SEM = {"tensor": "pe", "vector": "ve", "scalar": "ac", "gpsimd": "gp", "sync": "sy"}


class Ev:
    __slots__ = ("eng", "sem", "idx", "by", "used", "val")

    def __init__(self, eng, sem, idx, by):
        self.eng, self.sem, self.idx, self.by = eng, sem, idx, by
        self.used = False
        self.val = None


class Buf:
    __slots__ = ("w", "r", "name")

    def __init__(self, name=""):
        self.w = None
        self.r = []
        self.name = name


class Sched:
    def __init__(self):
        self.recs = {e: [] for e in ENGINES}
        self.dma_rot = {e: 0 for e in ENGINES}
        self.last_dma = {}
        self.tail = {e: None for e in ENGINES}
        self.pending = {e: [] for e in ENGINES}

    def barrier(self):
        evs = [e for e in self.tail.values() if e is not None]
        evs += [e for e in self.last_dma.values() if e is not None]
        for eng in ENGINES:
            self.pending[eng] = list(evs)

    def op(self, eng, fn, reads=(), writes=(), waits=(), dma=False, inc_by=None, dma_sem=None):
        if dma:
            if dma_sem is not None:
                sem = dma_sem
            else:
                sl = DMA_SEMS[eng]
                sem = sl[self.dma_rot[eng] % len(sl)]
                self.dma_rot[eng] += 1
        else:
            sem = ENG_SEM[eng]
        by = inc_by if inc_by is not None else (16 if dma else 1)
        ev = Ev(eng, sem, len(self.recs[eng]), by)
        wl = []
        if dma:
            ev.used = True
            prev = self.last_dma.get(sem)
            if prev is not None:
                wl.append(prev)
            self.last_dma[sem] = ev
        for b in reads:
            if b.w is not None:
                wl.append(b.w)
        for b in writes:
            if b.w is not None:
                wl.append(b.w)
            wl.extend(b.r)
        for e in waits:
            wl.append(e)
        seen, wl2 = set(), []
        for e in wl:
            if e is None or id(e) in seen:
                continue
            seen.add(id(e))
            wl2.append(e)
        if self.pending[eng]:
            for e in self.pending[eng]:
                if id(e) not in seen and e is not ev:
                    seen.add(id(e))
                    wl2.append(e)
            self.pending[eng] = []
        self.recs[eng].append((fn, wl2, ev))
        self.tail[eng] = ev
        for b in reads:
            b.r.append(ev)
        for b in writes:
            b.w = ev
            b.r = []
        return ev

    def mark(self, *evs):
        for e in evs:
            if e is not None:
                e.used = True

    def finalize(self, nc, block, sems):
        for eng in ENGINES:
            cnt = {}
            for fn, wl, ev in self.recs[eng]:
                for e in wl:
                    e.used = True
        for eng in ENGINES:
            cnt = {}
            for fn, wl, ev in self.recs[eng]:
                if ev.used:
                    cnt[ev.sem] = cnt.get(ev.sem, 0) + ev.by
                    ev.val = cnt[ev.sem]
        def make(eng):
            def emit(h):
                hw = {}
                for fn, wl, ev in self.recs[eng]:
                    for e in wl:
                        if e.val is None:
                            continue
                        key = e.sem
                        if hw.get(key, 0) >= e.val:
                            continue
                        h.wait_ge(sems[e.sem], e.val)
                        hw[key] = e.val
                    inst = fn(h)
                    if ev.used and ev.val is not None:
                        assert inst is not None, f"{eng} op needs inc but returned None"
                        inst.then_inc(sems[ev.sem], ev.by)
            return emit
        for eng in ENGINES:
            if not self.recs[eng]:
                continue
            getattr(block, eng)(make(eng))


class Cfg:
    def __init__(self, NH=16, S=4096, T=2048, QT=512, n_cores=8):
        self.NH = NH
        self.HC = NH
        self.H = NH * 128
        self.S = S
        self.T = T
        self.QT = QT
        self.KC = S // 128
        self.NQT = T // QT
        self.E, self.I = 8, 16
        self.EI = 128
        self.SCALE = 1.0 / math.sqrt(128.0)
        self.VD = min(512, self.H)
        self.NVD = self.H // self.VD
        self.TB = S // 128
        self.TBo = T // 128
        self.n_cores = n_cores
        self.EPS = 1e-5
        self.TW = min(2 * QT, T)
        self.NTW = T // self.TW
        self.JW = min(512, self.TW)
        self.JT = self.TW // self.JW


def build(cfg: Cfg, debug=False):
    nc = bass.Bass("TRN2", target_bir_lowering=False)
    K = Sched()
    H, S, T, QT, NH, HC, KC = cfg.H, cfg.S, cfg.T, cfg.QT, cfg.NH, cfg.HC, cfg.KC
    NQT, EI, VD, NVD, TB = cfg.NQT, cfg.EI, cfg.VD, cfg.NVD, cfg.TB
    TW, NTW, JW, JT = cfg.TW, cfg.NTW, cfg.JW, cfg.JT
    SC = cfg.SCALE
    IH = 1.0 / H

    xT = nc.dram_tensor("xT", [H, T], F32, kind="ExternalInput")
    wq_h = nc.dram_tensor("wq", [NH, HC * 128, 128], F32, kind="ExternalInput")
    wk_h = nc.dram_tensor("wk", [NH, HC * 128, 128], F32, kind="ExternalInput")
    wv_h = nc.dram_tensor("wv", [NVD, HC * 128, VD], F32, kind="ExternalInput")
    wo_h = nc.dram_tensor("wo", [HC, HC * 128, 128], F32, kind="ExternalInput")
    bqk_h = nc.dram_tensor("bqk", [128, 2 * NH], F32, kind="ExternalInput")
    bo_h = nc.dram_tensor("bo", [128, HC], F32, kind="ExternalInput")
    bv_h = nc.dram_tensor("bv", [1, H], F32, kind="ExternalInput")
    wr_h = nc.dram_tensor("wr", [128, HC * 8], F32, kind="ExternalInput")
    wd_h = nc.dram_tensor("wd", [128, HC * EI], F32, kind="ExternalInput")
    wu_h = nc.dram_tensor("wu", [EI, H], F32, kind="ExternalInput")
    cst_h = nc.dram_tensor("cst", [128, 256], BF16, kind="ExternalInput")
    cstf_h = nc.dram_tensor("cstf", [128, 128], F32, kind="ExternalInput")
    outT = nc.dram_tensor("outT", [H, T], F32, kind="ExternalOutput")

    _scr_kind = "Internal"
    TBo = cfg.TBo
    qscr = nc.dram_tensor("qscr", [NH, 128, T], BF16, kind=_scr_kind)
    kag_i = nc.dram_tensor("kag_i", [NH, 128, T], BF16)
    kag_o = nc.dram_tensor("kag_o", [2, NH, 128, T], BF16, kind=_scr_kind)
    vag_i = nc.dram_tensor("vag_i", [TBo, NH, 128, 128], BF16)
    vag_o = nc.dram_tensor("vag_o", [2, TBo, NH, 128, 128], BF16, kind=_scr_kind)
    wo_scr = nc.dram_tensor("wo_scr", [HC, 128, HC * 128], BF16, kind=_scr_kind)
    NGQ_ = T // (min(2, T // 128) * 128)
    d_ao = nc.dram_tensor("d_ao", [NGQ_, 128, HC * (T // NGQ_)], BF16, kind="ExternalOutput") if debug else None
    d_r1 = nc.dram_tensor("d_r1", [NGQ_, 128, HC * (T // NGQ_)], BF16, kind="ExternalOutput") if debug else None
    d_y = nc.dram_tensor("d_y", [NGQ_, 128, HC * (T // NGQ_)], BF16, kind="ExternalOutput") if debug else None
    d_r2 = nc.dram_tensor("d_r2", [NGQ_, 128, HC * (T // NGQ_)], BF16, kind="ExternalOutput") if debug else None
    d_st = nc.dram_tensor("d_st", [NGQ_, 2, 128, 6 * (T // NGQ_)], F32, kind="ExternalOutput") if debug else None
    d_pj = nc.dram_tensor("d_pj", [NGQ_, 2, 128, T // NGQ_], F32, kind="ExternalOutput") if debug else None
    d_xr = nc.dram_tensor("d_xr", [NGQ_, 2, 128, T // NGQ_], F32, kind="ExternalOutput") if debug else None

    es = ExitStack()
    sbt = lambda name, shape, dt: es.enter_context(nc.sbuf_tensor(name, shape, dt))

    cst = sbt("cst_sb", [128, 256], BF16)
    cstf = sbt("cstf_sb", [128, 128], F32)
    sel_sb = sbt("sel_sb", [8, 128], BF16)
    ones_sb = sbt("ones_sb", [128, 128], BF16)
    onesf_sb = sbt("onesf_sb", [1, 128], F32)
    bqk_sb = sbt("bqk_sb", [128, 2 * NH], F32)
    bo_sb = sbt("bo_sb", [128, HC], F32)
    bv_sb = sbt("bv_sb", [1, H], BF16)
    wr_sb = sbt("wr_sb", [128, HC * 8], BF16)
    wd_sb = sbt("wd_sb", [128, HC * EI], BF16)
    wu_sb = sbt("wu_sb", [128, H], BF16)

    NTB = T // 128
    def _al(x):
        return (x + 63) // 64 * 64
    A_XT = 0
    A_wv = _al(A_XT + HC * T)
    A_wqk = _al(A_wv + HC * VD)
    A_kst = _al(A_wqk + 2 * HC * 128)
    A_vst = _al(A_kst + 2 * 512)
    p1_end = _al(A_vst + 2 * 512)
    GQ = min(2, T // 128)
    QW = GQ * 128
    NGQ = T // QW
    SH = S // 2
    NQS_ = T // 128
    A_q2 = 0
    A_k2 = _al(A_q2 + 2 * T)
    A_v2 = _al(A_k2 + 2 * S)
    A_PT = _al(A_v2 + 2 * S)
    NSL = 4
    A_P2s = _al(A_PT + 2 * KC * QW)
    A_junk = _al(A_P2s + NSL * S)
    A_accQ = _al(A_junk + 1280)
    A_mnegs = _al(A_accQ + 2 * 4 * NH * NQS_)
    A_lrep = _al(A_mnegs + 2 * NH * NQS_)
    A_lac = _al(A_lrep + 2 * QW)
    A_lrw = _al(A_lac + 32 + 8)
    A_pvs = _al(A_lrw + 2 * QW)
    A_AO = _al(A_pvs + 2 * 2 * QW)
    A_r1 = _al(A_AO + 2 * HC * QW)
    A_xres = _al(A_r1 + HC * QW)
    A_yst = _al(A_xres + 2 * 2 * QW)
    A_wo = _al(A_yst + 3 * 2 * QW)
    A_ag = _al(A_wo + 2 * HC * 128)
    A_sq2 = _al(A_ag + 2 * QW)
    A_stf = _al(A_sq2 + 2 * QW)
    A_mgc = _al(A_stf + 6 * 2 * QW)
    A_ssm = _al(A_mgc + 2 * QW)
    p2_end = _al(A_ssm + 4 * QW)
    AR_COLS = _al(max(p2_end, p1_end) + 64)
    arena = sbt("arena", [128, AR_COLS], BF16)
    assert AR_COLS * 2 <= 190 * 1024, f"arena {AR_COLS * 2} B/partition too big"

    XT = arena[:, A_XT:A_XT + HC * T]
    wv_sb = arena[:, A_wv:A_wv + HC * VD]
    wqk0 = A_wqk
    kst = arena[:, A_kst:A_kst + 2 * 512]
    vst = arena[:, A_vst:A_vst + 2 * 512]
    q2o, k2o, v2o = A_q2, A_k2, A_v2
    PTs = lambda sb_: arena[:, A_PT + sb_ * KC * QW:A_PT + (sb_ + 1) * KC * QW]
    P2s = arena[:, A_P2s:A_P2s + NSL * S]
    junk = arena[:, A_junk:A_junk + 1280]
    accQ = arena[:, A_accQ:A_accQ + 2 * 4 * NH * NQS_].bitcast(F32)
    accQI = arena[:, A_accQ:A_accQ + 2 * 4 * NH * NQS_].bitcast(mybir.dt.int32)
    mnegs = arena[:, A_mnegs:A_mnegs + 2 * NH * NQS_].bitcast(F32)
    lrep_sb = arena[:, A_lrep:A_lrep + 2 * QW].bitcast(F32)
    lacc = arena[:, A_lac:A_lac + 32].bitcast(F32)
    lcol2 = arena[:, A_lac + 32:A_lac + 36].bitcast(F32)
    lcol2i = arena[:, A_lac + 36:A_lac + 40].bitcast(F32)
    lirow = arena[0:1, A_lrw:A_lrw + 2 * QW].bitcast(F32)
    pvst = arena[:, A_pvs:A_pvs + 2 * 2 * QW].bitcast(F32)
    AO = arena[:, A_AO:A_AO + 2 * HC * QW]
    r1sl = arena[:, A_r1:A_r1 + HC * QW]
    x_res = arena[:, A_xres:A_xres + 2 * 2 * QW].bitcast(F32)
    ystage = arena[:, A_yst:A_yst + 3 * 2 * QW].bitcast(F32)
    wo0 = A_wo
    agst = arena[:, A_ag:A_ag + 2 * QW]
    sqst = arena[:, A_sq2:A_sq2 + 2 * QW]
    stf = arena[:, A_stf:A_stf + 6 * 2 * QW].bitcast(F32)
    stfI = arena[:, A_stf:A_stf + 6 * 2 * QW].bitcast(mybir.dt.int32)
    mgcI = arena[:, A_mgc:A_mgc + 2 * QW].bitcast(mybir.dt.int32)
    mgcF = arena[:, A_mgc:A_mgc + 2 * QW].bitcast(F32)
    ssm = arena[:, A_ssm:A_ssm + 4 * QW]
    lgf = arena[0:8, A_junk:A_junk + 2 * QW].bitcast(F32)
    gT_sb = arena[0:8, A_junk + 2 * QW:A_junk + 3 * QW]
    g0 = A_junk + 3 * QW
    gtok = arena[:, g0:g0 + 2 * GQ * 8].bitcast(F32)
    gtk2 = arena[:, g0 + 2 * GQ * 8:g0 + 4 * GQ * 8].bitcast(F32)
    gtk3 = arena[:, g0 + 4 * GQ * 8:g0 + 6 * GQ * 8].bitcast(F32)
    gsc = arena[:, g0 + 6 * GQ * 8:g0 + 6 * GQ * 8 + 2 * GQ].bitcast(F32)
    gsc2 = arena[:, g0 + 6 * GQ * 8 + 2 * GQ:g0 + 6 * GQ * 8 + 4 * GQ].bitcast(F32)
    assert 3 * QW + 6 * GQ * 8 + 4 * GQ <= 1280

    psA = es.enter_context(nc.psum_tensor("psA", [128, 2048], F32))
    psB = es.enter_context(nc.psum_tensor("psB", [128, 2048], F32))

    sems = {}
    for s in ALL_SEMS:
        sems[s] = es.enter_context(nc.semaphore("sem_" + s))
    block = es.enter_context(nc.Block())

    b_cst, b_cstf, b_sel, b_ones = Buf("cst"), Buf("cstf"), Buf("sel"), Buf("ones")
    b_onesf = Buf("onesf")
    b_bqk, b_bo, b_bv = Buf("bqk"), Buf("bo"), Buf("bv")
    b_wr, b_wd, b_wu = Buf("wr"), Buf("wd"), Buf("wu")
    b_XT = Buf("XT")
    b_wv = Buf("wv")
    b_wqk = [Buf("wqk%d" % i) for i in range(2)]
    b_kst = [Buf("kst%d" % i) for i in range(2)]
    b_vst = [Buf("vst%d" % i) for i in range(2)]
    b_psA = [Buf("psA%d" % i) for i in range(4)]
    b_psB = [Buf("psB%d" % i) for i in range(4)]
    b_pA2 = [b_psA[2], b_psA[2]]
    b_pA3 = [b_psA[3], b_psA[3]]
    b_pB2 = [b_psB[2], b_psB[2]]
    b_pB3 = [b_psB[3], b_psB[3]]
    psA_all = b_psA
    psB_all = b_psB

    def pAb(slot):
        return [b_psA[slot]]

    def pBb(slot):
        return [b_psB[slot]]
    e_kscr = [[] for _ in range(NH)]
    e_qscr = [[] for _ in range(NH)]
    e_vscr = [[] for _ in range(NH)]
    b_k2 = [Buf("k2_%d" % i) for i in range(2)]
    b_q2 = [Buf("q2_%d" % i) for i in range(2)]
    b_q2s = [Buf("q2s%d" % i) for i in range(2)]
    b_v2s = [Buf("v2s%d" % i) for i in range(2)]
    b_P2s = [[Buf("P2s%d_%d" % (i, hh)) for hh in range(2)] for i in range(NSL)]
    b_lacc = [Buf("lacc%d" % i) for i in range(2)]
    b_lc2, b_lc2i, b_lirow = Buf("lc2"), Buf("lc2i"), Buf("lirow")
    b_pvs = [Buf("pvs%d" % i) for i in range(2)]
    b_accQ = Buf("accQ")
    b_mnegs = Buf("mnegs")
    b_PTq = [[Buf("PT%d_%d" % (sb_, q_)) for q_ in range(2)] for sb_ in range(2)]
    b_junk = Buf("junk")
    b_lrep = Buf("lrep")
    b_AO2 = [[Buf("AO%d_%d" % (par, n)) for n in range(HC)] for par in range(2)]
    b_wo = [Buf("wo%d" % i) for i in range(2)]
    b_x2 = [Buf("x2_%d" % i) for i in range(2)]
    b_r1n = [Buf("r1n%d" % o) for o in range(HC)]
    b_sq2 = [Buf("sq2_%d" % i) for i in range(2)]
    b_yst1 = Buf("yst1")
    b_yst2 = [Buf("yst2_%d" % i) for i in range(2)]
    b_ag2 = [Buf("ag2_%d" % i) for i in range(2)]
    b_stf = [Buf("stf%d" % i) for i in range(6)]
    b_ssm = [Buf("ssm%d" % i) for i in range(4)]
    b_lgf2, b_gT2 = Buf("lgf2"), Buf("gT2")
    b_gtok2, b_gtk22, b_gtk32 = Buf("gtok2"), Buf("gtk22"), Buf("gtk32")
    b_gsc_, b_gsc2_ = Buf("gsc_"), Buf("gsc2_")
    b_mgc = Buf("mgc")

    def aosl(par, c):
        return AO[:, par * HC * QW + c * QW:par * HC * QW + (c + 1) * QW]

    g = "gpsimd"
    K.op(g, lambda h: h.dma_start(cst[:, :], cst_h[:, :]), writes=[b_cst], dma=True)
    K.op(g, lambda h: h.dma_start(cstf[:, :], cstf_h[:, :]), writes=[b_cstf], dma=True)
    K.op(g, lambda h: h.dma_start(sel_sb[:, :], cst_h[0:8, 128:256]), writes=[b_sel], dma=True)
    K.op(g, lambda h: h.dma_start(bqk_sb[:, :], bqk_h[:, :]), writes=[b_bqk], dma=True)
    K.op(g, lambda h: h.dma_start(bo_sb[:, :], bo_h[:, :]), writes=[b_bo], dma=True)
    K.op(g, lambda h: h.dma_start(bv_sb[:, :], bv_h[:, :]), writes=[b_bv], dma=True)
    K.op(g, lambda h: h.dma_start(wr_sb[:, :], wr_h[:, :]), writes=[b_wr], dma=True)
    K.op(g, lambda h: h.dma_start(wd_sb[:, :], wd_h[:, :]), writes=[b_wd], dma=True)
    K.op(g, lambda h: h.dma_start(wu_sb[:, :], wu_h[:, :]), writes=[b_wu], dma=True)
    for c in range(HC):
        K.op(g, (lambda c: lambda h: h.dma_start(
            XT[:, c * T:(c + 1) * T], xT[c * 128:(c + 1) * 128, :]))(c),
            writes=[b_XT], dma=True)
    K.op("vector", lambda h: h.memset(ones_sb[:, :], 1.0), writes=[b_ones])
    MAGIC_F = float(np.frombuffer(np.uint32(0x5F3759DF).tobytes(), np.float32)[0])
    K.op("vector", lambda h: h.memset(mgcF[:, :], MAGIC_F), writes=[b_mgc])
    K.op("vector", lambda h: h.memset(onesf_sb[:, :], 1.0), writes=[b_onesf])

    ident = cst[:, 0:128]
    ident8f = cstf[0:8, 0:8]
    identf = cstf[:, :]

    for dt in range(NVD):
        K.op(g, (lambda dt: lambda h: h.dma_start(
            wv_sb[:, :].rearrange("p (c d) -> p c d", d=VD),
            wv_h[dt, :, :].rearrange("(c p) d -> p c d", p=128)))(dt),
            writes=[b_wv], dma=True)
        for tb in range(TBo):
            idx1 = dt * TBo + tb
            slot = idx1 % 4
            if idx1 < HC:
                wcs = idx1 % 2
                K.op(g, (lambda o, wcs: lambda h: h.dma_start(
                    arena[:, wqk0 + wcs * HC * 128:wqk0 + (wcs + 1) * HC * 128]
                    .rearrange("p (c q) -> p c q", q=128),
                    wo_h[o, :, :].rearrange("(c p) q -> p c q", p=128)))(idx1, wcs),
                    writes=[b_wqk[wcs]], dma=True)
                K.op("sync", (lambda o, wcs: lambda h: h.dma_start(
                    wo_scr[o, :, :],
                    arena[:, wqk0 + wcs * HC * 128:wqk0 + (wcs + 1) * HC * 128]))(idx1, wcs),
                    reads=[b_wqk[wcs]], dma=True)
            def mmg(h, dt=dt, tb=tb, slot=slot):
                for c in range(HC):
                    h.matmul(psA[:, slot * 512:slot * 512 + VD],
                             XT[:, c * T + tb * 128:c * T + (tb + 1) * 128],
                             wv_sb[:, c * VD:(c + 1) * VD],
                             start=(c == 0), stop=False)
                return h.matmul(psA[:, slot * 512:slot * 512 + VD],
                                ones_sb[0:1, 0:128], bv_sb[:, dt * VD:(dt + 1) * VD],
                                start=False, stop=True)
            K.op("tensor", mmg, reads=[b_XT, b_wv, b_ones, b_bv], writes=pAb(slot))
            vs = idx1 % 2
            K.op("vector", (lambda slot, vs: lambda h: h.tensor_copy(
                vst[:, vs * 512:vs * 512 + VD], psA[:, slot * 512:slot * 512 + VD]))(slot, vs),
                reads=pAb(slot), writes=[b_vst[vs]])
            nh_t = VD // 128
            def spill(h, dt=dt, tb=tb, vs=vs, nh_t=nh_t):
                return h.dma_start(vag_i[tb, dt * nh_t:(dt + 1) * nh_t, :, :].transpose([1, 0, 2]),
                                   vst[:, vs * 512:vs * 512 + VD].rearrange("p (j d) -> p j d", d=128))
            ev = K.op("sync", spill, reads=[b_vst[vs]], dma=True)
            e_vscr[0].append(ev)

    def qk_proj(which):
        span = T
        tile_w = min(512, span)
        ntile = span // tile_w
        wsrc = wk_h if which == "k" else wq_h
        scr = kag_i if which == "k" else qscr
        escr = e_kscr if which == "k" else e_qscr
        bias0 = NH if which == "k" else 0
        for n in range(NH):
            slot = n % 2
            K.op(g, (lambda n, slot, wsrc=wsrc: lambda h: h.dma_start(
                arena[:, wqk0 + slot * HC * 128: wqk0 + (slot + 1) * HC * 128]
                .rearrange("p (c o) -> p c o", o=128),
                wsrc[n, :, :].rearrange("(c p) o -> p c o", p=128)))(n, slot),
                writes=[b_wqk[slot]], dma=True)
            for j in range(ntile):
                idx = n * ntile + j
                ps = idx % 4
                def mmg(h, n=n, j=j, slot=slot, ps=ps, tile_w=tile_w):
                    for c in range(HC):
                        mm = h.matmul(psB[:, ps * 512:ps * 512 + tile_w],
                                      arena[:, wqk0 + slot * HC * 128 + c * 128:
                                            wqk0 + slot * HC * 128 + (c + 1) * 128],
                                      XT[:, c * T + j * tile_w:c * T + (j + 1) * tile_w],
                                      start=(c == 0), stop=(c == HC - 1))
                    return mm
                K.op("tensor", mmg, reads=[b_XT, b_wqk[slot]], writes=pBb(ps))
                ks = idx % 2
                K.op("scalar", (lambda ps, ks, n, tile_w: lambda h: h.activation(
                    kst[:, ks * 512:ks * 512 + tile_w], psB[:, ps * 512:ps * 512 + tile_w],
                    AF.Identity, bias=bqk_sb[:, bias0 + n:bias0 + n + 1], scale=1.0))(ps, ks, n, tile_w),
                    reads=pBb(ps) + [b_bqk], writes=[b_kst[ks]])
                ev = K.op("sync", (lambda n, j, ks, tile_w, scr=scr: lambda h: h.dma_start(
                    scr[n, :, j * tile_w:(j + 1) * tile_w],
                    kst[:, ks * 512:ks * 512 + tile_w]))(n, j, ks, tile_w),
                    reads=[b_kst[ks]], dma=True)
                escr[n].append(ev)
    qk_proj("k")

    pair_groups = [[i, i + 1] for i in range(0, cfg.n_cores, 2)]
    e_kag = K.op(g, (lambda: lambda h: h.collective_compute(
        "AllGather", AluOpType.bypass, replica_groups=pair_groups,
        ins=[kag_i.ap().opt()], outs=[kag_o.ap().opt()]))(),
        waits=[e for n in range(NH) for e in e_kscr[n]])
    e_vag = K.op(g, (lambda: lambda h: h.collective_compute(
        "AllGather", AluOpType.bypass, replica_groups=pair_groups,
        ins=[vag_i.ap().opt()], outs=[vag_o.ap().opt()]))(),
        waits=e_vscr[0])
    qk_proj("q")

    K.barrier()
    nSA = (SH + 511) // 512
    NQS = T // 128
    import functools

    nstr = max(1, KC // 4)
    NV = NH * NQS
    SSW = (nstr + 2) * 128
    for n in range(NH):
        sl = n % 2
        K.op("sync", (lambda n, sl: lambda h: h.dma_start(
            arena[:, q2o + sl * T:q2o + (sl + 1) * T], qscr[n, :, :]))(n, sl),
            waits=e_qscr[n], writes=[b_q2[sl]], dma=True)
        for half in range(2):
            K.op("sync", (lambda n, sl, half: lambda h: h.dma_start(
                arena[:, k2o + sl * S + half * T:k2o + sl * S + (half + 1) * T],
                kag_o[half, n, :, :]))(n, sl, half),
                waits=[e_kag], writes=[b_k2[sl]], dma=True)
        for qi in range(NQS):
            pp = qi % 2
            psS = psA if pp == 0 else psB
            bps = ([b_psA[0], b_psA[1]] + b_pA2) if pp == 0 else ([b_psB[0], b_psB[1]] + b_pB2)
            hn = max(1, nstr // 2)
            def smp(h, n=n, qi=qi, sl=sl, psS=psS, hn=hn):
                k2v = arena[:, k2o + sl * S:k2o + (sl + 1) * S]
                qv = arena[:, q2o + sl * T + qi * 128:q2o + sl * T + (qi + 1) * 128]
                kstr = k2v.rearrange("p (a b) -> p a b", b=512)[:, :, 0:128]
                h.matmul(psS[:, 0:hn * 128], qv, kstr[:, 0:hn, :], start=True, stop=True)
                if nstr > hn:
                    h.matmul(psS[:, 512:512 + (nstr - hn) * 128], qv, kstr[:, hn:nstr, :],
                             start=True, stop=True)
                return h.matmul(psS[:, 1024:1024 + 256],
                                qv, k2v.rearrange("p (h b) -> p h b", h=2)
                                [:, :, qi * 128:(qi + 1) * 128],
                                start=True, stop=True)
            K.op("tensor", smp, reads=[b_k2[sl], b_q2[sl]], writes=bps)
            c0 = n * NQS + qi
            if nstr == 8:
                K.op("scalar", (lambda c0, psS: lambda h: h.activation(
                    junk[:, 0:1280], psS[:, 0:1280], AF.Exp, scale=SC / 4.0,
                    accum_out=accQ[:, c0:c0 + 1]))(c0, psS),
                    reads=bps, writes=[b_junk, b_accQ])
            else:
                K.op("scalar", (lambda c0, psS, hn: lambda h: h.activation(
                    junk[:, 0:hn * 128], psS[:, 0:hn * 128], AF.Exp, scale=SC / 4.0,
                    accum_out=accQ[:, NV + c0:NV + c0 + 1]))(c0, psS, hn),
                    reads=bps, writes=[b_junk, b_accQ])
                K.op("scalar", (lambda c0, psS: lambda h: h.activation(
                    junk[:, 0:256], psS[:, 1024:1280], AF.Exp, scale=SC / 4.0,
                    accum_out=accQ[:, 2 * NV + c0:2 * NV + c0 + 1]))(c0, psS),
                    reads=bps, writes=[b_junk, b_accQ])
                K.op("vector", (lambda c0: lambda h: h.tensor_tensor(
                    accQ[:, c0:c0 + 1], accQ[:, NV + c0:NV + c0 + 1],
                    accQ[:, 2 * NV + c0:2 * NV + c0 + 1], op=AluOpType.add))(c0),
                    reads=[b_accQ], writes=[b_accQ])
        K.op("vector", (lambda n: lambda h: h.tensor_scalar(
            accQI[:, NH * NQS + n * NQS:NH * NQS + (n + 1) * NQS],
            accQI[:, n * NQS:(n + 1) * NQS], 23, None,
            op0=AluOpType.arith_shift_right))(n),
            reads=[b_accQ], writes=[b_accQ])
        K.op("vector", (lambda n: lambda h: h.tensor_copy(
            accQ[:, 2 * NH * NQS + n * NQS:2 * NH * NQS + (n + 1) * NQS],
            accQI[:, NH * NQS + n * NQS:NH * NQS + (n + 1) * NQS]))(n),
            reads=[b_accQ], writes=[b_accQ])
        K.op("vector", (lambda n: lambda h: h.tensor_scalar(
            mnegs[:, n * NQS:(n + 1) * NQS],
            accQ[:, 2 * NH * NQS + n * NQS:2 * NH * NQS + (n + 1) * NQS],
            -127.0, -4.0 * 0.6931471805599453,
            op0=AluOpType.add, op1=AluOpType.mult))(n),
            reads=[b_accQ], writes=[b_mnegs])

    K.barrier()
    pend = []

    def flush_pending():
        for f in pend:
            f()
        pend.clear()

    q34 = []
    last_out = {}

    def p3p4_tile(qg):
        par = qg % 2
        t0 = qg * QW
        ops = []
        add = lambda f: ops.append([f])
        addb = lambda fs: ops.append(list(fs))

        def accum_stats(o):
            cur = o % 2
            add((lambda o, cur: lambda: K.op("vector", lambda h: h.tensor_tensor(
                sqst[:, cur * QW:(cur + 1) * QW], r1sl[:, o * QW:(o + 1) * QW],
                r1sl[:, o * QW:(o + 1) * QW], op=AluOpType.mult),
                reads=[b_r1n[o]], writes=[b_sq2[cur]]))(o, cur))
            if o == 0:
                add(lambda: K.op("vector", lambda h: h.tensor_copy(
                    ssm[:, 0:QW], r1sl[:, 0:QW]), reads=[b_r1n[0]], writes=[b_ssm[0]]))
                add(lambda: K.op("vector", lambda h: h.tensor_copy(
                    ssm[:, 2 * QW:3 * QW], sqst[:, 0:QW]), reads=[b_sq2[0]], writes=[b_ssm[2]]))
            else:
                prv = 1 - cur
                add((lambda o, cur, prv: lambda: K.op("vector", lambda h: h.tensor_tensor(
                    ssm[:, cur * QW:(cur + 1) * QW], ssm[:, prv * QW:(prv + 1) * QW],
                    r1sl[:, o * QW:(o + 1) * QW], op=AluOpType.add),
                    reads=[b_ssm[prv], b_r1n[o]], writes=[b_ssm[cur]]))(o, cur, prv))
                add((lambda cur, prv: lambda: K.op("vector", lambda h: h.tensor_tensor(
                    ssm[:, (2 + cur) * QW:(3 + cur) * QW],
                    ssm[:, (2 + prv) * QW:(3 + prv) * QW],
                    sqst[:, cur * QW:(cur + 1) * QW], op=AluOpType.add),
                    reads=[b_ssm[2 + prv], b_sq2[cur]], writes=[b_ssm[2 + cur]]))(cur, prv))

        def ln_half(to_ao):
            fin = (HC - 1) % 2
            mu = stf[:, 0:QW]
            rs = stf[:, QW:2 * QW]
            mus = stf[:, 2 * QW:3 * QW]
            scr2 = stf[:, 3 * QW:4 * QW]
            var2 = stf[:, 4 * QW:5 * QW]
            addb([
                lambda: K.op("tensor", lambda h: h.matmul(
                    psB[:, 1280:1280 + QW], ones_sb[:, 0:128], ssm[:, fin * QW:(fin + 1) * QW],
                    start=True, stop=True), reads=[b_ssm[fin], b_ones], writes=[b_pB2[1]]),
                lambda: K.op("tensor", lambda h: h.matmul(
                    psB[:, 1792:1792 + QW], ones_sb[:, 0:128],
                    ssm[:, (2 + fin) * QW:(3 + fin) * QW],
                    start=True, stop=True), reads=[b_ssm[2 + fin], b_ones], writes=[b_pB3[1]]),
                lambda: K.op("vector", lambda h: h.tensor_scalar(
                    mu, psB[:, 1280:1280 + QW], IH, None, op0=AluOpType.mult),
                    reads=[b_pB2[1]], writes=[b_stf[0]]),
                lambda: K.op("vector", lambda h: h.tensor_scalar(
                    scr2, psB[:, 1792:1792 + QW], IH, cfg.EPS,
                    op0=AluOpType.mult, op1=AluOpType.add),
                    reads=[b_pB3[1]], writes=[b_stf[3]]),
            ])
            add(lambda: K.op("vector", lambda h: h.tensor_tensor(
                rs, mu, mu, op=AluOpType.mult), reads=[b_stf[0]], writes=[b_stf[1]]))
            add(lambda: K.op("vector", lambda h: h.tensor_tensor(
                var2, scr2, rs, op=AluOpType.subtract),
                reads=[b_stf[1], b_stf[3]], writes=[b_stf[4]]))
            add(lambda: K.op("vector", lambda h: h.tensor_scalar(
                stfI[:, 3 * QW:4 * QW], stfI[:, 4 * QW:5 * QW], 1, None,
                op0=AluOpType.arith_shift_right),
                reads=[b_stf[4]], writes=[b_stf[3]]))
            add(lambda: K.op("vector", lambda h: h.tensor_tensor(
                stfI[:, QW:2 * QW], mgcI[:, 0:QW], stfI[:, 3 * QW:4 * QW],
                op=AluOpType.subtract), reads=[b_mgc, b_stf[3]], writes=[b_stf[1]]))
            for it in range(2):
                sa, sb = (1, 2) if it % 2 == 0 else (2, 1)
                add((lambda sa: lambda: K.op("vector", lambda h: h.tensor_tensor(
                    stf[:, 3 * QW:4 * QW], stf[:, sa * QW:(sa + 1) * QW],
                    stf[:, sa * QW:(sa + 1) * QW], op=AluOpType.mult),
                    reads=[b_stf[sa]], writes=[b_stf[3]]))(sa))
                add(lambda: K.op("vector", lambda h: h.tensor_tensor(
                    stf[:, 5 * QW:6 * QW], stf[:, 3 * QW:4 * QW], var2,
                    op=AluOpType.mult), reads=[b_stf[3], b_stf[4]], writes=[b_stf[5]]))
                add(lambda: K.op("vector", lambda h: h.tensor_scalar(
                    stf[:, 3 * QW:4 * QW], stf[:, 5 * QW:6 * QW], -0.5, 1.5,
                    op0=AluOpType.mult, op1=AluOpType.add),
                    reads=[b_stf[5]], writes=[b_stf[3]]))
                add((lambda sa, sb: lambda: K.op("vector", lambda h: h.tensor_tensor(
                    stf[:, sb * QW:(sb + 1) * QW], stf[:, sa * QW:(sa + 1) * QW],
                    stf[:, 3 * QW:4 * QW], op=AluOpType.mult),
                    reads=[b_stf[sa], b_stf[3]], writes=[b_stf[sb]]))(sa, sb))
            add(lambda: K.op("vector", lambda h: h.tensor_tensor(
                mus, mu, rs, op=AluOpType.mult), reads=[b_stf[0], b_stf[1]], writes=[b_stf[2]]))
            for o in range(HC):
                ys = o % 2
                add((lambda o: lambda: K.op("vector", lambda h: h.tensor_tensor(
                    ystage[:, 0:QW], r1sl[:, o * QW:(o + 1) * QW], rs,
                    op=AluOpType.mult), reads=[b_r1n[o], b_stf[1]], writes=[b_yst1]))(o))
                add((lambda o, ys: lambda: K.op("vector", lambda h: h.tensor_tensor(
                    ystage[:, (1 + ys) * QW:(2 + ys) * QW], ystage[:, 0:QW], mus,
                    op=AluOpType.subtract),
                    reads=[b_yst1, b_stf[2]], writes=[b_yst2[ys]]))(o, ys))
                if to_ao:
                    add((lambda o, ys: lambda: K.op("scalar", lambda h: h.activation(
                        aosl(par, o), ystage[:, (1 + ys) * QW:(2 + ys) * QW], AF.Copy),
                        reads=[b_yst2[ys]], writes=[b_AO2[par][o]]))(o, ys))
                else:
                    def mk_out(o=o, ys=ys):
                        def go():
                            ev = K.op("scalar", lambda h: h.dma_start(
                                outT[o * 128:(o + 1) * 128, t0:t0 + QW],
                                ystage[:, (1 + ys) * QW:(2 + ys) * QW]),
                                reads=[b_yst2[ys]], dma=True)
                            last_out[ev.sem] = ev
                        return go
                    add(mk_out())

        if debug:
            add(lambda: K.op(g, lambda h: h.dma_start(
                d_ao[qg, :, :], AO[:, par * HC * QW:(par + 1) * HC * QW]),
                reads=list(b_AO2[par]), dma=True))
        for o in range(HC):
            ws = o % 2
            po = 1280 if o % 2 == 0 else 1792
            pb = b_pA2[1] if o % 2 == 0 else b_pA3[1]
            add((lambda o, ws: lambda: K.op(g, lambda h: h.dma_start(
                arena[:, wo0 + ws * HC * 128:wo0 + (ws + 1) * HC * 128], wo_scr[o, :, :]),
                writes=[b_wo[ws]], dma=True))(o, ws))
            add((lambda o: lambda: K.op(g, lambda h: h.dma_start(
                x_res[:, (o % 2) * QW:(o % 2) * QW + QW],
                xT[o * 128:(o + 1) * 128, t0:t0 + QW]),
                writes=[b_x2[o % 2]], dma=True))(o))
            def mk_proj(o=o, ws=ws, po=po, pb=pb):
                def mms(h):
                    for c in range(HC):
                        mm = h.matmul(psA[:, po:po + QW],
                                      arena[:, wo0 + ws * HC * 128 + c * 128:
                                            wo0 + ws * HC * 128 + (c + 1) * 128],
                                      aosl(par, c), start=(c == 0), stop=(c == HC - 1))
                    return mm
                return lambda: K.op("tensor", mms, reads=[b_wo[ws]] + b_AO2[par], writes=[pb])
            bnd = [mk_proj()]
            if debug and o < 2:
                bnd.append((lambda o, po, pb: lambda: K.op("vector", lambda h: h.tensor_copy(
                    ystage[:, 0:QW], psA[:, po:po + QW]),
                    reads=[pb], writes=[b_yst1]))(o, po, pb))
                bnd.append((lambda o: lambda: K.op(g, lambda h: h.dma_start(
                    d_pj[qg, o, :, :], ystage[:, 0:QW]), reads=[b_yst1], dma=True))(o))
                bnd.append((lambda o: lambda: K.op(g, lambda h: h.dma_start(
                    d_xr[qg, o, :, :], x_res[:, (o % 2) * QW:(o % 2) * QW + QW]),
                    reads=[b_x2[o % 2]], dma=True))(o))
            bnd.append((lambda o, po, pb: lambda: K.op("vector", lambda h: h.scalar_tensor_tensor(
                r1sl[:, o * QW:(o + 1) * QW], psA[:, po:po + QW], bo_sb[:, o:o + 1],
                x_res[:, (o % 2) * QW:(o % 2) * QW + QW],
                op0=AluOpType.add, op1=AluOpType.add),
                reads=[pb, b_bo, b_x2[o % 2]], writes=[b_r1n[o]]))(o, po, pb))
            addb(bnd)
            accum_stats(o)
        if debug:
            add(lambda: K.op(g, lambda h: h.dma_start(
                d_r1[qg, :, :], r1sl[:, 0:HC * QW]), reads=list(b_r1n), dma=True))
        ln_half(True)
        if debug:
            add(lambda: K.op(g, lambda h: h.dma_start(
                d_y[qg, :, :], AO[:, par * HC * QW:(par + 1) * HC * QW]),
                reads=list(b_AO2[par]), dma=True))
            add(lambda: K.op(g, lambda h: h.dma_start(
                d_st[qg, 0, :, :], stf[:, 0:6 * QW]), reads=list(b_stf), dma=True))

        def mk_router():
            def mms(h):
                for c in range(HC):
                    mm = h.matmul(psA[0:8, 1280:1280 + QW], wr_sb[:, c * 8:(c + 1) * 8],
                                  aosl(par, c), start=(c == 0), stop=(c == HC - 1))
                return mm
            return lambda: K.op("tensor", mms, reads=[b_wr] + b_AO2[par], writes=[b_pA2[1]])
        addb([mk_router(),
              lambda: K.op("vector", lambda h: h.tensor_copy(
                  lgf[:, 0:QW], psA[0:8, 1280:1280 + QW]), reads=[b_pA2[1]], writes=[b_lgf2])])
        addb([(lambda tb: lambda: K.op("tensor", lambda h: h.transpose(
                  psA[:, 1792 + tb * 8:1792 + tb * 8 + 8],
                  lgf[0:8, tb * 128:(tb + 1) * 128], ident8f),
                  reads=[b_lgf2, b_cstf], writes=[b_pA3[1]]))(tb) for tb in range(GQ)] +
              [lambda: K.op("vector", lambda h: h.tensor_copy(
                  gtok[:, 0:GQ * 8], psA[:, 1792:1792 + GQ * 8]),
                  reads=[b_pA3[1]], writes=[b_gtok2])])
        seg = gtok[:, :].rearrange("p (s e) -> p s e", e=8)
        seg2 = gtk2[:, :].rearrange("p (s e) -> p s e", e=8)
        seg3 = gtk3[:, :].rearrange("p (s e) -> p s e", e=8)
        gb = lambda: gsc[:, :, None].broadcast_to((128, GQ, 8))
        gb2 = lambda: gsc2[:, :, None].broadcast_to((128, GQ, 8))
        add(lambda: K.op("vector", lambda h: h.tensor_reduce(
            gsc[:, :, None], seg, axis=mybir.AxisListType.X, op=AluOpType.max),
            reads=[b_gtok2], writes=[b_gsc_]))
        add(lambda: K.op("vector", lambda h: h.tensor_tensor(
            seg2, seg, gb(), op=AluOpType.subtract),
            reads=[b_gtok2, b_gsc_], writes=[b_gtk22]))
        add(lambda: K.op("scalar", lambda h: h.activation(gtok[:, :], gtk2[:, :], AF.Exp),
            reads=[b_gtk22], writes=[b_gtok2]))
        add(lambda: K.op("vector", lambda h: h.tensor_reduce(
            gsc[:, :, None], seg, axis=mybir.AxisListType.X, op=AluOpType.add),
            reads=[b_gtok2], writes=[b_gsc_]))
        add(lambda: K.op("vector", lambda h: h.reciprocal(gsc2[:, :], gsc[:, :]),
            reads=[b_gsc_], writes=[b_gsc2_]))
        add(lambda: K.op("vector", lambda h: h.tensor_tensor(
            seg2, seg, gb2(), op=AluOpType.mult),
            reads=[b_gsc2_, b_gtok2], writes=[b_gtk22]))
        add(lambda: K.op("vector", lambda h: h.tensor_reduce(
            gsc[:, :, None], seg2, axis=mybir.AxisListType.X, op=AluOpType.max),
            reads=[b_gtk22], writes=[b_gsc_]))
        add(lambda: K.op("vector", lambda h: h.tensor_tensor(
            seg, seg2, gb(), op=AluOpType.is_ge),
            reads=[b_gtk22, b_gsc_], writes=[b_gtok2]))
        add(lambda: K.op("vector", lambda h: h.scalar_tensor_tensor(
            gtk3[:, :], gtok[:, :], -2.0, gtk2[:, :],
            op0=AluOpType.mult, op1=AluOpType.add),
            reads=[b_gtk22, b_gtok2], writes=[b_gtk32]))
        add(lambda: K.op("vector", lambda h: h.tensor_reduce(
            gsc[:, :, None], seg3, axis=mybir.AxisListType.X, op=AluOpType.max),
            reads=[b_gtk32], writes=[b_gsc_]))
        add(lambda: K.op("vector", lambda h: h.tensor_tensor(
            seg, seg2, gb(), op=AluOpType.is_ge),
            reads=[b_gtk22, b_gsc_], writes=[b_gtok2]))
        add(lambda: K.op("vector", lambda h: h.tensor_tensor(
            seg3, seg, seg2, op=AluOpType.mult),
            reads=[b_gtk22, b_gtok2], writes=[b_gtk32]))
        addb([(lambda tb: lambda: K.op("tensor", lambda h: h.transpose(
                  psA[0:8, 1280 + tb * 128:1280 + (tb + 1) * 128],
                  gtk3[:, tb * 8:(tb + 1) * 8], identf),
                  reads=[b_gtk32, b_cstf], writes=[b_pA2[1]]))(tb) for tb in range(GQ)] +
              [lambda: K.op("vector", lambda h: h.tensor_copy(
                  gT_sb[:, 0:QW], psA[0:8, 1280:1280 + QW]),
                  reads=[b_pA2[1]], writes=[b_gT2])])
        def mk_down():
            def mms(h):
                for c in range(HC):
                    mm = h.matmul(psA[:, 1280:1280 + QW],
                                  wd_sb[:, c * EI:(c + 1) * EI][:, 0:128],
                                  aosl(par, c), start=(c == 0), stop=(c == HC - 1))
                return mm
            return lambda: K.op("tensor", mms, reads=[b_wd] + b_AO2[par], writes=[b_pA2[1]])
        addb([
            lambda: K.op("tensor", lambda h: h.matmul(
                psA[:, 1792:1792 + QW], sel_sb[:, :], gT_sb[0:8, 0:QW], start=True, stop=True),
                reads=[b_sel, b_gT2], writes=[b_pA3[1]]),
            mk_down(),
            lambda: K.op("scalar", lambda h: h.activation(
                ystage[:, 0:QW], psA[:, 1280:1280 + QW], AF.Exp, scale=-1.0),
                reads=[b_pA2[1]], writes=[b_yst1]),
            lambda: K.op("vector", lambda h: h.tensor_scalar(
                ystage[:, QW:2 * QW], ystage[:, 0:QW], 1.0, None, op0=AluOpType.add),
                reads=[b_yst1], writes=[b_yst2[0]]),
            lambda: K.op("vector", lambda h: h.reciprocal(
                ystage[:, 2 * QW:3 * QW], ystage[:, QW:2 * QW]),
                reads=[b_yst2[0]], writes=[b_yst2[1]]),
            lambda: K.op("vector", lambda h: h.tensor_tensor(
                agst[:, 0:QW], psA[:, 1280:1280 + QW], ystage[:, 2 * QW:3 * QW],
                op=AluOpType.mult), reads=[b_pA2[1], b_yst2[1]], writes=[b_ag2[0]]),
            lambda: K.op("vector", lambda h: h.tensor_tensor(
                agst[:, QW:2 * QW], agst[:, 0:QW], psA[:, 1792:1792 + QW],
                op=AluOpType.mult), reads=[b_ag2[0], b_pA3[1]], writes=[b_ag2[1]]),
        ])
        for o in range(HC):
            po = 1280 if o % 2 == 0 else 1792
            pb = b_pA2[1] if o % 2 == 0 else b_pA3[1]
            addb([(lambda o, po, pb: lambda: K.op("tensor", lambda h: h.matmul(
                      psA[:, po:po + QW], wu_sb[:, o * 128:(o + 1) * 128],
                      agst[:, QW:2 * QW], start=True, stop=True),
                      reads=[b_wu, b_ag2[1]], writes=[pb]))(o, po, pb),
                  (lambda o, po, pb: lambda: K.op("vector", lambda h: h.tensor_tensor(
                      r1sl[:, o * QW:(o + 1) * QW], psA[:, po:po + QW], aosl(par, o),
                      op=AluOpType.add), reads=[pb, b_AO2[par][o]],
                      writes=[b_r1n[o]]))(o, po, pb)])
            accum_stats(o)
        if debug:
            add(lambda: K.op("sync", lambda h: h.dma_start(
                d_r2[qg, :, :], r1sl[:, 0:HC * QW]), reads=list(b_r1n), dma=True))
        ln_half(False)
        if debug:
            add(lambda: K.op(g, lambda h: h.dma_start(
                d_st[qg, 1, :, :], stf[:, 0:6 * QW]), reads=list(b_stf), dma=True))
        return ops

    for qg in range(NGQ):
        for n in range(NH):
            sl = n % 2
            K.op(g, (lambda n, sl, qg: lambda h: h.dma_start(
                arena[:, q2o + sl * QW:q2o + (sl + 1) * QW],
                qscr[n, :, qg * QW:(qg + 1) * QW]))(n, sl, qg),
                waits=e_qscr[n], writes=[b_q2s[sl]], dma=True)
            for half in range(2):
                K.op(g, (lambda n, sl, half: lambda h: h.dma_start(
                    arena[:, k2o + sl * S + half * T:k2o + sl * S + (half + 1) * T],
                    kag_o[half, n, :, :]))(n, sl, half),
                    waits=[e_kag], writes=[b_k2[sl]], dma=True)
                K.op(g, (lambda n, sl, half: lambda h: h.dma_start(
                    arena[:, v2o + sl * S + half * T:v2o + sl * S + (half + 1) * T]
                    .rearrange("p (tb d) -> p tb d", d=128),
                    vag_o[half, :, n, :, :].transpose([1, 0, 2])))(n, sl, half),
                    waits=[e_vag], writes=[b_v2s[sl]], dma=True)
            for qs in range(GQ):
                qi = qg * GQ + qs
                p = (qs * 2 + n % 2) % NSL
                def smm(h, qs=qs, sl=sl, half=0):
                    for m in range(nSA):
                        w = min(512, SH - m * 512)
                        ps = psA if half == 0 else psB
                        mm = h.matmul(ps[:, m * 512:m * 512 + w],
                                      arena[:, q2o + sl * QW + qs * 128:
                                            q2o + sl * QW + (qs + 1) * 128],
                                      arena[:, k2o + sl * S + half * SH + m * 512:
                                            k2o + sl * S + half * SH + m * 512 + w],
                                      start=True, stop=True)
                    return mm
                K.op("tensor", functools.partial(smm, half=0),
                     reads=[b_k2[sl], b_q2s[sl]], writes=psA_all)
                K.op("tensor", functools.partial(smm, half=1),
                     reads=[b_k2[sl], b_q2s[sl]], writes=psB_all)
                gp = n % 2
                K.op("scalar", (lambda n, qi, p, gp, qs: lambda h: h.activation(
                    P2s[:, p * S:p * S + SH], psA[:, 0:SH], AF.Exp, scale=SC,
                    bias=mnegs[:, n * NQS + qi:n * NQS + qi + 1],
                    accum_out=lacc[:, gp * 4 + 2 * qs:gp * 4 + 2 * qs + 1]))(n, qi, p, gp, qs),
                    reads=psA_all + [b_mnegs], writes=[b_P2s[p][0], b_lacc[gp]])
                K.op("scalar", (lambda n, qi, p, gp, qs: lambda h: h.activation(
                    P2s[:, p * S + SH:p * S + 2 * SH], psB[:, 0:SH], AF.Exp, scale=SC,
                    bias=mnegs[:, n * NQS + qi:n * NQS + qi + 1],
                    accum_out=lacc[:, gp * 4 + 2 * qs + 1:gp * 4 + 2 * qs + 2]))(n, qi, p, gp, qs),
                    reads=psB_all + [b_mnegs], writes=[b_P2s[p][1], b_lacc[gp]])
                flush_pending()
                K.op("sync", (lambda p, qs, sl: lambda h: h.dma_start_transpose(
                    PTs(sl)[:, :].rearrange("p (j q) -> p j q", q=QW)
                    [:, :, qs * 128:(qs + 1) * 128],
                    P2s[:, p * S:(p + 1) * S]))(p, qs, sl),
                    reads=[b_P2s[p][0], b_P2s[p][1]], writes=[b_PTq[sl][qs]],
                    dma=True, dma_sem=("xd" if n % 2 == 0 else "yd3"))
                if qs == GQ - 1:
                    def def_G(n=n, qg=qg, sl=sl, gp=n % 2):
                        def pv(h):
                            for kt in range(KC):
                                mm = h.matmul(psA[:, 1024:1024 + QW],
                                              arena[:, v2o + sl * S + kt * 128:
                                                    v2o + sl * S + (kt + 1) * 128],
                                              PTs(sl)[:, kt * QW:(kt + 1) * QW],
                                              start=(kt == 0), stop=(kt == KC - 1))
                            return mm
                        K.op("tensor", pv, reads=[b_v2s[sl]] + b_PTq[sl], writes=[b_pA2[0]])
                        K.op("vector", (lambda gp: lambda h: h.tensor_copy(
                            pvst[:, gp * QW:(gp + 1) * QW], psA[:, 1024:1024 + QW]))(gp),
                            reads=[b_pA2[0]], writes=[b_pvs[gp]])
                        K.op("vector", (lambda gp: lambda h: h.tensor_tensor(
                            lcol2[:, 0:GQ],
                            lacc[:, 0:8].rearrange("p (g s h) -> p g s h", g=2, h=2)[:, gp, 0:GQ, 0],
                            lacc[:, 0:8].rearrange("p (g s h) -> p g s h", g=2, h=2)[:, gp, 0:GQ, 1],
                            op=AluOpType.add))(gp),
                            reads=[b_lacc[gp]], writes=[b_lc2])
                        K.op("vector", lambda h: h.reciprocal(lcol2i[:, 0:GQ], lcol2[:, 0:GQ]),
                             reads=[b_lc2], writes=[b_lc2i])
                        def ltr(h):
                            for q_ in range(GQ):
                                tr = h.transpose(psA[0:1, 1536 + q_ * 128:1536 + (q_ + 1) * 128],
                                                 lcol2i[:, q_:q_ + 1], identf)
                            return tr
                        K.op("tensor", ltr, reads=[b_lc2i, b_cstf], writes=[b_pA3[0]])
                        K.op("vector", lambda h: h.tensor_copy(
                            lirow[0:1, 0:QW], psA[0:1, 1536:1536 + QW]),
                            reads=[b_pA3[0]], writes=[b_lirow])
                        def lrepmm(h):
                            for q_ in range(GQ):
                                mm = h.matmul(psA[:, 1536 + q_ * 128:1536 + (q_ + 1) * 128],
                                              onesf_sb[0:1, 0:128],
                                              lirow[0:1, q_ * 128:(q_ + 1) * 128],
                                              start=True, stop=True)
                            return mm
                        K.op("tensor", lrepmm, reads=[b_onesf, b_lirow], writes=[b_pA3[0]])
                        K.op("vector", lambda h: h.tensor_copy(
                            lrep_sb[:, 0:QW], psA[:, 1536:1536 + QW]),
                            reads=[b_pA3[0]], writes=[b_lrep])
                        K.op("vector", (lambda n, qg, gp: lambda h: h.tensor_tensor(
                            aosl(qg % 2, n), pvst[:, gp * QW:(gp + 1) * QW],
                            lrep_sb[:, 0:QW],
                            op=AluOpType.mult))(n, qg, gp),
                            reads=[b_pvs[gp], b_lrep], writes=[b_AO2[qg % 2][n]])
                    pend.append(def_G)
                if q34:
                    rem = NH * GQ - (n * GQ + qs)
                    for _ in range(min(len(q34), -(-len(q34) // rem))):
                        for f in q34.pop(0):
                            f()
        q34.extend(p3p4_tile(qg))
    flush_pending()
    for bundle in q34:
        for f in bundle:
            f()
    q34.clear()

    outs = list(last_out.values())
    K.op("sync", lambda h: None, waits=outs)
    K.mark(*outs)

    K.finalize(nc, block, sems)
    es.close()
    return nc


def prep_inputs(cfg, x_bh, w_qkv, b_qkv, w_out, b_out, router_w, down_w, up_w, own_first=True):
    H, NH, HC, S, VD, NVD, EI = cfg.H, cfg.NH, cfg.HC, cfg.S, cfg.VD, cfg.NVD, cfg.EI
    f32 = np.float32
    d = {}
    d["xT"] = np.ascontiguousarray(x_bh.T.astype(f32))
    Wq, Wk, Wv = w_qkv[0:H], w_qkv[H:2 * H], w_qkv[2 * H:3 * H]
    d["wq"] = np.ascontiguousarray(
        Wq.reshape(NH, 128, H).transpose(0, 2, 1))
    d["wk"] = np.ascontiguousarray(Wk.reshape(NH, 128, H).transpose(0, 2, 1))
    d["wv"] = np.ascontiguousarray(
        Wv.T.reshape(H, NVD, VD).transpose(1, 0, 2))
    d["wo"] = np.ascontiguousarray(w_out.reshape(HC, 128, H).transpose(0, 2, 1))
    d["bqk"] = np.ascontiguousarray(
        np.concatenate([b_qkv[0:H].reshape(NH, 128).T, b_qkv[H:2 * H].reshape(NH, 128).T], axis=1)).astype(f32)
    d["bo"] = np.ascontiguousarray(b_out.reshape(HC, 128).T).astype(f32)
    d["bv"] = b_qkv[2 * H:3 * H].reshape(1, H).astype(f32)
    d["wr"] = np.ascontiguousarray(
        router_w.T.reshape(HC, 128, 8).transpose(1, 0, 2).reshape(128, HC * 8)).astype(f32)
    Dcat = down_w.reshape(EI, H)
    d["wd"] = np.ascontiguousarray(
        Dcat.T.reshape(HC, 128, EI).transpose(1, 0, 2).reshape(128, HC * EI)).astype(f32)
    Ucat = up_w.transpose(1, 0, 2).reshape(H, EI)
    d["wu"] = np.ascontiguousarray(Ucat.T).astype(f32)
    cst = np.zeros((128, 256), dtype=BF)
    cst[:, 0:128] = np.eye(128, dtype=BF)
    sel = np.zeros((8, 128), dtype=BF)
    for e in range(8):
        sel[e, e * 16:(e + 1) * 16] = 1
    cst[0:8, 128:256] = sel
    d["cst"] = cst
    d["cstf"] = np.eye(128, dtype=f32)
    return d


from concourse.bass_utils import run_bass_kernel_spmd


def kernel(L0_attn_hidden_states, L0_attn_w_qkv, L0_attn_b_qkv, L0_attn_w_out,
           L0_attn_b_out, L0_ffn_router_w, L0_ffn_down_w, L0_ffn_up_w):
    x = np.asarray(L0_attn_hidden_states)
    w_qkv = np.asarray(L0_attn_w_qkv)
    b_qkv = np.asarray(L0_attn_b_qkv)
    w_out = np.asarray(L0_attn_w_out)
    b_out = np.asarray(L0_attn_b_out)
    router_w = np.asarray(L0_ffn_router_w)
    down_w = np.asarray(L0_ffn_down_w)
    up_w = np.asarray(L0_ffn_up_w)

    B, Sfull, H = x.shape
    cfg = Cfg(NH=16, S=4096, T=2048, QT=512, n_cores=8)
    nc = build(cfg)

    wdict = prep_inputs(cfg, np.zeros((cfg.T, H), np.float32), w_qkv, b_qkv,
                        w_out, b_out, router_w, down_w, up_w)
    del wdict["xT"]

    in_maps = []
    for c in range(8):
        b, hf = c // 2, c % 2
        m = dict(wdict)
        m["xT"] = np.ascontiguousarray(x[b, hf * cfg.T:(hf + 1) * cfg.T, :].T.astype(np.float32))
        in_maps.append(m)

    res = run_bass_kernel_spmd(nc, in_maps, core_ids=list(range(8)))

    out = np.empty((B, Sfull, H), dtype=np.float32)
    for c in range(8):
        b, hf = c // 2, c % 2
        out[b, hf * cfg.T:(hf + 1) * cfg.T, :] = res.results[c]["outT"].T
    return out


# version 30 (final)
# speedup vs baseline: 1.1107x; 1.0026x over previous
import math
import numpy as np
import ml_dtypes
from contextlib import ExitStack

import concourse.bass as bass
import concourse.mybir as mybir
from concourse.alu_op_type import AluOpType

F32 = mybir.dt.float32
BF16 = mybir.dt.bfloat16
AF = mybir.ActivationFunctionType
BF = ml_dtypes.bfloat16
ENGINES = ("tensor", "vector", "scalar", "gpsimd", "sync")
DMA_SEMS = {"vector": ["vd0"], "scalar": ["sd0", "sd1"],
            "gpsimd": ["gd0", "gd1", "gd2", "gd3"],
            "sync": ["yd0", "yd1"]}
ALL_SEMS = ["pe", "ve", "ac", "gp", "sy", "xd", "yd2", "yd3"] + [s for l in DMA_SEMS.values() for s in l]
ENG_SEM = {"tensor": "pe", "vector": "ve", "scalar": "ac", "gpsimd": "gp", "sync": "sy"}


class Ev:
    __slots__ = ("eng", "sem", "idx", "by", "used", "val")

    def __init__(self, eng, sem, idx, by):
        self.eng, self.sem, self.idx, self.by = eng, sem, idx, by
        self.used = False
        self.val = None


class Buf:
    __slots__ = ("w", "r", "name")

    def __init__(self, name=""):
        self.w = None
        self.r = []
        self.name = name


class Sched:
    def __init__(self):
        self.recs = {e: [] for e in ENGINES}
        self.dma_rot = {e: 0 for e in ENGINES}
        self.last_dma = {}
        self.tail = {e: None for e in ENGINES}
        self.pending = {e: [] for e in ENGINES}

    def barrier(self):
        evs = [e for e in self.tail.values() if e is not None]
        evs += [e for e in self.last_dma.values() if e is not None]
        for eng in ENGINES:
            self.pending[eng] = list(evs)

    def op(self, eng, fn, reads=(), writes=(), waits=(), dma=False, inc_by=None, dma_sem=None):
        if dma:
            if dma_sem is not None:
                sem = dma_sem
            else:
                sl = DMA_SEMS[eng]
                sem = sl[self.dma_rot[eng] % len(sl)]
                self.dma_rot[eng] += 1
        else:
            sem = ENG_SEM[eng]
        by = inc_by if inc_by is not None else (16 if dma else 1)
        ev = Ev(eng, sem, len(self.recs[eng]), by)
        wl = []
        if dma:
            ev.used = True
            prev = self.last_dma.get(sem)
            if prev is not None:
                wl.append(prev)
            self.last_dma[sem] = ev
        for b in reads:
            if b.w is not None:
                wl.append(b.w)
        for b in writes:
            if b.w is not None:
                wl.append(b.w)
            wl.extend(b.r)
        for e in waits:
            wl.append(e)
        seen, wl2 = set(), []
        for e in wl:
            if e is None or id(e) in seen:
                continue
            seen.add(id(e))
            wl2.append(e)
        if self.pending[eng]:
            for e in self.pending[eng]:
                if id(e) not in seen and e is not ev:
                    seen.add(id(e))
                    wl2.append(e)
            self.pending[eng] = []
        self.recs[eng].append((fn, wl2, ev))
        self.tail[eng] = ev
        for b in reads:
            b.r.append(ev)
        for b in writes:
            b.w = ev
            b.r = []
        return ev

    def mark(self, *evs):
        for e in evs:
            if e is not None:
                e.used = True

    def finalize(self, nc, block, sems):
        for eng in ENGINES:
            cnt = {}
            for fn, wl, ev in self.recs[eng]:
                for e in wl:
                    e.used = True
        for eng in ENGINES:
            cnt = {}
            for fn, wl, ev in self.recs[eng]:
                if ev.used:
                    cnt[ev.sem] = cnt.get(ev.sem, 0) + ev.by
                    ev.val = cnt[ev.sem]
        def make(eng):
            def emit(h):
                hw = {}
                for fn, wl, ev in self.recs[eng]:
                    for e in wl:
                        if e.val is None:
                            continue
                        key = e.sem
                        if hw.get(key, 0) >= e.val:
                            continue
                        h.wait_ge(sems[e.sem], e.val)
                        hw[key] = e.val
                    inst = fn(h)
                    if ev.used and ev.val is not None:
                        assert inst is not None, f"{eng} op needs inc but returned None"
                        inst.then_inc(sems[ev.sem], ev.by)
            return emit
        for eng in ENGINES:
            if not self.recs[eng]:
                continue
            getattr(block, eng)(make(eng))


class Cfg:
    def __init__(self, NH=16, S=4096, T=2048, QT=512, n_cores=8):
        self.NH = NH
        self.HC = NH
        self.H = NH * 128
        self.S = S
        self.T = T
        self.QT = QT
        self.KC = S // 128
        self.NQT = T // QT
        self.E, self.I = 8, 16
        self.EI = 128
        self.SCALE = 1.0 / math.sqrt(128.0)
        self.VD = min(512, self.H)
        self.NVD = self.H // self.VD
        self.TB = S // 128
        self.TBo = T // 128
        self.n_cores = n_cores
        self.EPS = 1e-5
        self.TW = min(2 * QT, T)
        self.NTW = T // self.TW
        self.JW = min(512, self.TW)
        self.JT = self.TW // self.JW


def build(cfg: Cfg, debug=False):
    nc = bass.Bass("TRN2", target_bir_lowering=False)
    K = Sched()
    H, S, T, QT, NH, HC, KC = cfg.H, cfg.S, cfg.T, cfg.QT, cfg.NH, cfg.HC, cfg.KC
    NQT, EI, VD, NVD, TB = cfg.NQT, cfg.EI, cfg.VD, cfg.NVD, cfg.TB
    TW, NTW, JW, JT = cfg.TW, cfg.NTW, cfg.JW, cfg.JT
    SC = cfg.SCALE
    IH = 1.0 / H

    xT = nc.dram_tensor("xT", [H, T], F32, kind="ExternalInput")
    wq_h = nc.dram_tensor("wq", [NH, HC * 128, 128], F32, kind="ExternalInput")
    wk_h = nc.dram_tensor("wk", [NH, HC * 128, 128], F32, kind="ExternalInput")
    wv_h = nc.dram_tensor("wv", [NVD, HC * 128, VD], F32, kind="ExternalInput")
    wo_h = nc.dram_tensor("wo", [HC, HC * 128, 128], F32, kind="ExternalInput")
    bqk_h = nc.dram_tensor("bqk", [128, 2 * NH], F32, kind="ExternalInput")
    bo_h = nc.dram_tensor("bo", [128, HC], F32, kind="ExternalInput")
    bv_h = nc.dram_tensor("bv", [1, H], F32, kind="ExternalInput")
    wr_h = nc.dram_tensor("wr", [128, HC * 8], F32, kind="ExternalInput")
    wd_h = nc.dram_tensor("wd", [128, HC * EI], F32, kind="ExternalInput")
    wu_h = nc.dram_tensor("wu", [EI, H], F32, kind="ExternalInput")
    cst_h = nc.dram_tensor("cst", [128, 256], BF16, kind="ExternalInput")
    cstf_h = nc.dram_tensor("cstf", [128, 128], F32, kind="ExternalInput")
    outT = nc.dram_tensor("outT", [H, T], F32, kind="ExternalOutput")

    _scr_kind = "Internal"
    TBo = cfg.TBo
    qscr = nc.dram_tensor("qscr", [NH, 128, T], BF16, kind=_scr_kind)
    kag_i = nc.dram_tensor("kag_i", [NH, 128, T], BF16)
    kag_o = nc.dram_tensor("kag_o", [2, NH, 128, T], BF16, kind=_scr_kind)
    vag_i = nc.dram_tensor("vag_i", [TBo, NH, 128, 128], BF16)
    vag_o = nc.dram_tensor("vag_o", [2, TBo, NH, 128, 128], BF16, kind=_scr_kind)
    wo_scr = nc.dram_tensor("wo_scr", [HC, 128, HC * 128], BF16, kind=_scr_kind)
    NGQ_ = T // (min(2, T // 128) * 128)
    d_ao = nc.dram_tensor("d_ao", [NGQ_, 128, HC * (T // NGQ_)], BF16, kind="ExternalOutput") if debug else None
    d_r1 = nc.dram_tensor("d_r1", [NGQ_, 128, HC * (T // NGQ_)], BF16, kind="ExternalOutput") if debug else None
    d_y = nc.dram_tensor("d_y", [NGQ_, 128, HC * (T // NGQ_)], BF16, kind="ExternalOutput") if debug else None
    d_r2 = nc.dram_tensor("d_r2", [NGQ_, 128, HC * (T // NGQ_)], BF16, kind="ExternalOutput") if debug else None
    d_st = nc.dram_tensor("d_st", [NGQ_, 2, 128, 6 * (T // NGQ_)], F32, kind="ExternalOutput") if debug else None
    d_pj = nc.dram_tensor("d_pj", [NGQ_, 2, 128, T // NGQ_], F32, kind="ExternalOutput") if debug else None
    d_xr = nc.dram_tensor("d_xr", [NGQ_, 2, 128, T // NGQ_], F32, kind="ExternalOutput") if debug else None

    es = ExitStack()
    sbt = lambda name, shape, dt: es.enter_context(nc.sbuf_tensor(name, shape, dt))

    cst = sbt("cst_sb", [128, 256], BF16)
    cstf = sbt("cstf_sb", [128, 128], F32)
    sel_sb = sbt("sel_sb", [8, 128], BF16)
    ones_sb = sbt("ones_sb", [128, 128], BF16)
    onesf_sb = sbt("onesf_sb", [1, 128], F32)
    bqk_sb = sbt("bqk_sb", [128, 2 * NH], F32)
    bo_sb = sbt("bo_sb", [128, HC], F32)
    bv_sb = sbt("bv_sb", [1, H], BF16)
    wr_sb = sbt("wr_sb", [128, HC * 8], BF16)
    wd_sb = sbt("wd_sb", [128, HC * EI], BF16)
    wu_sb = sbt("wu_sb", [128, H], BF16)

    NTB = T // 128
    def _al(x):
        return (x + 63) // 64 * 64
    A_XT = 0
    A_wv = _al(A_XT + HC * T)
    A_wqk = _al(A_wv + HC * VD)
    A_kst = _al(A_wqk + 2 * HC * 128)
    A_vst = _al(A_kst + 2 * 512)
    p1_end = _al(A_vst + 2 * 512)
    GQ = min(2, T // 128)
    QW = GQ * 128
    NGQ = T // QW
    SH = S // 2
    NQS_ = T // 128
    A_q2 = 0
    A_k2 = _al(A_q2 + 2 * T)
    A_v2 = _al(A_k2 + 2 * S)
    A_PT = _al(A_v2 + 2 * S)
    NSL = 4
    A_P2s = _al(A_PT + 2 * KC * QW)
    A_junk = _al(A_P2s + NSL * S)
    A_accQ = _al(A_junk + 1280)
    A_mnegs = _al(A_accQ + 2 * 4 * NH * NQS_)
    A_lrep = _al(A_mnegs + 2 * NH * NQS_)
    A_lac = _al(A_lrep + 2 * QW)
    A_lrw = _al(A_lac + 32 + 8)
    A_pvs = _al(A_lrw + 2 * QW)
    A_AO = _al(A_pvs + 2 * 2 * QW)
    A_r1 = _al(A_AO + 2 * HC * QW)
    A_xres = _al(A_r1 + HC * QW)
    A_yst = _al(A_xres + 2 * 2 * QW)
    A_wo = _al(A_yst + 3 * 2 * QW)
    A_ag = _al(A_wo + 2 * HC * 128)
    A_sq2 = _al(A_ag + 2 * QW)
    A_stf = _al(A_sq2 + 2 * QW)
    A_mgc = _al(A_stf + 6 * 2 * QW)
    A_ssm = _al(A_mgc + 2 * QW)
    p2_end = _al(A_ssm + 4 * QW)
    AR_COLS = _al(max(p2_end, p1_end) + 64)
    arena = sbt("arena", [128, AR_COLS], BF16)
    assert AR_COLS * 2 <= 190 * 1024, f"arena {AR_COLS * 2} B/partition too big"

    XT = arena[:, A_XT:A_XT + HC * T]
    wv_sb = arena[:, A_wv:A_wv + HC * VD]
    wqk0 = A_wqk
    kst = arena[:, A_kst:A_kst + 2 * 512]
    vst = arena[:, A_vst:A_vst + 2 * 512]
    q2o, k2o, v2o = A_q2, A_k2, A_v2
    PTs = lambda sb_: arena[:, A_PT + sb_ * KC * QW:A_PT + (sb_ + 1) * KC * QW]
    P2s = arena[:, A_P2s:A_P2s + NSL * S]
    junk = arena[:, A_junk:A_junk + 1280]
    accQ = arena[:, A_accQ:A_accQ + 2 * 4 * NH * NQS_].bitcast(F32)
    accQI = arena[:, A_accQ:A_accQ + 2 * 4 * NH * NQS_].bitcast(mybir.dt.int32)
    mnegs = arena[:, A_mnegs:A_mnegs + 2 * NH * NQS_].bitcast(F32)
    lrep_sb = arena[:, A_lrep:A_lrep + 2 * QW].bitcast(F32)
    lacc = arena[:, A_lac:A_lac + 32].bitcast(F32)
    lcol2 = arena[:, A_lac + 32:A_lac + 36].bitcast(F32)
    lcol2i = arena[:, A_lac + 36:A_lac + 40].bitcast(F32)
    lirow = arena[0:1, A_lrw:A_lrw + 2 * QW].bitcast(F32)
    pvst = arena[:, A_pvs:A_pvs + 2 * 2 * QW].bitcast(F32)
    AO = arena[:, A_AO:A_AO + 2 * HC * QW]
    r1sl = arena[:, A_r1:A_r1 + HC * QW]
    x_res = arena[:, A_xres:A_xres + 2 * 2 * QW].bitcast(F32)
    ystage = arena[:, A_yst:A_yst + 3 * 2 * QW].bitcast(F32)
    wo0 = A_wo
    agst = arena[:, A_ag:A_ag + 2 * QW]
    sqst = arena[:, A_sq2:A_sq2 + 2 * QW]
    stf = arena[:, A_stf:A_stf + 6 * 2 * QW].bitcast(F32)
    stfI = arena[:, A_stf:A_stf + 6 * 2 * QW].bitcast(mybir.dt.int32)
    mgcI = arena[:, A_mgc:A_mgc + 2 * QW].bitcast(mybir.dt.int32)
    mgcF = arena[:, A_mgc:A_mgc + 2 * QW].bitcast(F32)
    ssm = arena[:, A_ssm:A_ssm + 4 * QW]
    lgf = arena[0:8, A_junk:A_junk + 2 * QW].bitcast(F32)
    gT_sb = arena[0:8, A_junk + 2 * QW:A_junk + 3 * QW]
    g0 = A_junk + 3 * QW
    gtok = arena[:, g0:g0 + 2 * GQ * 8].bitcast(F32)
    gtk2 = arena[:, g0 + 2 * GQ * 8:g0 + 4 * GQ * 8].bitcast(F32)
    gtk3 = arena[:, g0 + 4 * GQ * 8:g0 + 6 * GQ * 8].bitcast(F32)
    gsc = arena[:, g0 + 6 * GQ * 8:g0 + 6 * GQ * 8 + 2 * GQ].bitcast(F32)
    gsc2 = arena[:, g0 + 6 * GQ * 8 + 2 * GQ:g0 + 6 * GQ * 8 + 4 * GQ].bitcast(F32)
    assert 3 * QW + 6 * GQ * 8 + 4 * GQ <= 1280

    psA = es.enter_context(nc.psum_tensor("psA", [128, 2048], F32))
    psB = es.enter_context(nc.psum_tensor("psB", [128, 2048], F32))

    sems = {}
    for s in ALL_SEMS:
        sems[s] = es.enter_context(nc.semaphore("sem_" + s))
    block = es.enter_context(nc.Block())

    b_cst, b_cstf, b_sel, b_ones = Buf("cst"), Buf("cstf"), Buf("sel"), Buf("ones")
    b_onesf = Buf("onesf")
    b_bqk, b_bo, b_bv = Buf("bqk"), Buf("bo"), Buf("bv")
    b_wr, b_wd, b_wu = Buf("wr"), Buf("wd"), Buf("wu")
    b_XT = Buf("XT")
    b_wv = Buf("wv")
    b_wqk = [Buf("wqk%d" % i) for i in range(2)]
    b_kst = [Buf("kst%d" % i) for i in range(2)]
    b_vst = [Buf("vst%d" % i) for i in range(2)]
    b_psA = [Buf("psA%d" % i) for i in range(4)]
    b_psB = [Buf("psB%d" % i) for i in range(4)]
    b_pA2 = [b_psA[2], b_psA[2]]
    b_pA3 = [b_psA[3], b_psA[3]]
    b_pB2 = [b_psB[2], b_psB[2]]
    b_pB3 = [b_psB[3], b_psB[3]]
    psA_all = b_psA
    psB_all = b_psB

    def pAb(slot):
        return [b_psA[slot]]

    def pBb(slot):
        return [b_psB[slot]]
    e_kscr = [[] for _ in range(NH)]
    e_qscr = [[] for _ in range(NH)]
    e_vscr = [[] for _ in range(NH)]
    b_k2 = [Buf("k2_%d" % i) for i in range(2)]
    b_q2 = [Buf("q2_%d" % i) for i in range(2)]
    b_q2s = [Buf("q2s%d" % i) for i in range(2)]
    b_v2s = [Buf("v2s%d" % i) for i in range(2)]
    b_P2s = [[Buf("P2s%d_%d" % (i, hh)) for hh in range(2)] for i in range(NSL)]
    b_lacc = [Buf("lacc%d" % i) for i in range(2)]
    b_lc2, b_lc2i, b_lirow = Buf("lc2"), Buf("lc2i"), Buf("lirow")
    b_pvs = [Buf("pvs%d" % i) for i in range(2)]
    b_accQ = Buf("accQ")
    b_mnegs = Buf("mnegs")
    b_PTq = [[Buf("PT%d_%d" % (sb_, q_)) for q_ in range(4)] for sb_ in range(2)]
    b_junk = Buf("junk")
    b_lrep = Buf("lrep")
    b_AO2 = [[Buf("AO%d_%d" % (par, n)) for n in range(HC)] for par in range(2)]
    b_wo = [Buf("wo%d" % i) for i in range(2)]
    b_x2 = [Buf("x2_%d" % i) for i in range(2)]
    b_r1n = [Buf("r1n%d" % o) for o in range(HC)]
    b_sq2 = [Buf("sq2_%d" % i) for i in range(2)]
    b_yst1 = Buf("yst1")
    b_yst2 = [Buf("yst2_%d" % i) for i in range(2)]
    b_ag2 = [Buf("ag2_%d" % i) for i in range(2)]
    b_stf = [Buf("stf%d" % i) for i in range(6)]
    b_ssm = [Buf("ssm%d" % i) for i in range(4)]
    b_lgf2, b_gT2 = Buf("lgf2"), Buf("gT2")
    b_gtok2, b_gtk22, b_gtk32 = Buf("gtok2"), Buf("gtk22"), Buf("gtk32")
    b_gsc_, b_gsc2_ = Buf("gsc_"), Buf("gsc2_")
    b_mgc = Buf("mgc")

    def aosl(par, c):
        return AO[:, par * HC * QW + c * QW:par * HC * QW + (c + 1) * QW]

    g = "gpsimd"
    K.op(g, lambda h: h.dma_start(cst[:, :], cst_h[:, :]), writes=[b_cst], dma=True)
    K.op(g, lambda h: h.dma_start(cstf[:, :], cstf_h[:, :]), writes=[b_cstf], dma=True)
    K.op(g, lambda h: h.dma_start(sel_sb[:, :], cst_h[0:8, 128:256]), writes=[b_sel], dma=True)
    K.op(g, lambda h: h.dma_start(bqk_sb[:, :], bqk_h[:, :]), writes=[b_bqk], dma=True)
    K.op(g, lambda h: h.dma_start(bo_sb[:, :], bo_h[:, :]), writes=[b_bo], dma=True)
    K.op(g, lambda h: h.dma_start(bv_sb[:, :], bv_h[:, :]), writes=[b_bv], dma=True)
    K.op(g, lambda h: h.dma_start(wr_sb[:, :], wr_h[:, :]), writes=[b_wr], dma=True)
    K.op(g, lambda h: h.dma_start(wd_sb[:, :], wd_h[:, :]), writes=[b_wd], dma=True)
    K.op(g, lambda h: h.dma_start(wu_sb[:, :], wu_h[:, :]), writes=[b_wu], dma=True)
    for c in range(HC):
        K.op(g, (lambda c: lambda h: h.dma_start(
            XT[:, c * T:(c + 1) * T], xT[c * 128:(c + 1) * 128, :]))(c),
            writes=[b_XT], dma=True)
    K.op("vector", lambda h: h.memset(ones_sb[:, :], 1.0), writes=[b_ones])
    MAGIC_F = float(np.frombuffer(np.uint32(0x5F3759DF).tobytes(), np.float32)[0])
    K.op("vector", lambda h: h.memset(mgcF[:, :], MAGIC_F), writes=[b_mgc])
    K.op("vector", lambda h: h.memset(onesf_sb[:, :], 1.0), writes=[b_onesf])

    ident = cst[:, 0:128]
    ident8f = cstf[0:8, 0:8]
    identf = cstf[:, :]

    for dt in range(NVD):
        K.op(g, (lambda dt: lambda h: h.dma_start(
            wv_sb[:, :].rearrange("p (c d) -> p c d", d=VD),
            wv_h[dt, :, :].rearrange("(c p) d -> p c d", p=128)))(dt),
            writes=[b_wv], dma=True)
        for tb in range(TBo):
            idx1 = dt * TBo + tb
            slot = idx1 % 4
            if idx1 < HC:
                wcs = idx1 % 2
                K.op(g, (lambda o, wcs: lambda h: h.dma_start(
                    arena[:, wqk0 + wcs * HC * 128:wqk0 + (wcs + 1) * HC * 128]
                    .rearrange("p (c q) -> p c q", q=128),
                    wo_h[o, :, :].rearrange("(c p) q -> p c q", p=128)))(idx1, wcs),
                    writes=[b_wqk[wcs]], dma=True)
                K.op("sync", (lambda o, wcs: lambda h: h.dma_start(
                    wo_scr[o, :, :],
                    arena[:, wqk0 + wcs * HC * 128:wqk0 + (wcs + 1) * HC * 128]))(idx1, wcs),
                    reads=[b_wqk[wcs]], dma=True)
            def mmg(h, dt=dt, tb=tb, slot=slot):
                for c in range(HC):
                    h.matmul(psA[:, slot * 512:slot * 512 + VD],
                             XT[:, c * T + tb * 128:c * T + (tb + 1) * 128],
                             wv_sb[:, c * VD:(c + 1) * VD],
                             start=(c == 0), stop=False)
                return h.matmul(psA[:, slot * 512:slot * 512 + VD],
                                ones_sb[0:1, 0:128], bv_sb[:, dt * VD:(dt + 1) * VD],
                                start=False, stop=True)
            K.op("tensor", mmg, reads=[b_XT, b_wv, b_ones, b_bv], writes=pAb(slot))
            vs = idx1 % 2
            K.op("vector", (lambda slot, vs: lambda h: h.tensor_copy(
                vst[:, vs * 512:vs * 512 + VD], psA[:, slot * 512:slot * 512 + VD]))(slot, vs),
                reads=pAb(slot), writes=[b_vst[vs]])
            nh_t = VD // 128
            def spill(h, dt=dt, tb=tb, vs=vs, nh_t=nh_t):
                return h.dma_start(vag_i[tb, dt * nh_t:(dt + 1) * nh_t, :, :].transpose([1, 0, 2]),
                                   vst[:, vs * 512:vs * 512 + VD].rearrange("p (j d) -> p j d", d=128))
            ev = K.op("sync", spill, reads=[b_vst[vs]], dma=True)
            e_vscr[0].append(ev)

    def qk_proj(which):
        span = T
        tile_w = min(512, span)
        ntile = span // tile_w
        wsrc = wk_h if which == "k" else wq_h
        scr = kag_i if which == "k" else qscr
        escr = e_kscr if which == "k" else e_qscr
        bias0 = NH if which == "k" else 0
        for n in range(NH):
            slot = n % 2
            K.op(g, (lambda n, slot, wsrc=wsrc: lambda h: h.dma_start(
                arena[:, wqk0 + slot * HC * 128: wqk0 + (slot + 1) * HC * 128]
                .rearrange("p (c o) -> p c o", o=128),
                wsrc[n, :, :].rearrange("(c p) o -> p c o", p=128)))(n, slot),
                writes=[b_wqk[slot]], dma=True)
            for j in range(ntile):
                idx = n * ntile + j
                ps = idx % 4
                def mmg(h, n=n, j=j, slot=slot, ps=ps, tile_w=tile_w):
                    for c in range(HC):
                        mm = h.matmul(psB[:, ps * 512:ps * 512 + tile_w],
                                      arena[:, wqk0 + slot * HC * 128 + c * 128:
                                            wqk0 + slot * HC * 128 + (c + 1) * 128],
                                      XT[:, c * T + j * tile_w:c * T + (j + 1) * tile_w],
                                      start=(c == 0), stop=(c == HC - 1))
                    return mm
                K.op("tensor", mmg, reads=[b_XT, b_wqk[slot]], writes=pBb(ps))
                ks = idx % 2
                K.op("scalar", (lambda ps, ks, n, tile_w: lambda h: h.activation(
                    kst[:, ks * 512:ks * 512 + tile_w], psB[:, ps * 512:ps * 512 + tile_w],
                    AF.Identity, bias=bqk_sb[:, bias0 + n:bias0 + n + 1], scale=1.0))(ps, ks, n, tile_w),
                    reads=pBb(ps) + [b_bqk], writes=[b_kst[ks]])
                ev = K.op("sync", (lambda n, j, ks, tile_w, scr=scr: lambda h: h.dma_start(
                    scr[n, :, j * tile_w:(j + 1) * tile_w],
                    kst[:, ks * 512:ks * 512 + tile_w]))(n, j, ks, tile_w),
                    reads=[b_kst[ks]], dma=True)
                escr[n].append(ev)
    qk_proj("k")

    pair_groups = [[i, i + 1] for i in range(0, cfg.n_cores, 2)]
    e_kag = K.op(g, (lambda: lambda h: h.collective_compute(
        "AllGather", AluOpType.bypass, replica_groups=pair_groups,
        ins=[kag_i.ap().opt()], outs=[kag_o.ap().opt()]))(),
        waits=[e for n in range(NH) for e in e_kscr[n]])
    e_vag = K.op(g, (lambda: lambda h: h.collective_compute(
        "AllGather", AluOpType.bypass, replica_groups=pair_groups,
        ins=[vag_i.ap().opt()], outs=[vag_o.ap().opt()]))(),
        waits=e_vscr[0])
    qk_proj("q")

    K.barrier()
    nSA = (SH + 511) // 512
    NQS = T // 128
    import functools

    nstr = max(1, KC // 4)
    NV = NH * NQS
    SSW = (nstr + 2) * 128
    for n in range(NH):
        sl = n % 2
        K.op("sync", (lambda n, sl: lambda h: h.dma_start(
            arena[:, q2o + sl * T:q2o + (sl + 1) * T], qscr[n, :, :]))(n, sl),
            waits=e_qscr[n], writes=[b_q2[sl]], dma=True)
        for half in range(2):
            K.op("sync", (lambda n, sl, half: lambda h: h.dma_start(
                arena[:, k2o + sl * S + half * T:k2o + sl * S + (half + 1) * T],
                kag_o[half, n, :, :]))(n, sl, half),
                waits=[e_kag], writes=[b_k2[sl]], dma=True)
        for qi in range(NQS):
            pp = qi % 2
            psS = psA if pp == 0 else psB
            bps = ([b_psA[0], b_psA[1]] + b_pA2) if pp == 0 else ([b_psB[0], b_psB[1]] + b_pB2)
            hn = max(1, nstr // 2)
            def smp(h, n=n, qi=qi, sl=sl, psS=psS, hn=hn):
                k2v = arena[:, k2o + sl * S:k2o + (sl + 1) * S]
                qv = arena[:, q2o + sl * T + qi * 128:q2o + sl * T + (qi + 1) * 128]
                kstr = k2v.rearrange("p (a b) -> p a b", b=512)[:, :, 0:128]
                h.matmul(psS[:, 0:hn * 128], qv, kstr[:, 0:hn, :], start=True, stop=True)
                if nstr > hn:
                    h.matmul(psS[:, 512:512 + (nstr - hn) * 128], qv, kstr[:, hn:nstr, :],
                             start=True, stop=True)
                return h.matmul(psS[:, 1024:1024 + 256],
                                qv, k2v.rearrange("p (h b) -> p h b", h=2)
                                [:, :, qi * 128:(qi + 1) * 128],
                                start=True, stop=True)
            K.op("tensor", smp, reads=[b_k2[sl], b_q2[sl]], writes=bps)
            c0 = n * NQS + qi
            if nstr == 8:
                K.op("scalar", (lambda c0, psS: lambda h: h.activation(
                    junk[:, 0:1280], psS[:, 0:1280], AF.Exp, scale=SC / 4.0,
                    accum_out=accQ[:, c0:c0 + 1]))(c0, psS),
                    reads=bps, writes=[b_junk, b_accQ])
            else:
                K.op("scalar", (lambda c0, psS, hn: lambda h: h.activation(
                    junk[:, 0:hn * 128], psS[:, 0:hn * 128], AF.Exp, scale=SC / 4.0,
                    accum_out=accQ[:, NV + c0:NV + c0 + 1]))(c0, psS, hn),
                    reads=bps, writes=[b_junk, b_accQ])
                K.op("scalar", (lambda c0, psS: lambda h: h.activation(
                    junk[:, 0:256], psS[:, 1024:1280], AF.Exp, scale=SC / 4.0,
                    accum_out=accQ[:, 2 * NV + c0:2 * NV + c0 + 1]))(c0, psS),
                    reads=bps, writes=[b_junk, b_accQ])
                K.op("vector", (lambda c0: lambda h: h.tensor_tensor(
                    accQ[:, c0:c0 + 1], accQ[:, NV + c0:NV + c0 + 1],
                    accQ[:, 2 * NV + c0:2 * NV + c0 + 1], op=AluOpType.add))(c0),
                    reads=[b_accQ], writes=[b_accQ])
        K.op("vector", (lambda n: lambda h: h.tensor_scalar(
            accQI[:, NH * NQS + n * NQS:NH * NQS + (n + 1) * NQS],
            accQI[:, n * NQS:(n + 1) * NQS], 23, None,
            op0=AluOpType.arith_shift_right))(n),
            reads=[b_accQ], writes=[b_accQ])
        K.op("vector", (lambda n: lambda h: h.tensor_copy(
            accQ[:, 2 * NH * NQS + n * NQS:2 * NH * NQS + (n + 1) * NQS],
            accQI[:, NH * NQS + n * NQS:NH * NQS + (n + 1) * NQS]))(n),
            reads=[b_accQ], writes=[b_accQ])
        K.op("vector", (lambda n: lambda h: h.tensor_scalar(
            mnegs[:, n * NQS:(n + 1) * NQS],
            accQ[:, 2 * NH * NQS + n * NQS:2 * NH * NQS + (n + 1) * NQS],
            -127.0, -4.0 * 0.6931471805599453,
            op0=AluOpType.add, op1=AluOpType.mult))(n),
            reads=[b_accQ], writes=[b_mnegs])

    K.barrier()
    pend = []

    def flush_pending():
        for f in pend:
            f()
        pend.clear()

    q34 = []
    last_out = {}

    def p3p4_tile(qg):
        par = qg % 2
        t0 = qg * QW
        ops = []
        add = lambda f: ops.append([f])
        addb = lambda fs: ops.append(list(fs))

        def accum_stats(o):
            cur = o % 2
            add((lambda o, cur: lambda: K.op("vector", lambda h: h.tensor_tensor(
                sqst[:, cur * QW:(cur + 1) * QW], r1sl[:, o * QW:(o + 1) * QW],
                r1sl[:, o * QW:(o + 1) * QW], op=AluOpType.mult),
                reads=[b_r1n[o]], writes=[b_sq2[cur]]))(o, cur))
            if o == 0:
                add(lambda: K.op("vector", lambda h: h.tensor_copy(
                    ssm[:, 0:QW], r1sl[:, 0:QW]), reads=[b_r1n[0]], writes=[b_ssm[0]]))
                add(lambda: K.op("vector", lambda h: h.tensor_copy(
                    ssm[:, 2 * QW:3 * QW], sqst[:, 0:QW]), reads=[b_sq2[0]], writes=[b_ssm[2]]))
            else:
                prv = 1 - cur
                add((lambda o, cur, prv: lambda: K.op("vector", lambda h: h.tensor_tensor(
                    ssm[:, cur * QW:(cur + 1) * QW], ssm[:, prv * QW:(prv + 1) * QW],
                    r1sl[:, o * QW:(o + 1) * QW], op=AluOpType.add),
                    reads=[b_ssm[prv], b_r1n[o]], writes=[b_ssm[cur]]))(o, cur, prv))
                add((lambda cur, prv: lambda: K.op("vector", lambda h: h.tensor_tensor(
                    ssm[:, (2 + cur) * QW:(3 + cur) * QW],
                    ssm[:, (2 + prv) * QW:(3 + prv) * QW],
                    sqst[:, cur * QW:(cur + 1) * QW], op=AluOpType.add),
                    reads=[b_ssm[2 + prv], b_sq2[cur]], writes=[b_ssm[2 + cur]]))(cur, prv))

        def ln_half(to_ao):
            fin = (HC - 1) % 2
            mu = stf[:, 0:QW]
            rs = stf[:, QW:2 * QW]
            mus = stf[:, 2 * QW:3 * QW]
            scr2 = stf[:, 3 * QW:4 * QW]
            var2 = stf[:, 4 * QW:5 * QW]
            addb([
                lambda: K.op("tensor", lambda h: h.matmul(
                    psB[:, 1280:1280 + QW], ones_sb[:, 0:128], ssm[:, fin * QW:(fin + 1) * QW],
                    start=True, stop=True), reads=[b_ssm[fin], b_ones], writes=[b_pB2[1]]),
                lambda: K.op("tensor", lambda h: h.matmul(
                    psB[:, 1792:1792 + QW], ones_sb[:, 0:128],
                    ssm[:, (2 + fin) * QW:(3 + fin) * QW],
                    start=True, stop=True), reads=[b_ssm[2 + fin], b_ones], writes=[b_pB3[1]]),
                lambda: K.op("vector", lambda h: h.tensor_scalar(
                    mu, psB[:, 1280:1280 + QW], IH, None, op0=AluOpType.mult),
                    reads=[b_pB2[1]], writes=[b_stf[0]]),
                lambda: K.op("vector", lambda h: h.tensor_scalar(
                    scr2, psB[:, 1792:1792 + QW], IH, cfg.EPS,
                    op0=AluOpType.mult, op1=AluOpType.add),
                    reads=[b_pB3[1]], writes=[b_stf[3]]),
            ])
            add(lambda: K.op("vector", lambda h: h.tensor_tensor(
                rs, mu, mu, op=AluOpType.mult), reads=[b_stf[0]], writes=[b_stf[1]]))
            add(lambda: K.op("vector", lambda h: h.tensor_tensor(
                var2, scr2, rs, op=AluOpType.subtract),
                reads=[b_stf[1], b_stf[3]], writes=[b_stf[4]]))
            add(lambda: K.op("vector", lambda h: h.tensor_scalar(
                stfI[:, 3 * QW:4 * QW], stfI[:, 4 * QW:5 * QW], 1, None,
                op0=AluOpType.arith_shift_right),
                reads=[b_stf[4]], writes=[b_stf[3]]))
            add(lambda: K.op("vector", lambda h: h.tensor_tensor(
                stfI[:, QW:2 * QW], mgcI[:, 0:QW], stfI[:, 3 * QW:4 * QW],
                op=AluOpType.subtract), reads=[b_mgc, b_stf[3]], writes=[b_stf[1]]))
            for it in range(2):
                sa, sb = (1, 2) if it % 2 == 0 else (2, 1)
                add((lambda sa: lambda: K.op("vector", lambda h: h.tensor_tensor(
                    stf[:, 3 * QW:4 * QW], stf[:, sa * QW:(sa + 1) * QW],
                    stf[:, sa * QW:(sa + 1) * QW], op=AluOpType.mult),
                    reads=[b_stf[sa]], writes=[b_stf[3]]))(sa))
                add(lambda: K.op("vector", lambda h: h.tensor_tensor(
                    stf[:, 5 * QW:6 * QW], stf[:, 3 * QW:4 * QW], var2,
                    op=AluOpType.mult), reads=[b_stf[3], b_stf[4]], writes=[b_stf[5]]))
                add(lambda: K.op("vector", lambda h: h.tensor_scalar(
                    stf[:, 3 * QW:4 * QW], stf[:, 5 * QW:6 * QW], -0.5, 1.5,
                    op0=AluOpType.mult, op1=AluOpType.add),
                    reads=[b_stf[5]], writes=[b_stf[3]]))
                add((lambda sa, sb: lambda: K.op("vector", lambda h: h.tensor_tensor(
                    stf[:, sb * QW:(sb + 1) * QW], stf[:, sa * QW:(sa + 1) * QW],
                    stf[:, 3 * QW:4 * QW], op=AluOpType.mult),
                    reads=[b_stf[sa], b_stf[3]], writes=[b_stf[sb]]))(sa, sb))
            add(lambda: K.op("vector", lambda h: h.tensor_tensor(
                mus, mu, rs, op=AluOpType.mult), reads=[b_stf[0], b_stf[1]], writes=[b_stf[2]]))
            for o in range(HC):
                ys = o % 2
                add((lambda o: lambda: K.op("vector", lambda h: h.tensor_tensor(
                    ystage[:, 0:QW], r1sl[:, o * QW:(o + 1) * QW], rs,
                    op=AluOpType.mult), reads=[b_r1n[o], b_stf[1]], writes=[b_yst1]))(o))
                add((lambda o, ys: lambda: K.op("vector", lambda h: h.tensor_tensor(
                    ystage[:, (1 + ys) * QW:(2 + ys) * QW], ystage[:, 0:QW], mus,
                    op=AluOpType.subtract),
                    reads=[b_yst1, b_stf[2]], writes=[b_yst2[ys]]))(o, ys))
                if to_ao:
                    add((lambda o, ys: lambda: K.op("scalar", lambda h: h.activation(
                        aosl(par, o), ystage[:, (1 + ys) * QW:(2 + ys) * QW], AF.Copy),
                        reads=[b_yst2[ys]], writes=[b_AO2[par][o]]))(o, ys))
                else:
                    def mk_out(o=o, ys=ys):
                        def go():
                            ev = K.op("scalar", lambda h: h.dma_start(
                                outT[o * 128:(o + 1) * 128, t0:t0 + QW],
                                ystage[:, (1 + ys) * QW:(2 + ys) * QW]),
                                reads=[b_yst2[ys]], dma=True)
                            last_out[ev.sem] = ev
                        return go
                    add(mk_out())

        if debug:
            add(lambda: K.op(g, lambda h: h.dma_start(
                d_ao[qg, :, :], AO[:, par * HC * QW:(par + 1) * HC * QW]),
                reads=list(b_AO2[par]), dma=True))
        for o in range(HC):
            ws = o % 2
            po = 1280 if o % 2 == 0 else 1792
            pb = b_pA2[1] if o % 2 == 0 else b_pA3[1]
            add((lambda o, ws: lambda: K.op(g, lambda h: h.dma_start(
                arena[:, wo0 + ws * HC * 128:wo0 + (ws + 1) * HC * 128], wo_scr[o, :, :]),
                writes=[b_wo[ws]], dma=True))(o, ws))
            add((lambda o: lambda: K.op(g, lambda h: h.dma_start(
                x_res[:, (o % 2) * QW:(o % 2) * QW + QW],
                xT[o * 128:(o + 1) * 128, t0:t0 + QW]),
                writes=[b_x2[o % 2]], dma=True))(o))
            def mk_proj(o=o, ws=ws, po=po, pb=pb):
                def mms(h):
                    for c in range(HC):
                        mm = h.matmul(psA[:, po:po + QW],
                                      arena[:, wo0 + ws * HC * 128 + c * 128:
                                            wo0 + ws * HC * 128 + (c + 1) * 128],
                                      aosl(par, c), start=(c == 0), stop=(c == HC - 1))
                    return mm
                return lambda: K.op("tensor", mms, reads=[b_wo[ws]] + b_AO2[par], writes=[pb])
            bnd = [mk_proj()]
            if debug and o < 2:
                bnd.append((lambda o, po, pb: lambda: K.op("vector", lambda h: h.tensor_copy(
                    ystage[:, 0:QW], psA[:, po:po + QW]),
                    reads=[pb], writes=[b_yst1]))(o, po, pb))
                bnd.append((lambda o: lambda: K.op(g, lambda h: h.dma_start(
                    d_pj[qg, o, :, :], ystage[:, 0:QW]), reads=[b_yst1], dma=True))(o))
                bnd.append((lambda o: lambda: K.op(g, lambda h: h.dma_start(
                    d_xr[qg, o, :, :], x_res[:, (o % 2) * QW:(o % 2) * QW + QW]),
                    reads=[b_x2[o % 2]], dma=True))(o))
            bnd.append((lambda o, po, pb: lambda: K.op("vector", lambda h: h.scalar_tensor_tensor(
                r1sl[:, o * QW:(o + 1) * QW], psA[:, po:po + QW], bo_sb[:, o:o + 1],
                x_res[:, (o % 2) * QW:(o % 2) * QW + QW],
                op0=AluOpType.add, op1=AluOpType.add),
                reads=[pb, b_bo, b_x2[o % 2]], writes=[b_r1n[o]]))(o, po, pb))
            addb(bnd)
            accum_stats(o)
        if debug:
            add(lambda: K.op(g, lambda h: h.dma_start(
                d_r1[qg, :, :], r1sl[:, 0:HC * QW]), reads=list(b_r1n), dma=True))
        ln_half(True)
        if debug:
            add(lambda: K.op(g, lambda h: h.dma_start(
                d_y[qg, :, :], AO[:, par * HC * QW:(par + 1) * HC * QW]),
                reads=list(b_AO2[par]), dma=True))
            add(lambda: K.op(g, lambda h: h.dma_start(
                d_st[qg, 0, :, :], stf[:, 0:6 * QW]), reads=list(b_stf), dma=True))

        def mk_router():
            def mms(h):
                for c in range(HC):
                    mm = h.matmul(psA[0:8, 1280:1280 + QW], wr_sb[:, c * 8:(c + 1) * 8],
                                  aosl(par, c), start=(c == 0), stop=(c == HC - 1))
                return mm
            return lambda: K.op("tensor", mms, reads=[b_wr] + b_AO2[par], writes=[b_pA2[1]])
        addb([mk_router(),
              lambda: K.op("vector", lambda h: h.tensor_copy(
                  lgf[:, 0:QW], psA[0:8, 1280:1280 + QW]), reads=[b_pA2[1]], writes=[b_lgf2])])
        addb([(lambda tb: lambda: K.op("tensor", lambda h: h.transpose(
                  psA[:, 1792 + tb * 8:1792 + tb * 8 + 8],
                  lgf[0:8, tb * 128:(tb + 1) * 128], ident8f),
                  reads=[b_lgf2, b_cstf], writes=[b_pA3[1]]))(tb) for tb in range(GQ)] +
              [lambda: K.op("vector", lambda h: h.tensor_copy(
                  gtok[:, 0:GQ * 8], psA[:, 1792:1792 + GQ * 8]),
                  reads=[b_pA3[1]], writes=[b_gtok2])])
        seg = gtok[:, :].rearrange("p (s e) -> p s e", e=8)
        seg2 = gtk2[:, :].rearrange("p (s e) -> p s e", e=8)
        seg3 = gtk3[:, :].rearrange("p (s e) -> p s e", e=8)
        gb = lambda: gsc[:, :, None].broadcast_to((128, GQ, 8))
        gb2 = lambda: gsc2[:, :, None].broadcast_to((128, GQ, 8))
        add(lambda: K.op("vector", lambda h: h.tensor_reduce(
            gsc[:, :, None], seg, axis=mybir.AxisListType.X, op=AluOpType.max),
            reads=[b_gtok2], writes=[b_gsc_]))
        add(lambda: K.op("vector", lambda h: h.tensor_tensor(
            seg2, seg, gb(), op=AluOpType.subtract),
            reads=[b_gtok2, b_gsc_], writes=[b_gtk22]))
        add(lambda: K.op("scalar", lambda h: h.activation(gtok[:, :], gtk2[:, :], AF.Exp),
            reads=[b_gtk22], writes=[b_gtok2]))
        add(lambda: K.op("vector", lambda h: h.tensor_reduce(
            gsc[:, :, None], seg, axis=mybir.AxisListType.X, op=AluOpType.add),
            reads=[b_gtok2], writes=[b_gsc_]))
        add(lambda: K.op("vector", lambda h: h.reciprocal(gsc2[:, :], gsc[:, :]),
            reads=[b_gsc_], writes=[b_gsc2_]))
        add(lambda: K.op("vector", lambda h: h.tensor_tensor(
            seg2, seg, gb2(), op=AluOpType.mult),
            reads=[b_gsc2_, b_gtok2], writes=[b_gtk22]))
        add(lambda: K.op("vector", lambda h: h.tensor_reduce(
            gsc[:, :, None], seg2, axis=mybir.AxisListType.X, op=AluOpType.max),
            reads=[b_gtk22], writes=[b_gsc_]))
        add(lambda: K.op("vector", lambda h: h.tensor_tensor(
            seg, seg2, gb(), op=AluOpType.is_ge),
            reads=[b_gtk22, b_gsc_], writes=[b_gtok2]))
        add(lambda: K.op("vector", lambda h: h.scalar_tensor_tensor(
            gtk3[:, :], gtok[:, :], -2.0, gtk2[:, :],
            op0=AluOpType.mult, op1=AluOpType.add),
            reads=[b_gtk22, b_gtok2], writes=[b_gtk32]))
        add(lambda: K.op("vector", lambda h: h.tensor_reduce(
            gsc[:, :, None], seg3, axis=mybir.AxisListType.X, op=AluOpType.max),
            reads=[b_gtk32], writes=[b_gsc_]))
        add(lambda: K.op("vector", lambda h: h.tensor_tensor(
            seg, seg2, gb(), op=AluOpType.is_ge),
            reads=[b_gtk22, b_gsc_], writes=[b_gtok2]))
        add(lambda: K.op("vector", lambda h: h.tensor_tensor(
            seg3, seg, seg2, op=AluOpType.mult),
            reads=[b_gtk22, b_gtok2], writes=[b_gtk32]))
        addb([(lambda tb: lambda: K.op("tensor", lambda h: h.transpose(
                  psA[0:8, 1280 + tb * 128:1280 + (tb + 1) * 128],
                  gtk3[:, tb * 8:(tb + 1) * 8], identf),
                  reads=[b_gtk32, b_cstf], writes=[b_pA2[1]]))(tb) for tb in range(GQ)] +
              [lambda: K.op("vector", lambda h: h.tensor_copy(
                  gT_sb[:, 0:QW], psA[0:8, 1280:1280 + QW]),
                  reads=[b_pA2[1]], writes=[b_gT2])])
        def mk_down():
            def mms(h):
                for c in range(HC):
                    mm = h.matmul(psA[:, 1280:1280 + QW],
                                  wd_sb[:, c * EI:(c + 1) * EI][:, 0:128],
                                  aosl(par, c), start=(c == 0), stop=(c == HC - 1))
                return mm
            return lambda: K.op("tensor", mms, reads=[b_wd] + b_AO2[par], writes=[b_pA2[1]])
        addb([
            lambda: K.op("tensor", lambda h: h.matmul(
                psA[:, 1792:1792 + QW], sel_sb[:, :], gT_sb[0:8, 0:QW], start=True, stop=True),
                reads=[b_sel, b_gT2], writes=[b_pA3[1]]),
            mk_down(),
            lambda: K.op("scalar", lambda h: h.activation(
                ystage[:, 0:QW], psA[:, 1280:1280 + QW], AF.Exp, scale=-1.0),
                reads=[b_pA2[1]], writes=[b_yst1]),
            lambda: K.op("vector", lambda h: h.tensor_scalar(
                ystage[:, QW:2 * QW], ystage[:, 0:QW], 1.0, None, op0=AluOpType.add),
                reads=[b_yst1], writes=[b_yst2[0]]),
            lambda: K.op("vector", lambda h: h.reciprocal(
                ystage[:, 2 * QW:3 * QW], ystage[:, QW:2 * QW]),
                reads=[b_yst2[0]], writes=[b_yst2[1]]),
            lambda: K.op("vector", lambda h: h.tensor_tensor(
                agst[:, 0:QW], psA[:, 1280:1280 + QW], ystage[:, 2 * QW:3 * QW],
                op=AluOpType.mult), reads=[b_pA2[1], b_yst2[1]], writes=[b_ag2[0]]),
            lambda: K.op("vector", lambda h: h.tensor_tensor(
                agst[:, QW:2 * QW], agst[:, 0:QW], psA[:, 1792:1792 + QW],
                op=AluOpType.mult), reads=[b_ag2[0], b_pA3[1]], writes=[b_ag2[1]]),
        ])
        for o in range(HC):
            po = 1280 if o % 2 == 0 else 1792
            pb = b_pA2[1] if o % 2 == 0 else b_pA3[1]
            addb([(lambda o, po, pb: lambda: K.op("tensor", lambda h: h.matmul(
                      psA[:, po:po + QW], wu_sb[:, o * 128:(o + 1) * 128],
                      agst[:, QW:2 * QW], start=True, stop=True),
                      reads=[b_wu, b_ag2[1]], writes=[pb]))(o, po, pb),
                  (lambda o, po, pb: lambda: K.op("vector", lambda h: h.tensor_tensor(
                      r1sl[:, o * QW:(o + 1) * QW], psA[:, po:po + QW], aosl(par, o),
                      op=AluOpType.add), reads=[pb, b_AO2[par][o]],
                      writes=[b_r1n[o]]))(o, po, pb)])
            accum_stats(o)
        if debug:
            add(lambda: K.op("sync", lambda h: h.dma_start(
                d_r2[qg, :, :], r1sl[:, 0:HC * QW]), reads=list(b_r1n), dma=True))
        ln_half(False)
        if debug:
            add(lambda: K.op(g, lambda h: h.dma_start(
                d_st[qg, 1, :, :], stf[:, 0:6 * QW]), reads=list(b_stf), dma=True))
        return ops

    for qg in range(NGQ):
        for n in range(NH):
            sl = n % 2
            K.op(g, (lambda n, sl, qg: lambda h: h.dma_start(
                arena[:, q2o + sl * QW:q2o + (sl + 1) * QW],
                qscr[n, :, qg * QW:(qg + 1) * QW]))(n, sl, qg),
                waits=e_qscr[n], writes=[b_q2s[sl]], dma=True)
            for half in range(2):
                K.op(g, (lambda n, sl, half: lambda h: h.dma_start(
                    arena[:, k2o + sl * S + half * T:k2o + sl * S + (half + 1) * T],
                    kag_o[half, n, :, :]))(n, sl, half),
                    waits=[e_kag], writes=[b_k2[sl]], dma=True)
                K.op(g, (lambda n, sl, half: lambda h: h.dma_start(
                    arena[:, v2o + sl * S + half * T:v2o + sl * S + (half + 1) * T]
                    .rearrange("p (tb d) -> p tb d", d=128),
                    vag_o[half, :, n, :, :].transpose([1, 0, 2])))(n, sl, half),
                    waits=[e_vag], writes=[b_v2s[sl]], dma=True)
            for qs in range(GQ):
                qi = qg * GQ + qs
                p = (qs * 2 + n % 2) % NSL
                def smm(h, qs=qs, sl=sl, half=0):
                    for m in range(nSA):
                        w = min(512, SH - m * 512)
                        ps = psA if half == 0 else psB
                        mm = h.matmul(ps[:, m * 512:m * 512 + w],
                                      arena[:, q2o + sl * QW + qs * 128:
                                            q2o + sl * QW + (qs + 1) * 128],
                                      arena[:, k2o + sl * S + half * SH + m * 512:
                                            k2o + sl * S + half * SH + m * 512 + w],
                                      start=True, stop=True)
                    return mm
                K.op("tensor", functools.partial(smm, half=0),
                     reads=[b_k2[sl], b_q2s[sl]], writes=psA_all)
                K.op("tensor", functools.partial(smm, half=1),
                     reads=[b_k2[sl], b_q2s[sl]], writes=psB_all)
                gp = n % 2
                K.op("scalar", (lambda n, qi, p, gp, qs: lambda h: h.activation(
                    P2s[:, p * S:p * S + SH], psA[:, 0:SH], AF.Exp, scale=SC,
                    bias=mnegs[:, n * NQS + qi:n * NQS + qi + 1],
                    accum_out=lacc[:, gp * 4 + 2 * qs:gp * 4 + 2 * qs + 1]))(n, qi, p, gp, qs),
                    reads=psA_all + [b_mnegs], writes=[b_P2s[p][0], b_lacc[gp]])
                K.op("scalar", (lambda n, qi, p, gp, qs: lambda h: h.activation(
                    P2s[:, p * S + SH:p * S + 2 * SH], psB[:, 0:SH], AF.Exp, scale=SC,
                    bias=mnegs[:, n * NQS + qi:n * NQS + qi + 1],
                    accum_out=lacc[:, gp * 4 + 2 * qs + 1:gp * 4 + 2 * qs + 2]))(n, qi, p, gp, qs),
                    reads=psB_all + [b_mnegs], writes=[b_P2s[p][1], b_lacc[gp]])
                flush_pending()
                for hf in range(2):
                    K.op("sync", (lambda p, qs, sl, hf: lambda h: h.dma_start_transpose(
                        PTs(sl)[:, :].rearrange("p (j q) -> p j q", q=QW)
                        [:, hf * (KC // 2):(hf + 1) * (KC // 2),
                         qs * 128:(qs + 1) * 128],
                        P2s[:, p * S + hf * SH:p * S + (hf + 1) * SH]))(p, qs, sl, hf),
                        reads=[b_P2s[p][hf]], writes=[b_PTq[sl][qs * 2 + hf]],
                        dma=True,
                        dma_sem={(0, 0): "xd", (0, 1): "yd2",
                                 (1, 0): "yd3", (1, 1): "vd0"}[(n % 2, hf)])
                if qs == GQ - 1:
                    def def_G(n=n, qg=qg, sl=sl, gp=n % 2):
                        def pv(h):
                            for kt in range(KC):
                                mm = h.matmul(psA[:, 1024:1024 + QW],
                                              arena[:, v2o + sl * S + kt * 128:
                                                    v2o + sl * S + (kt + 1) * 128],
                                              PTs(sl)[:, kt * QW:(kt + 1) * QW],
                                              start=(kt == 0), stop=(kt == KC - 1))
                            return mm
                        K.op("tensor", pv, reads=[b_v2s[sl]] + b_PTq[sl], writes=[b_pA2[0]])
                        K.op("vector", (lambda gp: lambda h: h.tensor_copy(
                            pvst[:, gp * QW:(gp + 1) * QW], psA[:, 1024:1024 + QW]))(gp),
                            reads=[b_pA2[0]], writes=[b_pvs[gp]])
                        K.op("vector", (lambda gp: lambda h: h.tensor_tensor(
                            lcol2[:, 0:GQ],
                            lacc[:, 0:8].rearrange("p (g s h) -> p g s h", g=2, h=2)[:, gp, 0:GQ, 0],
                            lacc[:, 0:8].rearrange("p (g s h) -> p g s h", g=2, h=2)[:, gp, 0:GQ, 1],
                            op=AluOpType.add))(gp),
                            reads=[b_lacc[gp]], writes=[b_lc2])
                        K.op("vector", lambda h: h.reciprocal(lcol2i[:, 0:GQ], lcol2[:, 0:GQ]),
                             reads=[b_lc2], writes=[b_lc2i])
                        def ltr(h):
                            for q_ in range(GQ):
                                tr = h.transpose(psA[0:1, 1536 + q_ * 128:1536 + (q_ + 1) * 128],
                                                 lcol2i[:, q_:q_ + 1], identf)
                            return tr
                        K.op("tensor", ltr, reads=[b_lc2i, b_cstf], writes=[b_pA3[0]])
                        K.op("vector", lambda h: h.tensor_copy(
                            lirow[0:1, 0:QW], psA[0:1, 1536:1536 + QW]),
                            reads=[b_pA3[0]], writes=[b_lirow])
                        def lrepmm(h):
                            for q_ in range(GQ):
                                mm = h.matmul(psA[:, 1536 + q_ * 128:1536 + (q_ + 1) * 128],
                                              onesf_sb[0:1, 0:128],
                                              lirow[0:1, q_ * 128:(q_ + 1) * 128],
                                              start=True, stop=True)
                            return mm
                        K.op("tensor", lrepmm, reads=[b_onesf, b_lirow], writes=[b_pA3[0]])
                        K.op("vector", lambda h: h.tensor_copy(
                            lrep_sb[:, 0:QW], psA[:, 1536:1536 + QW]),
                            reads=[b_pA3[0]], writes=[b_lrep])
                        K.op("vector", (lambda n, qg, gp: lambda h: h.tensor_tensor(
                            aosl(qg % 2, n), pvst[:, gp * QW:(gp + 1) * QW],
                            lrep_sb[:, 0:QW],
                            op=AluOpType.mult))(n, qg, gp),
                            reads=[b_pvs[gp], b_lrep], writes=[b_AO2[qg % 2][n]])
                    pend.append(def_G)
                if q34:
                    rem = NH * GQ - (n * GQ + qs)
                    for _ in range(min(len(q34), -(-len(q34) // rem))):
                        for f in q34.pop(0):
                            f()
        q34.extend(p3p4_tile(qg))
    flush_pending()
    for bundle in q34:
        for f in bundle:
            f()
    q34.clear()

    outs = list(last_out.values())
    K.op("sync", lambda h: None, waits=outs)
    K.mark(*outs)

    K.finalize(nc, block, sems)
    es.close()
    return nc


def prep_inputs(cfg, x_bh, w_qkv, b_qkv, w_out, b_out, router_w, down_w, up_w, own_first=True):
    H, NH, HC, S, VD, NVD, EI = cfg.H, cfg.NH, cfg.HC, cfg.S, cfg.VD, cfg.NVD, cfg.EI
    f32 = np.float32
    d = {}
    d["xT"] = np.ascontiguousarray(x_bh.T.astype(f32))
    Wq, Wk, Wv = w_qkv[0:H], w_qkv[H:2 * H], w_qkv[2 * H:3 * H]
    d["wq"] = np.ascontiguousarray(
        Wq.reshape(NH, 128, H).transpose(0, 2, 1))
    d["wk"] = np.ascontiguousarray(Wk.reshape(NH, 128, H).transpose(0, 2, 1))
    d["wv"] = np.ascontiguousarray(
        Wv.T.reshape(H, NVD, VD).transpose(1, 0, 2))
    d["wo"] = np.ascontiguousarray(w_out.reshape(HC, 128, H).transpose(0, 2, 1))
    d["bqk"] = np.ascontiguousarray(
        np.concatenate([b_qkv[0:H].reshape(NH, 128).T, b_qkv[H:2 * H].reshape(NH, 128).T], axis=1)).astype(f32)
    d["bo"] = np.ascontiguousarray(b_out.reshape(HC, 128).T).astype(f32)
    d["bv"] = b_qkv[2 * H:3 * H].reshape(1, H).astype(f32)
    d["wr"] = np.ascontiguousarray(
        router_w.T.reshape(HC, 128, 8).transpose(1, 0, 2).reshape(128, HC * 8)).astype(f32)
    Dcat = down_w.reshape(EI, H)
    d["wd"] = np.ascontiguousarray(
        Dcat.T.reshape(HC, 128, EI).transpose(1, 0, 2).reshape(128, HC * EI)).astype(f32)
    Ucat = up_w.transpose(1, 0, 2).reshape(H, EI)
    d["wu"] = np.ascontiguousarray(Ucat.T).astype(f32)
    cst = np.zeros((128, 256), dtype=BF)
    cst[:, 0:128] = np.eye(128, dtype=BF)
    sel = np.zeros((8, 128), dtype=BF)
    for e in range(8):
        sel[e, e * 16:(e + 1) * 16] = 1
    cst[0:8, 128:256] = sel
    d["cst"] = cst
    d["cstf"] = np.eye(128, dtype=f32)
    return d


from concourse.bass_utils import run_bass_kernel_spmd


def kernel(L0_attn_hidden_states, L0_attn_w_qkv, L0_attn_b_qkv, L0_attn_w_out,
           L0_attn_b_out, L0_ffn_router_w, L0_ffn_down_w, L0_ffn_up_w):
    x = np.asarray(L0_attn_hidden_states)
    w_qkv = np.asarray(L0_attn_w_qkv)
    b_qkv = np.asarray(L0_attn_b_qkv)
    w_out = np.asarray(L0_attn_w_out)
    b_out = np.asarray(L0_attn_b_out)
    router_w = np.asarray(L0_ffn_router_w)
    down_w = np.asarray(L0_ffn_down_w)
    up_w = np.asarray(L0_ffn_up_w)

    B, Sfull, H = x.shape
    cfg = Cfg(NH=16, S=4096, T=2048, QT=512, n_cores=8)
    nc = build(cfg)

    wdict = prep_inputs(cfg, np.zeros((cfg.T, H), np.float32), w_qkv, b_qkv,
                        w_out, b_out, router_w, down_w, up_w)
    del wdict["xT"]

    in_maps = []
    for c in range(8):
        b, hf = c // 2, c % 2
        m = dict(wdict)
        m["xT"] = np.ascontiguousarray(x[b, hf * cfg.T:(hf + 1) * cfg.T, :].T.astype(np.float32))
        in_maps.append(m)

    res = run_bass_kernel_spmd(nc, in_maps, core_ids=list(range(8)))

    out = np.empty((B, Sfull, H), dtype=np.float32)
    for c in range(8):
        b, hf = c // 2, c % 2
        out[b, hf * cfg.T:(hf + 1) * cfg.T, :] = res.results[c]["outT"].T
    return out
